# version 07
# baseline (speedup 1.0000x reference)
import math
import numpy as np
from contextlib import ExitStack

import concourse.bass as bass
import concourse.mybir as mybir
from concourse.bass import IndirectOffsetOnAxis
from concourse.bass_utils import run_bass_kernel_spmd

F32 = mybir.dt.float32
BF16 = mybir.dt.bfloat16
I32 = mybir.dt.int32
U32 = mybir.dt.uint32
AF = mybir.ActivationFunctionType
ALU = mybir.AluOpType
AX = mybir.AxisListType

D = 1024
NTOK = 4096
SEQ = 2048
NB = 2
NTILE = 32
NEXP = 16
FF = 4096
TT = 256
NCT = 32
NSLOT = NCT * TT
WINDOW = 512
HEADS = 16
HDIM = 64
DST = 256
FFI = 3072
LN_EPS = 1e-5


class Ctx:
    def __init__(self, nc, es):
        self.nc = nc
        self.es = es
        self.sems = {}
        self.cnt = {}
        self.nbar = 0

    def sem(self, name):
        if name not in self.sems:
            self.sems[name] = self.es.enter_context(self.nc.semaphore(name))
            self.cnt[name] = 0
        return self.sems[name]

    def val(self, name):
        self.sem(name)
        return self.cnt[name]

    def bump(self, name, amount):
        self.sem(name)
        self.cnt[name] += amount


def sbuf(ctx, name, shape, dtype, es=None):
    return (es or ctx.es).enter_context(ctx.nc.sbuf_tensor(name, shape, dtype))


def barrier(ctx, block):
    ctx.nbar += 1
    b = ctx.sem("BAR")
    tgt = 5 * ctx.nbar

    def f(e):
        e.sem_inc(b, 1)
        e.wait_ge(b, tgt)
    for eng in (block.sync, block.scalar, block.vector, block.tensor, block.gpsimd):
        eng(f)
    ctx.cnt["BAR"] = tgt


def emit_convert(ctx, block, es, jobs, tag):
    tiles = []
    for (src, dst, rows, cols, scale) in jobs:
        isint = src.dtype == I32
        g = 4
        while g > 1 and (rows % (128 * g) != 0 or cols * 4 * g > 16384):
            g //= 2
        for r0 in range(0, rows, 128 * g):
            tiles.append((src, dst, r0, g, cols, scale, isint))
    maxc = max(g * cols for (_, _, _, g, cols, _, _) in tiles)
    tin = [sbuf(ctx, f"cvi{i}{tag}", [128, maxc], F32, es) for i in range(2)]
    has_int = any(t[6] for t in tiles)
    tini = [sbuf(ctx, f"cvj{i}{tag}", [128, maxc], I32, es) for i in range(2)] if has_int else None
    tout = [sbuf(ctx, f"cvo{i}{tag}", [128, maxc], BF16, es) for i in range(2)]
    cv = "CVC"
    n = len(tiles)
    bcv = ctx.val(cv)
    bld = [ctx.val("CVL0"), ctx.val("CVL1")]
    bst = [ctx.val("CVS0"), ctx.val("CVS1")]

    def f_load(par):
        def f(e):
            for i, (src, dst, r0, g, cols, scale, isint) in enumerate(tiles):
                if i % 2 != par:
                    continue
                if i >= 2:
                    e.wait_ge(ctx.sem(cv), bcv + i - 1)
                buf = tini[i % 2] if isint else tin[i % 2]
                e.dma_start(out=buf[:, 0:g * cols],
                            in_=src[r0:r0 + 128 * g, :].rearrange("(p g) c -> p (g c)", g=g)).then_inc(ctx.sem(f"CVL{i % 2}"), 16)
        return f
    ctx.bump("CVL0", 16 * ((n + 1) // 2))
    ctx.bump("CVL1", 16 * (n // 2))

    def f_vector(e):
        for i, (src, dst, r0, g, cols, scale, isint) in enumerate(tiles):
            e.wait_ge(ctx.sem(f"CVL{i % 2}"), bld[i % 2] + 16 * (i // 2 + 1))
            if i >= 2:
                e.wait_ge(ctx.sem(f"CVS{i % 2}"), bst[i % 2] + 16 * ((i - 2) // 2 + 1))
            buf = tini[i % 2] if isint else tin[i % 2]
            if scale is None:
                inst = e.tensor_copy(out=tout[i % 2][:, 0:g * cols], in_=buf[:, 0:g * cols])
            else:
                inst = e.tensor_scalar(out=tout[i % 2][:, 0:g * cols], in0=buf[:, 0:g * cols],
                                       scalar1=float(scale), scalar2=None, op0=ALU.mult)
            inst.then_inc(ctx.sem(cv), 1)
    ctx.bump(cv, n)

    def f_gpsimd(e):
        for i, (src, dst, r0, g, cols, scale, isint) in enumerate(tiles):
            e.wait_ge(ctx.sem(cv), bcv + i + 1)
            e.dma_start(out=dst[r0:r0 + 128 * g, :].rearrange("(p g) c -> p (g c)", g=g),
                        in_=tout[i % 2][:, 0:g * cols]).then_inc(ctx.sem(f"CVS{i % 2}"), 16)
        e.wait_ge(ctx.sem("CVS0"), bst[0] + 16 * ((n + 1) // 2))
        e.wait_ge(ctx.sem("CVS1"), bst[1] + 16 * (n // 2))
    ctx.bump("CVS0", 16 * ((n + 1) // 2))
    ctx.bump("CVS1", 16 * (n // 2))

    block.sync(f_load(0))
    block.scalar(f_load(1))
    block.vector(f_vector)
    block.gpsimd(f_gpsimd)


def emit_convert_pretile(ctx, block, es, w1_ap, w2_ap, w1pre_ap, w2pre_ap, tag):
    L = [sbuf(ctx, f"cpi{i}{tag}", [128, 4096], F32, es) for i in range(2)]
    wbuf = [[sbuf(ctx, f"cpw{q}{h}{tag}", [128, 4096], BF16, es) for h in range(2)] for q in range(4)]
    w2b = [sbuf(ctx, f"cpz{h}{tag}", [128, 8192], BF16, es) for h in range(2)]
    ld, cv, st = "CPL", "CPC", "CPS"
    bcv = ctx.val(cv)
    bld = [ctx.val("CPL0"), ctx.val("CPL1")]
    bst = [ctx.val("CPS0"), ctx.val("CPS1")]
    recs = []
    for e_ in range(NEXP):
        for kb in range(8):
            i = e_ * 8 + kb
            h = i // 4
            casts = [(q * 1024, wbuf[q][h % 2], (kb % 4) * 1024, 1024) for q in range(4)]
            war = (h % 2, 64 * (h // 2)) if (kb % 4 == 0 and h >= 2) else None
            stores = []
            if kb % 4 == 3:
                stores = [(w1pre_ap, (4 * e_ + q) * 128, (h % 2) * 4096, wbuf[q][h % 2], 4096, h % 2)
                          for q in range(4)]
            recs.append((w1_ap, e_ * 1024 + kb * 128, 4096, casts, war, stores))
    for e_ in range(NEXP):
        for q in range(4):
            jj = e_ * 4 + q
            for hb in range(8):
                casts = [(0, w2b[jj % 2], hb * 1024, 1024)]
                war = (jj % 2, 1024 + 16 * (jj // 2)) if (hb == 0 and jj >= 2) else None
                stores = []
                if hb == 7:
                    stores = [(w2pre_ap, (4 * e_ + q) * 128, 0, w2b[jj % 2], 8192, jj % 2)]
                recs.append((w2_ap, e_ * 4096 + q * 1024 + hb * 128, 1024, casts, war, stores))
    n = len(recs)
    cpre = [0]
    for r in recs:
        cpre.append(cpre[-1] + len(r[3]))

    def f_load(par):
        def f(e):
            for i, (srcap, r0, w, casts, war, stores) in enumerate(recs):
                if i % 2 != par:
                    continue
                if i >= 2:
                    e.wait_ge(ctx.sem(cv), bcv + cpre[i - 1])
                e.dma_start(out=L[i % 2][:, :w], in_=srcap[r0:r0 + 128, :]).then_inc(ctx.sem(f"CPL{i % 2}"), 16)
        return f
    ctx.bump("CPL0", 16 * ((n + 1) // 2))
    ctx.bump("CPL1", 16 * (n // 2))

    def f_vector(e):
        for i, (srcap, r0, w, casts, war, stores) in enumerate(recs):
            e.wait_ge(ctx.sem(f"CPL{i % 2}"), bld[i % 2] + 16 * (i // 2 + 1))
            if war is not None:
                e.wait_ge(ctx.sem(f"CPS{war[0]}"), bst[war[0]] + war[1])
            for (so, ob, oo, cw) in casts:
                e.tensor_copy(out=ob[:, oo:oo + cw], in_=L[i % 2][:, so:so + cw]).then_inc(ctx.sem(cv), 1)
    ctx.bump(cv, cpre[-1])

    stot = [0, 0]

    def f_gpsimd(e):
        cnt = [0, 0]
        for i, (srcap, r0, w, casts, war, stores) in enumerate(recs):
            if not stores:
                continue
            e.wait_ge(ctx.sem(cv), bcv + cpre[i + 1])
            for (dap, dr, dc, ob, sw, p) in stores:
                e.dma_start(out=dap[dr:dr + 128, dc:dc + sw],
                            in_=ob[:, 0:sw]).then_inc(ctx.sem(f"CPS{p}"), 16)
                cnt[p] += 16
        e.wait_ge(ctx.sem("CPS0"), bst[0] + cnt[0])
        e.wait_ge(ctx.sem("CPS1"), bst[1] + cnt[1])
    for (srcap, r0, w, casts, war, stores) in recs:
        for s in stores:
            stot[s[5]] += 16
    ctx.bump("CPS0", stot[0])
    ctx.bump("CPS1", stot[1])

    block.sync(f_load(0))
    block.scalar(f_load(1))
    block.vector(f_vector)
    block.gpsimd(f_gpsimd)


def emit_unipool(ctx, block, P, router_idx, tag, mode, xsrc=None, xdst=None):
    nc = ctx.nc
    do_ln = mode in ("first", "single")
    do_res = mode in ("last", "single")
    wscale = 0.5 if do_res else 1.0
    with ExitStack() as es:
        sb = lambda n, s, d: sbuf(ctx, n + tag, s, d, es)

        router = P["routers_bf"].ap()[router_idx * D:(router_idx + 1) * D, :]
        zbuf = P["xln"].ap()
        src = xsrc
        dst = xdst
        perm_d = P["perm"].ap()
        wslot_d = P["wslot"].ap()
        ttot_d = P["ttot"].ap()
        ctab_d = P["ctab"].ap()
        etab_d = P["etab"].ap()
        w1pre = P["w1bf"].ap()
        w2pre = P["w2bf"].ap()
        TRI = P["tri_sb"]
        TRI32 = P["tri32_sb"]
        ONES32 = P["ones32_sb"]
        ONESCOL = P["onescol_sb"]
        ONES128B = P["ones128b_sb"]
        ONES128F = P["ones128f_sb"]
        ONES1x32 = P["ones1x32_sb"]
        IOTA16 = P["iota16_sb"]
        pa = P["psum"]

        use_skip = not do_res
        xrow = [sb(f"u1x{i}", [128, D], F32) for i in range(2)] if do_ln else None
        xc = [sb(f"u1c{i}", [128, D], F32) for i in range(2)] if do_ln else None
        zrow = None if do_ln else [sb(f"u1z{i}", [128, D], BF16) for i in range(2)]
        sq = sb("u1sq", [128, D], F32) if do_ln else None
        lnrow = [sb(f"u1l{i}", [128, D], BF16) for i in range(2)] if do_ln else None
        xT1 = sb("u1xT", [128, 2 * D], BF16)
        ssum = sb("u1sm", [128, 1], F32)
        negmean = sb("u1nm", [128, 1], F32)
        varD = sb("u1vd", [128, 1], F32)
        lnv = sb("u1lv", [128, 1], F32)
        rstd = sb("u1rs", [128, 1], F32)
        nmx = sb("u1mx", [128, 1], F32)
        exbuf_all = sb("u1ex", [128, NEXP * NTILE], F32)
        sexp = sb("u1se", [128, 1], F32)
        recw = sb("u1rw", [128, 1], F32)
        mx8 = sb("u1m8", [128, 8], F32)
        mi8 = sb("u1i8", [128, 8], U32)
        idxf_all = sb("u1if", [128, NTILE], F32)
        w_all = sb("u1wa", [128, NTILE], F32)
        rank_all = sb("u1ra", [128, NTILE], F32)
        oh_all = sb("u1oh", [128, NEXP * NTILE], BF16)
        cumsel = sb("u1cs", [128, NEXP], F32)
        ttrow = sb("u1tr", [1, NEXP], F32)
        router_sb = sb("u1rt", [128, 8 * NEXP], BF16)

        ps_log, ps_cum, ps_tot = pa[0], pa[1], pa[2]
        LD, TP, XS, TS = "ULD", "UTP", "UXS", "UTS"
        V, S, M = "UV", "US", "UM"
        bTP, bTS = ctx.val(TP), ctx.val(TS)
        bLD = [ctx.val("ULD0"), ctx.val("ULD1")]
        bXS = [ctx.val("UXS0"), ctx.val("UXS1")]
        bV, bS, bM = ctx.val(V), ctx.val(S), ctx.val(M)
        VP = 15 if do_ln else 10
        SP = 4 if do_ln else 1
        VR = 5 if do_ln else 0

        RT = "URT"
        bRT = ctx.val(RT)

        def s1_sync(e):
            for kb in range(8):
                e.dma_start(out=router_sb[:, kb * NEXP:(kb + 1) * NEXP],
                            in_=router[kb * 128:(kb + 1) * 128, :]).then_inc(ctx.sem(RT), 16)
            for i in range(NTILE):
                if do_ln:
                    if i >= 2:
                        e.wait_ge(ctx.sem(V), bV + VP * (i - 2) + 3)
                    e.dma_start(out=xrow[i % 2][:, :], in_=src[i * 128:(i + 1) * 128, :]).then_inc(ctx.sem(f"ULD{i % 2}"), 16)
                    e.wait_ge(ctx.sem(V), bV + VP * i + 5)
                else:
                    e.wait_ge(ctx.sem(f"UZG{i % 2}"), bUZG[i % 2] + 16 * (i // 2 + 1))
                if i >= 1:
                    e.wait_ge(ctx.sem(M), bM + 3 * i - 2)
                srcap = lnrow[i % 2][:, :] if do_ln else zrow[i % 2][:, :]
                e.dma_start_transpose(out=b3(xT1[:, (i % 2) * D:(i % 2 + 1) * D]),
                                      in_=b3(srcap)).then_inc(ctx.sem(TP), 16)
        if do_ln:
            ctx.bump("ULD0", 16 * (NTILE // 2))
            ctx.bump("ULD1", 16 * (NTILE // 2))
        bUZG = [ctx.val("UZG0"), ctx.val("UZG1")]
        if not do_ln:
            ctx.bump("UZG0", 16 * (NTILE // 2))
            ctx.bump("UZG1", 16 * (NTILE // 2))

        def s1_scalar(e):
            for i in range(NTILE):
                if do_ln:
                    e.wait_ge(ctx.sem(V), bV + VP * i + 1)
                    e.activation(out=negmean[:], in_=ssum[:], func=AF.Copy, scale=-1.0 / D).then_inc(ctx.sem(S), 1)
                    e.wait_ge(ctx.sem(V), bV + VP * i + 4)
                    e.activation(out=lnv[:], in_=varD[:], func=AF.Ln, bias=P["eps_sb"][:, :], scale=1.0 / D).then_inc(ctx.sem(S), 1)
                    e.wait_ge(ctx.sem(S), bS + SP * i + 2)
                    e.activation(out=rstd[:], in_=lnv[:], func=AF.Exp, scale=-0.5).then_inc(ctx.sem(S), 1)
                    e.wait_ge(ctx.sem(V), bV + VP * i + 5)
                e.wait_ge(ctx.sem(M), bM + 3 * i + 1)
                e.wait_ge(ctx.sem(V), bV + VP * i + VR + 1)
                e.activation(out=exbuf_all[:, i * NEXP:(i + 1) * NEXP], in_=ps_log[:, 0:NEXP], func=AF.Exp, bias=nmx[:],
                             scale=1.0, accum_out=sexp[:]).then_inc(ctx.sem(S), 1)
        ctx.bump(S, SP * NTILE)
        ctx.bump(TP, 16 * NTILE)

        def s1_gpsimd(e):
            if do_ln:
                for i in range(NTILE):
                    e.wait_ge(ctx.sem(V), bV + VP * i + 5)
                    e.dma_start(out=zbuf[i * 128:(i + 1) * 128, :], in_=lnrow[i % 2][:, :]).then_inc(ctx.sem(f"UXS{i % 2}"), 16)
                    e.wait_ge(ctx.sem(V), bV + VP * i + VR + 10)
                    e.dma_start(out=ttot_d[i:i + 1, :], in_=ttrow[:, :]).then_inc(ctx.sem(TS), 16)
            else:
                K = 4
                for j in range(NTILE + K):
                    if j < NTILE:
                        if j >= 2:
                            e.wait_ge(ctx.sem(TP), bTP + 16 * (j - 1))
                        e.dma_start(out=zrow[j % 2][:, :],
                                    in_=zbuf[j * 128:(j + 1) * 128, :]).then_inc(ctx.sem(f"UZG{j % 2}"), 16)
                    if j >= K:
                        i = j - K
                        e.wait_ge(ctx.sem(V), bV + VP * i + VR + 10)
                        e.dma_start(out=ttot_d[i:i + 1, :], in_=ttrow[:, :]).then_inc(ctx.sem(TS), 16)
            if "dbg_ex" in P:
                e.wait_ge(ctx.sem(M), bM + 3 * NTILE)
                bdbg = ctx.val("DBGS")
                e.dma_start(out=P["dbg_ex"].ap(), in_=exbuf_all[:, :]).then_inc(ctx.sem("DBGS"), 16)
                e.dma_start(out=P["dbg_idx"].ap(), in_=idxf_all[:, :]).then_inc(ctx.sem("DBGS"), 16)
                e.dma_start(out=P["dbg_xt"].ap(), in_=xT1[:, :]).then_inc(ctx.sem("DBGS"), 16)
                e.wait_ge(ctx.sem("DBGS"), bdbg + 48)
        if do_ln:
            ctx.bump("UXS0", 16 * (NTILE // 2))
            ctx.bump("UXS1", 16 * (NTILE // 2))
        ctx.bump(TS, 16 * NTILE)
        if "dbg_ex" in P:
            ctx.bump("DBGS", 48)

        def s1_vector(e):
            for i in range(NTILE):
                b = i % 2
                vb = bV + VP * i

                def step(k, inst):
                    inst.then_inc(ctx.sem(V), 1)
                    e.wait_ge(ctx.sem(V), vb + k)
                if do_ln:
                    e.wait_ge(ctx.sem(f"ULD{i % 2}"), bLD[i % 2] + 16 * (i // 2 + 1))
                    if i >= 1:
                        e.wait_ge(ctx.sem(S), bS + SP * (i - 1) + SP)
                    step(1, e.reduce_sum(ssum[:], xrow[b][:, :], axis=AX.X))
                    e.wait_ge(ctx.sem(S), bS + SP * i + 1)
                    step(2, e.tensor_scalar(out=xc[b][:, :], in0=xrow[b][:, :], scalar1=negmean[:],
                                            scalar2=None, op0=ALU.add))
                    step(3, e.tensor_tensor(out=sq[:, :], in0=xc[b][:, :], in1=xc[b][:, :], op=ALU.mult))
                    step(4, e.reduce_sum(varD[:], sq[:, :], axis=AX.X))
                    e.wait_ge(ctx.sem(S), bS + SP * i + 3)
                    if i >= 2:
                        e.wait_ge(ctx.sem(TP), bTP + 16 * (i - 1))
                        e.wait_ge(ctx.sem(f"UXS{i % 2}"), bXS[i % 2] + 16 * ((i - 2) // 2 + 1))
                    step(5, e.tensor_scalar(out=lnrow[b][:, :], in0=xc[b][:, :], scalar1=rstd[:],
                                            scalar2=None, op0=ALU.mult))
                e.wait_ge(ctx.sem(M), bM + 3 * i + 1)
                if i >= 1:
                    e.wait_ge(ctx.sem(S), bS + SP * (i - 1) + SP)
                step(VR + 1, e.reduce_max(nmx[:], ps_log[:, 0:NEXP], axis=AX.X, negate=True))
                e.wait_ge(ctx.sem(S), bS + SP * i + SP)
                step(VR + 2, e.reciprocal(recw[:], sexp[:]))
                step(VR + 3, e.tensor_scalar(out=w_all[:, i:i + 1], in0=recw[:], scalar1=wscale,
                                             scalar2=None, op0=ALU.mult))
                step(VR + 4, e.max(mx8[:], exbuf_all[:, i * NEXP:(i + 1) * NEXP]))
                step(VR + 5, e.max_index(mi8[:], mx8[:], exbuf_all[:, i * NEXP:(i + 1) * NEXP]))
                step(VR + 6, e.tensor_copy(out=idxf_all[:, i:i + 1], in_=mi8[:, 0:1]))
                step(VR + 7, e.tensor_scalar(out=oh_all[:, i * NEXP:(i + 1) * NEXP], in0=IOTA16[:, :],
                                             scalar1=idxf_all[:, i:i + 1], scalar2=None, op0=ALU.is_equal))
                e.wait_ge(ctx.sem(M), bM + 3 * i + 2)
                step(VR + 8, e.tensor_tensor(out=cumsel[:, :], in0=ps_cum[:, 0:NEXP],
                                             in1=oh_all[:, i * NEXP:(i + 1) * NEXP], op=ALU.mult))
                step(VR + 9, e.reduce_sum(rank_all[:, i:i + 1], cumsel[:, :], axis=AX.X))
                e.wait_ge(ctx.sem(M), bM + 3 * i + 3)
                if i >= 1:
                    e.wait_ge(ctx.sem(TS), bTS + 16 * i)
                step(VR + 10, e.tensor_copy(out=ttrow[:, :], in_=ps_tot[0:1, 0:NEXP]))
        ctx.bump(V, VP * NTILE)

        ctx.bump(RT, 16 * 8)

        def s1_tensor(e):
            e.wait_ge(ctx.sem(RT), bRT + 16 * 8)
            for i in range(NTILE):
                e.wait_ge(ctx.sem(TP), bTP + 16 * (i + 1))
                if i >= 1:
                    e.wait_ge(ctx.sem(S), bS + SP * (i - 1) + SP)
                    e.wait_ge(ctx.sem(V), bV + VP * (i - 1) + VR + 1)
                for db in range(8):
                    inst = e.matmul(ps_log[:, 0:NEXP],
                                    xT1[:, (i % 2) * D + db * 128:(i % 2) * D + (db + 1) * 128],
                                    router_sb[:, db * NEXP:(db + 1) * NEXP],
                                    start=(db == 0), stop=(db == 7))
                inst.then_inc(ctx.sem(M), 1)
                e.wait_ge(ctx.sem(V), bV + VP * i + VR + 7)
                if i >= 1:
                    e.wait_ge(ctx.sem(V), bV + VP * (i - 1) + VR + 8)
                e.matmul(ps_cum[:, 0:NEXP], TRI[:, :], oh_all[:, i * NEXP:(i + 1) * NEXP],
                         start=True, stop=True).then_inc(ctx.sem(M), 1)
                if i >= 1:
                    e.wait_ge(ctx.sem(V), bV + VP * (i - 1) + VR + 10)
                e.matmul(ps_tot[0:1, 0:NEXP], ONESCOL[:, :], oh_all[:, i * NEXP:(i + 1) * NEXP],
                         start=True, stop=True).then_inc(ctx.sem(M), 1)
        ctx.bump(M, 3 * NTILE)

        block.gpsimd(s1_gpsimd)
        block.sync(s1_sync)
        block.scalar(s1_scalar)
        block.vector(s1_vector)
        block.tensor(s1_tensor)

        tt32 = sb("u2tt", [32, NEXP], F32)
        offs_sb = sb("u2of", [32, NEXP], F32)
        c1 = sb("u2c1", [32, NEXP], F32)
        ctab_sb = sb("u2ct", [32, NEXP], F32)
        pcnt = sb("u2pc", [1, NEXP], I32)
        pcs = [sb(f"u2s{i}", [1, 48], I32) for i in range(2)]
        pbasef = sb("u2pb", [1, NEXP], F32)
        pb256 = sb("u2p2", [32, NEXP], F32)
        iota32 = sb("u2i32", [32, 1], F32)
        cmpm = sb("u2cm", [32, NEXP], F32)
        ecnt = sb("u2ec", [32, 1], F32)
        etab_i = sb("u2ei", [32, 1], I32)
        invbuf = sb("u2iv", [128, 64], U32)
        ps_off, ps_bc = pa[3], pa[4]
        G2, V2, M2, D2 = "UG2", "UV2", "UM2", "UD2"
        bG2, bV2, bM2, bD2 = ctx.val(G2), ctx.val(V2), ctx.val(M2), ctx.val(D2)
        bD2G = ctx.val("UD2G")

        def s2_gpsimd(e):
            e.iota(iota32[0:32, :], [[0, 1]], base=0, channel_multiplier=1,
                   allow_small_or_imprecise_dtypes=True).then_inc(ctx.sem(G2), 1)
            e.memset(invbuf[:, :], 0xFFFFFFFF).then_inc(ctx.sem(G2), 1)
            e.wait_ge(ctx.sem(G2), bG2 + 2)
            e.dma_start(out=perm_d.rearrange("(p c) o -> p (c o)", p=128), in_=invbuf[:, :]).then_inc(ctx.sem("UD2G"), 16)
            e.wait_ge(ctx.sem(TS), bTS + 16 * NTILE)
            e.dma_start(out=tt32[:, :], in_=ttot_d[:, :]).then_inc(ctx.sem("UD2G"), 16)
        ctx.bump(G2, 2)
        ctx.bump("UD2G", 32)

        def s2_tensor(e):
            e.wait_ge(ctx.sem("UD2G"), bD2G + 32)
            e.wait_ge(ctx.sem(V), bV + VP * NTILE)
            e.matmul(ps_off[0:32, 0:NEXP], TRI32[:, :], tt32[:, :], start=True, stop=True).then_inc(ctx.sem(M2), 1)
            e.matmul(ps_tot[0:1, 0:NEXP], ONES32[:, :], tt32[:, :], start=True, stop=True).then_inc(ctx.sem(M2), 1)
            e.wait_ge(ctx.sem(V2), bV2 + 11)
            e.matmul(ps_bc[0:32, 0:NEXP], ONES1x32[:, :], pbasef[:, :], start=True, stop=True).then_inc(ctx.sem(M2), 1)
        ctx.bump(M2, 3)

        def s2_vector(e):
            vv = [bV2]

            def step(inst):
                inst.then_inc(ctx.sem(V2), 1)
                vv[0] += 1
                e.wait_ge(ctx.sem(V2), vv[0])
            e.wait_ge(ctx.sem(G2), bG2 + 1)
            e.wait_ge(ctx.sem(M2), bM2 + 2)
            step(e.tensor_copy(out=pcnt[:, :], in_=ps_tot[0:1, 0:NEXP]))
            step(e.tensor_scalar(out=pcnt[:, :], in0=pcnt[:, :], scalar1=TT - 1, scalar2=None, op0=ALU.add))
            step(e.tensor_scalar(out=pcnt[:, :], in0=pcnt[:, :], scalar1=-TT, scalar2=None, op0=ALU.bitwise_and))
            step(e.memset(pcs[0][:, :], 0))
            step(e.memset(pcs[1][:, :], 0))
            step(e.tensor_copy(out=pcs[0][:, 16:32], in_=pcnt[:, :]))
            srcb = 0
            for sh in (1, 2, 4, 8):
                step(e.tensor_tensor(out=pcs[1 - srcb][:, 16:32], in0=pcs[srcb][:, 16:32],
                                     in1=pcs[srcb][:, 16 - sh:32 - sh], op=ALU.add))
                srcb = 1 - srcb
            step(e.tensor_copy(out=pbasef[:, :], in_=pcs[srcb][:, 15:31]))
            e.wait_ge(ctx.sem(M2), bM2 + 1)
            step(e.tensor_copy(out=offs_sb[:, :], in_=ps_off[0:32, 0:NEXP]))
            e.wait_ge(ctx.sem(M2), bM2 + 3)
            step(e.tensor_tensor(out=c1[:, :], in0=offs_sb[:, :], in1=ps_bc[0:32, 0:NEXP], op=ALU.add))
            step(e.tensor_scalar(out=ctab_sb[:, :], in0=c1[:, :], scalar1=-1.0, scalar2=None, op0=ALU.add))
            step(e.tensor_scalar(out=pb256[:, :], in0=ps_bc[0:32, 0:NEXP], scalar1=1.0 / TT,
                                 scalar2=None, op0=ALU.mult))
            step(e.tensor_scalar(out=cmpm[:, :], in0=pb256[:, :], scalar1=iota32[:, :],
                                 scalar2=None, op0=ALU.is_le))
            step(e.reduce_sum(ecnt[:, :], cmpm[:, :], axis=AX.X))
            step(e.tensor_scalar(out=ecnt[:, :], in0=ecnt[:, :], scalar1=-1.0, scalar2=None, op0=ALU.add))
            step(e.tensor_copy(out=etab_i[:, :], in_=ecnt[:, :]))
        ctx.bump(V2, 19)

        def s2_sync(e):
            e.wait_ge(ctx.sem(V2), bV2 + 19)
            e.dma_start(out=ctab_d[:, :], in_=ctab_sb[:, :]).then_inc(ctx.sem(D2), 16)
            e.dma_start(out=etab_d.rearrange("a b -> b a"), in_=etab_i[0:32, :],
                        allow_slow_non_contiguous=True).then_inc(ctx.sem(D2), 16)
        ctx.bump(D2, 32)

        block.gpsimd(s2_gpsimd)
        block.tensor(s2_tensor)
        block.vector(s2_vector)
        block.sync(s2_sync)
        D2_s2 = ctx.val(D2)

        tokid = sb("u2tk", [128, 1], U32)
        crow = sb("u2cr", [1, NEXP], F32)
        cbsel = sb("u2cb", [128, NEXP], F32)
        slotf = sb("u2sf", [128, 1], F32)
        slot_i = sb("u2si", [128, 1], I32)
        slot_u = sb("u2su", [128, 1], U32)
        fa = sb("u2fa", [128, 1], I32)
        fb = sb("u2fb", [128, 1], I32)
        fc = sb("u2fc", [128, 1], U32)
        perm_sb = sb("u2pm", [128, 64], U32)
        etab_sb = sb("u2et", [1, NCT], I32)
        etab_f = sb("u2ef", [1, NCT], F32)
        iota_p = sb("u2ip", [128, 1], F32)
        idx_all = sb("u2ia", [128, NCT], U32)
        idx_f = sb("u2if2", [128, NCT], F32)
        eqr = sb("u2eq", [1, NCT], F32) if use_skip else None
        idx_skp = sb("u2is", [128, NCT], U32) if use_skip else None
        bbV, bbM, bbG, bbD = ctx.val(V), ctx.val(M), ctx.val(G2), ctx.val(D2)
        bbGS = ctx.val("G2S")
        V2B = 10

        def s2b_sync(e):
            for i in range(NTILE):
                e.wait_ge(ctx.sem(D2), D2_s2)
                if i >= 1:
                    e.wait_ge(ctx.sem(M), bbM + i)
                e.dma_start(out=crow[:, :], in_=ctab_d[i:i + 1, :]).then_inc(ctx.sem(D2), 16)
        ctx.bump(D2, 16 * NTILE)

        def s2b_tensor(e):
            for i in range(NTILE):
                e.wait_ge(ctx.sem(D2), bbD + 16 * (i + 1))
                if i >= 1:
                    e.wait_ge(ctx.sem(V), bbV + V2B * (i - 1) + 1)
                e.matmul(ps_bc[:, 0:NEXP], ONES128F[:, :], crow[:, :], start=True, stop=True).then_inc(ctx.sem(M), 1)
        ctx.bump(M, NTILE)

        def s2b_vector(e):
            for i in range(NTILE):
                vb = bbV + V2B * i

                def step(k, inst):
                    inst.then_inc(ctx.sem(V), 1)
                    e.wait_ge(ctx.sem(V), vb + k)
                e.wait_ge(ctx.sem(M), bbM + i + 1)
                if i >= 1:
                    e.wait_ge(ctx.sem("G2S"), bbGS + 32 * i)
                step(1, e.tensor_tensor(out=cbsel[:, :], in0=ps_bc[:, 0:NEXP],
                                        in1=oh_all[:, i * NEXP:(i + 1) * NEXP], op=ALU.mult))
                step(2, e.reduce_sum(slotf[:, :], cbsel[:, :], axis=AX.X))
                step(3, e.tensor_tensor(out=slotf[:, :], in0=slotf[:, :], in1=rank_all[:, i:i + 1], op=ALU.add))
                step(4, e.tensor_copy(out=slot_i[:, :], in_=slotf[:, :]))
                step(5, e.tensor_copy(out=slot_u[:, :], in_=slot_i[:, :]))
                step(6, e.tensor_scalar(out=fa[:, :], in0=slot_i[:, :], scalar1=127, scalar2=None, op0=ALU.bitwise_and))
                step(7, e.tensor_scalar(out=fa[:, :], in0=fa[:, :], scalar1=64, scalar2=None, op0=ALU.mult))
                step(8, e.tensor_scalar(out=fb[:, :], in0=slot_i[:, :], scalar1=7, scalar2=None, op0=ALU.arith_shift_right))
                step(9, e.tensor_tensor(out=fb[:, :], in0=fa[:, :], in1=fb[:, :], op=ALU.add))
                step(10, e.tensor_copy(out=fc[:, :], in_=fb[:, :]))
        ctx.bump(V, V2B * NTILE)

        def s2b_gpsimd(e):
            for i in range(NTILE):
                e.wait_ge(ctx.sem(V), bbV + V2B * i + V2B)
                if i >= 1:
                    e.wait_ge(ctx.sem("G2S"), bbGS + 32 * i)
                e.iota(tokid[:, :], [[0, 1]], base=128 * i, channel_multiplier=1,
                       allow_small_or_imprecise_dtypes=True).then_inc(ctx.sem(G2), 1)
                e.wait_ge(ctx.sem(G2), bbG + i + 1)
                e.indirect_dma_start(out=perm_d[:, :],
                                     out_offset=IndirectOffsetOnAxis(fc[:, :], 0),
                                     in_=tokid[:, :], in_offset=None).then_inc(ctx.sem("G2S"), 16)
                e.indirect_dma_start(out=wslot_d[:, :],
                                     out_offset=IndirectOffsetOnAxis(slot_u[:, :], 0),
                                     in_=w_all[:, i:i + 1], in_offset=None).then_inc(ctx.sem("G2S"), 16)
            e.wait_ge(ctx.sem("G2S"), bbGS + 32 * NTILE)
            e.wait_ge(ctx.sem(D2), D2_s2)
            e.dma_start(out=perm_sb[:, :], in_=perm_d.rearrange("(p c) o -> p (c o)", p=128)).then_inc(ctx.sem("G2S"), 16)
            e.dma_start(out=etab_sb[:, :], in_=etab_d[:, :]).then_inc(ctx.sem("G2S"), 16)
            e.iota(iota_p[:, :], [[0, 1]], base=0, channel_multiplier=1,
                   allow_small_or_imprecise_dtypes=True).then_inc(ctx.sem(G2), 1)
            e.wait_ge(ctx.sem("G2S"), bbGS + 32 * NTILE + 32)
        ctx.bump(G2, NTILE + 1)
        ctx.bump("G2S", 32 * NTILE + 32)

        block.sync(s2b_sync)
        block.tensor(s2b_tensor)
        block.vector(s2b_vector)
        block.gpsimd(s2b_gpsimd)

        ps_eb = pa[7]
        bIV, bIM = ctx.val(V), ctx.val(M)

        def s2c_vector(e):
            e.wait_ge(ctx.sem(G2), ctx_G2_final)
            e.wait_ge(ctx.sem("G2S"), ctx_G2S_final)
            e.tensor_copy(out=etab_f[:, :], in_=etab_sb[:, :]).then_inc(ctx.sem(V), 1)
            if use_skip:
                e.memset(eqr[:, :], 0.0).then_inc(ctx.sem(V), 1)
                e.wait_ge(ctx.sem(V), bIV + 2)
                e.tensor_tensor(out=eqr[:, 1:NCT], in0=etab_f[:, 1:NCT], in1=etab_f[:, 0:NCT - 1],
                                op=ALU.is_equal).then_inc(ctx.sem(V), 1)
            nv = 3 if use_skip else 1
            e.wait_ge(ctx.sem(M), bIM + 1)
            e.tensor_scalar(out=idx_f[:, :], in0=ps_eb[:, 0:NCT], scalar1=512.0, scalar2=iota_p[:, :],
                            op0=ALU.mult, op1=ALU.add).then_inc(ctx.sem(V), 1)
            e.wait_ge(ctx.sem(V), bIV + nv + 1)
            e.tensor_copy(out=idx_all[:, :], in_=idx_f[:, :]).then_inc(ctx.sem(V), 1)
            e.wait_ge(ctx.sem(V), bIV + nv + 2)
            if use_skip:
                e.wait_ge(ctx.sem(M), bIM + 2)
                e.scalar_tensor_tensor(out=idx_f[:, :], in0=ps_eb[:, 0:NCT], scalar=float(2 ** 14),
                                       in1=idx_f[:, :], op0=ALU.mult, op1=ALU.add).then_inc(ctx.sem(V), 1)
                e.wait_ge(ctx.sem(V), bIV + nv + 3)
                e.tensor_copy(out=idx_skp[:, :], in_=idx_f[:, :]).then_inc(ctx.sem(V), 1)
                e.wait_ge(ctx.sem(V), bIV + nv + 4)

        def s2c_tensor(e):
            e.wait_ge(ctx.sem(V), bIV + 1)
            e.matmul(ps_eb[:, 0:NCT], ONES128F[:, :], etab_f[:, :], start=True, stop=True).then_inc(ctx.sem(M), 1)
            if use_skip:
                e.wait_ge(ctx.sem(V), bIV + 5)
                e.matmul(ps_eb[:, 0:NCT], ONES128F[:, :], eqr[:, :], start=True, stop=True).then_inc(ctx.sem(M), 1)
        ctx_G2_final = ctx.val(G2)
        ctx_G2S_final = ctx.val("G2S")
        block.vector(s2c_vector)
        block.tensor(s2c_tensor)
        ctx.bump(V, 7 if use_skip else 3)
        ctx.bump(M, 2 if use_skip else 1)

        xg = [sb(f"u3x{i}", [128, 2 * D], BF16) for i in range(2)]
        rg = [sb(f"u3r{i}", [128, 2 * D], F32) for i in range(2)] if do_res else None
        xT = [sb(f"u3t{i}", [128, 8 * TT], BF16) for i in range(2)]
        NWB = 4 if use_skip else 2
        wq1 = [sb(f"u3w{i}", [128, 8 * 1024], BF16) for i in range(NWB)]
        wq2 = [sb(f"u3v{i}", [128, 8 * 1024], BF16) for i in range(NWB)]
        hsb = [sb(f"u3h{i}", [128, 8 * TT], BF16) for i in range(2)]
        wrow = sb("u3wr", [1, TT], F32)
        wrowb = sb("u3wb", [1, TT], BF16)
        wbc = sb("u3wc", [128, TT], F32)
        ybf = sb("u3yb", [128, 8 * TT], BF16)
        ytok = [sb(f"u3y{i}", [128, 2 * D], BF16) for i in range(2)]
        og = [sb(f"u3o{i}", [128, 2 * D], F32) for i in range(2)] if do_res else None
        psY = pa[0:4]
        psH = pa[4:6]
        psW = pa[6]

        TI, TO = "T3I", "T3O"
        M1, PQ, PM, SIL = "M31", "M3Q", "M3T", "S3S"
        YV, VF, VC = "Y3V", "V3F", "V3C"
        bTI, bTO = ctx.val(TI), ctx.val(TO)
        bM1, bPQ, bPM, bSIL = ctx.val(M1), ctx.val(PQ), ctx.val(PM), ctx.val(SIL)
        bYV, bVF, bVC = ctx.val(YV), ctx.val(VF), ctx.val(VC)
        idx_ready = ctx.val(V)
        bGX = ctx.val("GXG")
        bGR = [ctx.val("GRG0"), ctx.val("GRG1")]
        bGS = [ctx.val("GSC0"), ctx.val("GSC1")]
        bWG = [ctx.val(f"WG{i}") for i in range(4 if use_skip else 2)]

        def s3_gpsimd(e):
          with e.register("bc" + tag) as bcreg, e.register("b2" + tag) as bc2:
            e.reg_mov(bcreg, NTOK - 1)
            e.reg_mov(bc2, NEXP * 512 - 1)
            e.wait_ge(ctx.sem(V), idx_ready)
            if do_ln:
                e.wait_ge(ctx.sem("UXS0"), bXS[0] + 16 * (NTILE // 2))
                e.wait_ge(ctx.sem("UXS1"), bXS[1] + 16 * (NTILE // 2))
            for t in range(NCT + 1):
                if t < NCT:
                    if t >= 2:
                        e.wait_ge(ctx.sem(TI), bTI + 16 * 2 * (t - 1))
                        if do_res:
                            e.wait_ge(ctx.sem(VF), bVF + (t - 1))
                    elif t == 1 and use_skip:
                        e.wait_ge(ctx.sem("GXG"), bGX + 32)
                    for g in range(2):
                        e.indirect_dma_start(out=xg[t % 2][:, g * D:(g + 1) * D], out_offset=None,
                                             in_=zbuf[:, :],
                                             in_offset=IndirectOffsetOnAxis(perm_sb[:, 2 * t + g:2 * t + g + 1], 0),
                                             bounds_check=bcreg, oob_is_err=False).then_inc(ctx.sem("GXG"), 16)
                    if do_res:
                        for g in range(2):
                            e.indirect_dma_start(out=rg[t % 2][:, g * D:(g + 1) * D], out_offset=None,
                                                 in_=src[:, :],
                                                 in_offset=IndirectOffsetOnAxis(perm_sb[:, 2 * t + g:2 * t + g + 1], 0),
                                                 bounds_check=bcreg, oob_is_err=False).then_inc(ctx.sem(f"GRG{t % 2}"), 16)
                    for q in range(4):
                        qg = 4 * t + q
                        if use_skip:
                            buf = q
                            wsem = f"WG{q}"
                            if t >= 1:
                                e.wait_ge(ctx.sem(PQ), bPQ + 4 * (t - 1) + q + 1)
                            idxsrc = idx_skp
                            kw = dict(bounds_check=bc2, oob_is_err=False)
                        else:
                            buf = qg % 2
                            wsem = f"WG{qg % 2}"
                            if qg >= 2:
                                e.wait_ge(ctx.sem(PQ), bPQ + qg - 1)
                            idxsrc = idx_all
                            kw = {}
                        e.indirect_dma_start(out=wq1[buf][:, :], out_offset=None,
                                             in_=w1pre[:, :],
                                             in_offset=IndirectOffsetOnAxis(idxsrc[:, t:t + 1], 0),
                                             element_offset=q * 128 * 8192, **kw).then_inc(ctx.sem(wsem), 16)
                        e.indirect_dma_start(out=wq2[buf][:, :], out_offset=None,
                                             in_=w2pre[:, :],
                                             in_offset=IndirectOffsetOnAxis(idxsrc[:, t:t + 1], 0),
                                             element_offset=q * 128 * 8192, **kw).then_inc(ctx.sem(wsem), 16)
                if t >= 1:
                    if do_res:
                        e.wait_ge(ctx.sem(VF), bVF + t)
                    else:
                        e.wait_ge(ctx.sem(TO), bTO + 16 * t)
                    for g in range(2):
                        sap = og[(t - 1) % 2][:, g * D:(g + 1) * D] if do_res else \
                            ytok[(t - 1) % 2][:, g * D:(g + 1) * D]
                        dap = dst[:, :] if do_res else zbuf[:, :]
                        e.indirect_dma_start(out=dap,
                                             out_offset=IndirectOffsetOnAxis(perm_sb[:, 2 * (t - 1) + g:2 * (t - 1) + g + 1], 0),
                                             in_=sap, in_offset=None,
                                             bounds_check=bcreg, oob_is_err=False).then_inc(ctx.sem(f"GSC{(t - 1) % 2}"), 16)
            e.wait_ge(ctx.sem("GXG"), bGX + 16 * 2 * NCT)
            for par in range(2):
                if do_res:
                    e.wait_ge(ctx.sem(f"GRG{par}"), bGR[par] + 32 * (NCT // 2))
                e.wait_ge(ctx.sem(f"GSC{par}"), bGS[par] + 32 * (NCT // 2))
            if use_skip:
                for q in range(4):
                    e.wait_ge(ctx.sem(f"WG{q}"), bWG[q] + 32 * NCT)
            else:
                for par in range(2):
                    e.wait_ge(ctx.sem(f"WG{par}"), bWG[par] + 16 * 4 * NCT)
        ctx.bump("GXG", 16 * 2 * NCT)
        if do_res:
            ctx.bump("GRG0", 32 * (NCT // 2))
            ctx.bump("GRG1", 32 * (NCT // 2))
        ctx.bump("GSC0", 32 * (NCT // 2))
        ctx.bump("GSC1", 32 * (NCT // 2))
        if use_skip:
            for q in range(4):
                ctx.bump(f"WG{q}", 32 * NCT)
        else:
            ctx.bump("WG0", 16 * 4 * NCT)
            ctx.bump("WG1", 16 * 4 * NCT)

        def s3_sync(e):
            if True:
                for t in range(NCT + 1):
                    if t < NCT:
                        e.wait_ge(ctx.sem("GXG"), bGX + 16 * 2 * (t + 1))
                        if t >= 2:
                            e.wait_ge(ctx.sem(PM), bPM + (t - 1))
                        e.wait_ge(ctx.sem("G2S"), bbGS + 32 * NTILE)
                        if t >= 1:
                            e.wait_ge(ctx.sem(VC), bVC + 2 * (t - 1) + 1)
                        e.dma_start(out=wrow[:, :],
                                    in_=wslot_d[t * TT:(t + 1) * TT, :].rearrange("a b -> b a")).then_inc(ctx.sem(TI), 16)
                        e.dma_start_transpose(out=b3(xT[t % 2][:, :]),
                                              in_=b3(xg[t % 2][:, :])).then_inc(ctx.sem(TI), 16)
                    if t >= 1:
                        e.wait_ge(ctx.sem(YV), bYV + t)
                        if t >= 3:
                            if do_res:
                                e.wait_ge(ctx.sem(VF), bVF + (t - 2))
                            else:
                                e.wait_ge(ctx.sem(f"GSC{(t - 3) % 2}"), bGS[(t - 3) % 2] + 32 * ((t - 3) // 2 + 1))
                        e.dma_start_transpose(out=b3(ytok[(t - 1) % 2][:, :]),
                                              in_=b3(ybf[:, :])).then_inc(ctx.sem(TO), 16)
        ctx.bump(TI, 16 * 2 * NCT)
        ctx.bump(TO, 16 * NCT)

        def s3_tensor(e):
            for t in range(NCT):
                e.wait_ge(ctx.sem(TI), bTI + 16 * 2 * (t + 1))
                e.wait_ge(ctx.sem(YV), bYV + t)
                for q in range(4):
                    qg = 4 * t + q
                    buf = q if use_skip else qg % 2
                    if use_skip:
                        e.wait_ge(ctx.sem(f"WG{q}"), bWG[q] + 32 * (t + 1))
                    else:
                        e.wait_ge(ctx.sem(f"WG{qg % 2}"), bWG[qg % 2] + 16 * 2 * (qg // 2 + 1))
                    for hb in range(8):
                        gidx = qg * 8 + hb
                        if gidx >= 2:
                            e.wait_ge(ctx.sem(SIL), bSIL + gidx - 1)
                        bank = psH[gidx % 2]
                        for kb in range(8):
                            inst = e.matmul(bank[:, 0:TT],
                                            wq1[buf][:, kb * 1024 + hb * 128:kb * 1024 + (hb + 1) * 128],
                                            xT[t % 2][:, :].rearrange("p (g c) -> p g c", c=D)[:, :, kb * 128:(kb + 1) * 128],
                                            start=(kb == 0), stop=(kb == 7))
                        inst.then_inc(ctx.sem(M1), 1)
                    e.wait_ge(ctx.sem(SIL), bSIL + qg * 8 + 8)
                    for mb in range(8):
                        for hb in range(8):
                            inst = e.matmul(psY[mb // 2][:, (mb % 2) * TT:(mb % 2 + 1) * TT],
                                            wq2[buf][:, hb * 1024 + mb * 128:hb * 1024 + (mb + 1) * 128],
                                            hsb[q % 2][:, hb * TT:(hb + 1) * TT],
                                            start=(q == 0 and hb == 0 and mb % 2 == 0), stop=(q == 3 and hb == 7), skip_group_check=True)
                    inst.then_inc(ctx.sem(PQ), 1)
                e.wait_ge(ctx.sem(VC), bVC + 2 * t + 1)
                e.matmul(psW[:, 0:TT], ONES128B[:, :], wrowb[:, :], start=True, stop=True).then_inc(ctx.sem(PM), 1)
        ctx.bump(M1, 32 * NCT)
        ctx.bump(PQ, 4 * NCT)
        ctx.bump(PM, NCT)

        def s3_scalar(e):
            for t in range(NCT):
                for q in range(4):
                    qg = 4 * t + q
                    if qg >= 2:
                        e.wait_ge(ctx.sem(PQ), bPQ + qg - 1)
                    for hb in range(8):
                        gidx = qg * 8 + hb
                        e.wait_ge(ctx.sem(M1), bM1 + gidx + 1)
                        e.activation(out=hsb[q % 2][:, hb * TT:(hb + 1) * TT],
                                     in_=psH[gidx % 2][:, 0:TT], func=AF.Silu).then_inc(ctx.sem(SIL), 1)
        ctx.bump(SIL, 32 * NCT)

        def s3_vector(e):
            for t in range(NCT):
                e.wait_ge(ctx.sem(TI), bTI + 16 * 2 * (t + 1))
                e.tensor_copy(out=wrowb[:, :], in_=wrow[:, :]).then_inc(ctx.sem(VC), 1)
                e.wait_ge(ctx.sem(VC), bVC + 2 * t + 1)
                e.wait_ge(ctx.sem(PM), bPM + t + 1)
                if t >= 1:
                    e.wait_ge(ctx.sem(TO), bTO + 16 * t)
                e.tensor_copy(out=wbc[:, :], in_=psW[:, 0:TT]).then_inc(ctx.sem(VC), 1)
                e.wait_ge(ctx.sem(VC), bVC + 2 * t + 2)
                for mb in range(8):
                    inst = e.tensor_tensor(out=ybf[:, :].rearrange("p (g c) -> p g c", c=D)[:, :, mb * 128:(mb + 1) * 128],
                                           in0=psY[mb // 2][:, (mb % 2) * TT:(mb % 2 + 1) * TT],
                                           in1=wbc[:, :], op=ALU.mult)
                inst.then_inc(ctx.sem(YV), 1)
                if do_res:
                    e.wait_ge(ctx.sem(TO), bTO + 16 * (t + 1))
                    e.wait_ge(ctx.sem(f"GRG{t % 2}"), bGR[t % 2] + 32 * (t // 2 + 1))
                    if t >= 2:
                        e.wait_ge(ctx.sem(f"GSC{t % 2}"), bGS[t % 2] + 32 * ((t - 2) // 2 + 1))
                    for g in range(2):
                        inst = e.tensor_tensor(out=og[t % 2][:, g * D:(g + 1) * D],
                                               in0=rg[t % 2][:, g * D:(g + 1) * D],
                                               in1=ytok[t % 2][:, g * D:(g + 1) * D], op=ALU.add)
                    inst.then_inc(ctx.sem(VF), 1)
        ctx.bump(VC, 2 * NCT)
        ctx.bump(YV, NCT)
        if do_res:
            ctx.bump(VF, NCT)

        block.gpsimd(s3_gpsimd)
        block.sync(s3_sync)
        block.tensor(s3_tensor)
        block.scalar(s3_scalar)
        block.vector(s3_vector)






def b3(ap):
    return ap.rearrange("p (k c) -> p k c", c=128)


def tmslice(xT_sb, kb, c0, w):
    return xT_sb[:, :].rearrange("p (i c) -> p i c", c=D)[:, c0 // 128:(c0 + w) // 128, kb * 128:(kb + 1) * 128]

def emit_lnT(ctx, block, tag, src_rows, row0, ntile, xT_sb, col_of, lnrow_extra=None):
    nc = ctx.nc
    with ExitStack() as es:
        sb = lambda n, s, d: sbuf(ctx, n + tag, s, d, es)
        xrow = [sb(f"lt{i}", [128, D], F32) for i in range(2)]
        xc = [sb(f"lc{i}", [128, D], F32) for i in range(2)]
        sq = sb("lsq", [128, D], F32)
        lnrow = [sb(f"ll{i}", [128, D], BF16) for i in range(2)]
        ssum = sb("lsm", [128, 1], F32)
        negmean = sb("lnm", [128, 1], F32)
        varD = sb("lvd", [128, 1], F32)
        lnv = sb("llv", [128, 1], F32)
        rstd = sb("lrs", [128, 1], F32)
        V, S = "LTV", "LTS"
        bV, bS = ctx.val(V), ctx.val(S)
        bLD = [ctx.val("LTL0"), ctx.val("LTL1")]
        bTP = [ctx.val("LTT0"), ctx.val("LTT1")]
        bXS = [ctx.val("LTX0"), ctx.val("LTX1")]

        def f_sync(e):
            for i in range(ntile):
                if i >= 2:
                    e.wait_ge(ctx.sem(V), bV + 5 * (i - 2) + 3)
                e.dma_start(out=xrow[i % 2][:, :], in_=src_rows[row0 + i * 128:row0 + (i + 1) * 128, :]).then_inc(ctx.sem(f"LTL{i % 2}"), 16)
                e.wait_ge(ctx.sem(V), bV + 5 * i + 5)
                e.dma_start_transpose(out=b3(xT_sb[:, i * D:(i + 1) * D]),
                                      in_=b3(lnrow[i % 2][:, :])).then_inc(ctx.sem(f"LTT{i % 2}"), 16)
        ctx.bump("LTL0", 16 * ((ntile + 1) // 2))
        ctx.bump("LTL1", 16 * (ntile // 2))
        ctx.bump("LTT0", 16 * ((ntile + 1) // 2))
        ctx.bump("LTT1", 16 * (ntile // 2))

        def f_scalar(e):
            for i in range(ntile):
                e.wait_ge(ctx.sem(V), bV + 5 * i + 1)
                e.activation(out=negmean[:], in_=ssum[:], func=AF.Copy, scale=-1.0 / D).then_inc(ctx.sem(S), 1)
                e.wait_ge(ctx.sem(V), bV + 5 * i + 4)
                e.activation(out=lnv[:], in_=varD[:], func=AF.Ln, bias=P_EPS[0][:, :], scale=1.0 / D).then_inc(ctx.sem(S), 1)
                e.wait_ge(ctx.sem(S), bS + 3 * i + 2)
                e.activation(out=rstd[:], in_=lnv[:], func=AF.Exp, scale=-0.5).then_inc(ctx.sem(S), 1)
        ctx.bump(S, 3 * ntile)

        def f_vector(e):
            for i in range(ntile):
                b = i % 2
                vb = bV + 5 * i

                def step(k, inst):
                    inst.then_inc(ctx.sem(V), 1)
                    e.wait_ge(ctx.sem(V), vb + k)
                e.wait_ge(ctx.sem(f"LTL{i % 2}"), bLD[i % 2] + 16 * (i // 2 + 1))
                if i >= 1:
                    e.wait_ge(ctx.sem(S), bS + 3 * (i - 1) + 3)
                step(1, e.reduce_sum(ssum[:], xrow[b][:, :], axis=AX.X))
                e.wait_ge(ctx.sem(S), bS + 3 * i + 1)
                step(2, e.tensor_scalar(out=xc[b][:, :], in0=xrow[b][:, :], scalar1=negmean[:],
                                        scalar2=None, op0=ALU.add))
                step(3, e.tensor_tensor(out=sq[:, :], in0=xc[b][:, :], in1=xc[b][:, :], op=ALU.mult))
                step(4, e.reduce_sum(varD[:], sq[:, :], axis=AX.X))
                e.wait_ge(ctx.sem(S), bS + 3 * i + 3)
                if i >= 2:
                    e.wait_ge(ctx.sem(f"LTT{i % 2}"), bTP[i % 2] + 16 * ((i - 2) // 2 + 1))
                    if lnrow_extra is not None:
                        e.wait_ge(ctx.sem(f"LTX{i % 2}"), bXS[i % 2] + 16 * ((i - 2) // 2 + 1))
                step(5, e.tensor_scalar(out=lnrow[b][:, :], in0=xc[b][:, :], scalar1=rstd[:],
                                        scalar2=None, op0=ALU.mult))
        ctx.bump(V, 5 * ntile)

        if lnrow_extra is not None:
            dap, drow0 = lnrow_extra

            def f_gpsimd(e):
                for i in range(ntile):
                    e.wait_ge(ctx.sem(V), bV + 5 * i + 5)
                    e.dma_start(out=dap[drow0 + i * 128:drow0 + (i + 1) * 128, :],
                                in_=lnrow[i % 2][:, :]).then_inc(ctx.sem(f"LTX{i % 2}"), 16)
            ctx.bump("LTX0", 16 * ((ntile + 1) // 2))
            ctx.bump("LTX1", 16 * (ntile // 2))
            block.gpsimd(f_gpsimd)

        block.sync(f_sync)
        block.scalar(f_scalar)
        block.vector(f_vector)
    return (ctx.val("LTT0"), ctx.val("LTT1"), ctx.val(V))


P_EPS = [None]


def emit_attn(ctx, block, P, tag):
    nc = ctx.nc
    resid = P["resid"].ap()
    AMASK = P["amask_sb"]
    IDENT = None
    pa = P["psum"]
    for b in range(NB):
        btag = f"{tag}b{b}"
        with ExitStack() as es:
            sb = lambda n, s, d: sbuf(ctx, n + btag, s, d, es)
            QT = sb("aQ", [128, 8 * SEQ], BF16)
            KT = sb("aK", [128, 8 * SEQ], BF16)
            VTOK = sb("aV", [128, 16 * 1024], BF16)
            AVT = sb("aA", [128, 8 * SEQ], BF16)
            wt = sb("aW", [128, 8 * 1024], BF16)
            es3 = es.enter_context(ExitStack())
            sb3 = lambda n, s, d: sbuf(ctx, n + btag, s, d, es3)
            row0 = b * SEQ
            with ExitStack() as es2:
                lnxT = sbuf(ctx, "aL" + btag, [128, 8 * SEQ], BF16, es2)
                tp0, tp1, _ = emit_lnT(ctx, block, "al" + btag, resid, row0, SEQ // 128,
                                       lnxT, lambda i, db: db * SEQ + i * 128)

                WT, PM2, EV2 = "AWT", "APM", "AEV"
                bWT, bPM2, bEV2 = ctx.val(WT), ctx.val(PM2), ctx.val(EV2)
                wnames = [P["wq_bf"].ap(), P["wk_bf"].ap(), P["wv_bf"].ap()]

                def ph2_sync(e):
                    for m in range(3):
                        if m >= 1:
                            e.wait_ge(ctx.sem(PM2), bPM2 + m * 32)
                        for jb in range(8):
                            e.dma_start_transpose(out=b3(wt[:, jb * 1024:(jb + 1) * 1024]),
                                                  in_=b3(wnames[m][jb * 128:(jb + 1) * 128, :])).then_inc(ctx.sem(WT), 16)
                ctx.bump(WT, 16 * 8 * 3)

                def ph2_tensor(e):
                    e.wait_ge(ctx.sem("LTT0"), tp0)
                    e.wait_ge(ctx.sem("LTT1"), tp1)
                    for m in range(3):
                        e.wait_ge(ctx.sem(WT), bWT + 16 * 8 * (m + 1))
                        if m < 2:
                            for jb in range(8):
                                for c in range(4):
                                    g = m * 32 + jb * 4 + c
                                    if g >= 2:
                                        e.wait_ge(ctx.sem(EV2), bEV2 + g - 1)
                                    bank = pa[g % 2]
                                    for kb in range(8):
                                        inst = e.matmul(bank[:, 0:512],
                                                        wt[:, jb * 1024 + kb * 128:jb * 1024 + (kb + 1) * 128],
                                                        tmslice(lnxT, kb, c * 512, 512),
                                                        start=(kb == 0), stop=(kb == 7))
                                    inst.then_inc(ctx.sem(PM2), 1)
                        else:
                            for i in range(16):
                                for jh in range(2):
                                    g = 64 + i * 2 + jh
                                    if g >= 2:
                                        e.wait_ge(ctx.sem(EV2), bEV2 + g - 1)
                                    bank = pa[g % 2]
                                    for kb in range(8):
                                        inst = e.matmul(bank[:, 0:512],
                                                        lnxT[:, i * D + kb * 128:i * D + (kb + 1) * 128],
                                                        wt[:, :].rearrange("p (j c) -> p j c", c=1024)[:, 4 * jh:4 * (jh + 1), kb * 128:(kb + 1) * 128],
                                                        start=(kb == 0), stop=(kb == 7))
                                    inst.then_inc(ctx.sem(PM2), 1)
                ctx.bump(PM2, 96)

                def ph2_vector(e):
                    for g in range(96):
                        e.wait_ge(ctx.sem(PM2), bPM2 + g + 1)
                        if g < 32:
                            jb, c = g // 4, g % 4
                            inst = e.tensor_copy(out=QT[:, jb * SEQ + c * 512:jb * SEQ + (c + 1) * 512],
                                                 in_=pa[g % 2][:, 0:512])
                        elif g < 64:
                            jb, c = (g - 32) // 4, (g - 32) % 4
                            inst = e.tensor_copy(out=KT[:, jb * SEQ + c * 512:jb * SEQ + (c + 1) * 512],
                                                 in_=pa[g % 2][:, 0:512])
                        else:
                            i, jh = (g - 64) // 2, (g - 64) % 2
                            inst = e.tensor_copy(out=VTOK[:, i * 1024 + jh * 512:i * 1024 + (jh + 1) * 512],
                                                 in_=pa[g % 2][:, 0:512])
                        inst.then_inc(ctx.sem(EV2), 1)
                ctx.bump(EV2, 96)

                block.sync(ph2_sync)
                block.tensor(ph2_tensor)
                block.vector(ph2_vector)

            S_sb = [sb3(f"aS{i}", [128, 640], F32) for i in range(2)]
            Pexp = [sb3(f"aP{i}", [128, 640], F32) for i in range(2)]
            Pbf = [sb3(f"aB{i}", [128, 640], BF16) for i in range(2)]
            PT = [sb3(f"aT{i}", [128, 5 * 128], BF16) for i in range(3)]
            nmx = [sb3(f"aM{i}", [128, 1], F32) for i in range(2)]
            sexp = [sb3(f"aX{i}", [128, 1], F32) for i in range(2)]
            rv = [sb3(f"aR{i}", [128, 1], F32) for i in range(2)]
            orow = [sb3(f"aO{i}", [128, D], F32) for i in range(2)]
            obf = sb3("aF", [128, 8 * 512], BF16)
            otk = [sb3(f"aU{i}", [128, D], BF16) for i in range(2)]
            pairs = []
            for h in range(HEADS):
                for qb in range(SEQ // 128):
                    pairs.append((h, qb, min(qb + 1, 5)))
            NP = len(pairs)
            SA = [pa[0], pa[1], pa[4]]
            SBB = [pa[2], pa[3], pa[5]]
            AVB = [pa[6], pa[7]]
            LEAD = 6
            PE3, DV3, AC3 = "A3P", "A3D", "A3A"
            bPE3, bDV3, bAC3 = ctx.val(PE3), ctx.val(DV3), ctx.val(AC3)
            bTP3 = [ctx.val("A3X0"), ctx.val("A3X1")]
            dv_mask, dv_rowmax, dv_recip = [0] * NP, [0] * NP, [0] * NP
            c = 0
            for p in range(NP):
                c += (2 if pairs[p][2] == 5 else 1); dv_mask[p] = c
                c += 1; dv_rowmax[p] = c
                if p >= 1:
                    c += 1; dv_recip[p - 1] = c
            c += 1; dv_recip[NP - 1] = c
            DV3_total = c
            ac_exp, ac_pbf, ac_evac = [0] * NP, [0] * NP, [0] * NP
            c = 0
            for p in range(NP):
                c += 1; ac_exp[p] = c
                if p >= 1:
                    c += 1; ac_pbf[p - 1] = c
                if p >= 2:
                    c += 1; ac_evac[p - 2] = c
            c += 1; ac_pbf[NP - 1] = c
            c += 1; ac_evac[NP - 2] = c
            c += 1; ac_evac[NP - 1] = c
            AC3_total = c
            pe_sc, pe_av = [0] * NP, [0] * NP
            c = 0
            for q in range(min(LEAD, NP)):
                c += 1; pe_sc[q] = c
            for p in range(NP):
                c += 1; pe_av[p] = c
                if p + LEAD < NP:
                    c += 1; pe_sc[p + LEAD] = c
            PE3_total = c

            def ph3_tensor(e):
                e.wait_ge(ctx.sem(EV2), bEV2 + 96)

                def scores(q):
                    h, qb, nk = pairs[q]
                    hb2, hoff = h // 2, (h % 2) * 64
                    kb0 = qb - nk + 1
                    if q >= 3:
                        e.wait_ge(ctx.sem(DV3), bDV3 + dv_mask[q - 3])
                    q_ap = QT[hoff:hoff + 64, hb2 * SEQ + qb * 128:hb2 * SEQ + (qb + 1) * 128]
                    if nk == 5:
                        e.matmul(SA[q % 3][:, 0:512], q_ap,
                                 KT[hoff:hoff + 64, hb2 * SEQ + kb0 * 128:hb2 * SEQ + (kb0 + 4) * 128],
                                 start=True, stop=True)
                        e.matmul(SBB[q % 3][:, 0:128], q_ap,
                                 KT[hoff:hoff + 64, hb2 * SEQ + (kb0 + 4) * 128:hb2 * SEQ + (kb0 + 5) * 128],
                                 start=True, stop=True).then_inc(ctx.sem(PE3), 1)
                    else:
                        e.matmul(SA[q % 3][:, 0:128 * nk], q_ap,
                                 KT[hoff:hoff + 64, hb2 * SEQ + kb0 * 128:hb2 * SEQ + (kb0 + nk) * 128],
                                 start=True, stop=True).then_inc(ctx.sem(PE3), 1)
                for q in range(min(LEAD, NP)):
                    scores(q)
                for p, (h, qb, nk) in enumerate(pairs):
                    kb0 = qb - nk + 1
                    e.wait_ge(ctx.sem(f"A3X{p % 2}"), bTP3[p % 2] + 16 * (p // 2 + 1))
                    if p >= 2:
                        e.wait_ge(ctx.sem(AC3), bAC3 + ac_evac[p - 2])
                    for t in range(nk):
                        inst = e.matmul(AVB[p % 2][0:64, 0:128],
                                        VTOK[:, (kb0 + t) * 1024 + h * 64:(kb0 + t) * 1024 + (h + 1) * 64],
                                        PT[p % 3][:, t * 128:(t + 1) * 128],
                                        start=(t == 0), stop=(t == nk - 1))
                    inst.then_inc(ctx.sem(PE3), 1)
                    if p + LEAD < NP:
                        scores(p + LEAD)
            ctx.bump(PE3, PE3_total)

            def ph3_sync(e):
                for p, (h, qb, nk) in enumerate(pairs):
                    e.wait_ge(ctx.sem(AC3), bAC3 + ac_pbf[p])
                    if p >= 3:
                        e.wait_ge(ctx.sem(PE3), bPE3 + pe_av[p - 3])
                    e.dma_start_transpose(out=b3(PT[p % 3][:, 0:128 * nk]),
                                          in_=b3(Pbf[p % 2][:, 0:128 * nk])).then_inc(ctx.sem(f"A3X{p % 2}"), 16)
            ctx.bump("A3X0", 16 * ((NP + 1) // 2))
            ctx.bump("A3X1", 16 * (NP // 2))

            def ph3_vector(e):
                for p, (h, qb, nk) in enumerate(pairs):
                    e.wait_ge(ctx.sem(PE3), bPE3 + pe_sc[p])
                    if p >= 2:
                        e.wait_ge(ctx.sem(AC3), bAC3 + ac_exp[p - 2])
                        e.wait_ge(ctx.sem(DV3), bDV3 + dv_rowmax[p - 2])
                    if nk == 5:
                        e.scalar_tensor_tensor(out=S_sb[p % 2][:, 0:512], in0=SA[p % 3][:, 0:512],
                                               scalar=0.125, in1=AMASK[:, 0:512],
                                               op0=ALU.mult, op1=ALU.add).then_inc(ctx.sem(DV3), 1)
                        e.scalar_tensor_tensor(out=S_sb[p % 2][:, 512:640], in0=SBB[p % 3][:, 0:128],
                                               scalar=0.125, in1=AMASK[:, 512:640],
                                               op0=ALU.mult, op1=ALU.add).then_inc(ctx.sem(DV3), 1)
                    else:
                        e.scalar_tensor_tensor(out=S_sb[p % 2][:, 0:128 * nk], in0=SA[p % 3][:, 0:128 * nk],
                                               scalar=0.125, in1=AMASK[:, 640 - 128 * nk:640],
                                               op0=ALU.mult, op1=ALU.add).then_inc(ctx.sem(DV3), 1)
                    e.wait_ge(ctx.sem(DV3), bDV3 + dv_mask[p])
                    e.reduce_max(nmx[p % 2][:], S_sb[p % 2][:, 0:128 * nk], axis=AX.X,
                                 negate=True).then_inc(ctx.sem(DV3), 1)
                    if p >= 1:
                        e.wait_ge(ctx.sem(AC3), bAC3 + ac_exp[p - 1])
                        e.reciprocal(rv[(p - 1) % 2][:], sexp[(p - 1) % 2][:]).then_inc(ctx.sem(DV3), 1)
                e.wait_ge(ctx.sem(AC3), bAC3 + ac_exp[NP - 1])
                e.reciprocal(rv[(NP - 1) % 2][:], sexp[(NP - 1) % 2][:]).then_inc(ctx.sem(DV3), 1)
            ctx.bump(DV3, DV3_total)

            def ph3_scalar(e):
                def pbfcopy(q):
                    nkq = pairs[q][2]
                    e.wait_ge(ctx.sem(DV3), bDV3 + dv_recip[q])
                    if q >= 2:
                        e.wait_ge(ctx.sem(f"A3X{q % 2}"), bTP3[q % 2] + 16 * ((q - 2) // 2 + 1))
                    e.activation(out=Pbf[q % 2][:, 0:128 * nkq], in_=Pexp[q % 2][:, 0:128 * nkq],
                                 func=AF.Copy, scale=rv[q % 2][:]).then_inc(ctx.sem(AC3), 1)

                def avevac(q):
                    h2, qb2, _ = pairs[q]
                    hb2, hoff = h2 // 2, (h2 % 2) * 64
                    e.wait_ge(ctx.sem(PE3), bPE3 + pe_av[q])
                    e.activation(out=AVT[hoff:hoff + 64, hb2 * SEQ + qb2 * 128:hb2 * SEQ + (qb2 + 1) * 128],
                                 in_=AVB[q % 2][0:64, 0:128], func=AF.Copy).then_inc(ctx.sem(AC3), 1)
                for p, (h, qb, nk) in enumerate(pairs):
                    e.wait_ge(ctx.sem(DV3), bDV3 + dv_rowmax[p])
                    if p >= 2:
                        e.wait_ge(ctx.sem(AC3), bAC3 + ac_pbf[p - 2])
                    e.activation(out=Pexp[p % 2][:, 0:128 * nk], in_=S_sb[p % 2][:, 0:128 * nk], func=AF.Exp,
                                 bias=nmx[p % 2][:], scale=1.0, accum_out=sexp[p % 2][:]).then_inc(ctx.sem(AC3), 1)
                    if p >= 1:
                        pbfcopy(p - 1)
                    if p >= 2:
                        avevac(p - 2)
                pbfcopy(NP - 1)
                avevac(NP - 2)
                avevac(NP - 1)
            ctx.bump(AC3, AC3_total)

            block.tensor(ph3_tensor)
            block.sync(ph3_sync)
            block.vector(ph3_vector)
            block.scalar(ph3_scalar)

            WT4, PM4, EV4, TO4, LD4, VA4, ST4 = ("A4W", "A4P", "A4E", "A4T", "A4L", "A4V", "A4S")
            bWT4, bPM4, bEV4, bVA4 = ctx.val(WT4), ctx.val(PM4), ctx.val(EV4), ctx.val(VA4)
            bTO4 = [ctx.val("A4T0"), ctx.val("A4T1")]
            bLD4 = [ctx.val("A4L0"), ctx.val("A4L1")]
            bST4 = [ctx.val("A4S0"), ctx.val("A4S1")]

            def ph4_sync(e):
                e.wait_ge(ctx.sem(PM2), bPM2 + 96)
                for jb in range(8):
                    e.dma_start_transpose(out=b3(wt[:, jb * 1024:(jb + 1) * 1024]),
                                          in_=b3(P["wo_bf"].ap()[jb * 128:(jb + 1) * 128, :])).then_inc(ctx.sem(WT4), 16)
                for c in range(4):
                    for i in range(4):
                        ti = c * 4 + i
                        if ti >= 2:
                            e.wait_ge(ctx.sem(f"A4S{ti % 2}"), bST4[ti % 2] + 16 * ((ti - 2) // 2 + 1))
                        e.dma_start(out=orow[ti % 2][:, :], in_=resid[row0 + c * 512 + i * 128:row0 + c * 512 + (i + 1) * 128, :]).then_inc(ctx.sem(f"A4L{ti % 2}"), 16)
                        e.wait_ge(ctx.sem(EV4), bEV4 + c * 8 + 8)
                        if ti >= 2:
                            e.wait_ge(ctx.sem(VA4), bVA4 + ti - 1)
                        e.dma_start_transpose(out=b3(otk[ti % 2][:, :]),
                                              in_=b3(obf[:, i * 1024:(i + 1) * 1024])).then_inc(ctx.sem(f"A4T{ti % 2}"), 16)
            ctx.bump(WT4, 16 * 8)
            ctx.bump("A4L0", 16 * 8)
            ctx.bump("A4L1", 16 * 8)
            ctx.bump("A4T0", 16 * 8)
            ctx.bump("A4T1", 16 * 8)

            def ph4_tensor(e):
                e.wait_ge(ctx.sem(WT4), bWT4 + 16 * 8)
                e.wait_ge(ctx.sem(AC3), bAC3 + AC3_total)
                for c in range(4):
                    for jb in range(8):
                        g = c * 8 + jb
                        if g >= 2:
                            e.wait_ge(ctx.sem(EV4), bEV4 + g - 1)
                        bank = pa[g % 2]
                        for kb in range(8):
                            inst = e.matmul(bank[:, 0:512],
                                            wt[:, jb * 1024 + kb * 128:jb * 1024 + (kb + 1) * 128],
                                            AVT[:, kb * SEQ + c * 512:kb * SEQ + (c + 1) * 512],
                                            start=(kb == 0), stop=(kb == 7))
                        inst.then_inc(ctx.sem(PM4), 1)
            ctx.bump(PM4, 32)

            def ph4_vector(e):
                for c in range(4):
                    for jb in range(8):
                        g = c * 8 + jb
                        e.wait_ge(ctx.sem(PM4), bPM4 + g + 1)
                        if c >= 1:
                            e.wait_ge(ctx.sem("A4T0"), bTO4[0] + 16 * 2 * c)
                            e.wait_ge(ctx.sem("A4T1"), bTO4[1] + 16 * 2 * c)
                        e.tensor_copy(out=obf[:, :].rearrange("p (i c) -> p i c", c=1024)[:, :, jb * 128:(jb + 1) * 128],
                                      in_=pa[g % 2][:, 0:512]).then_inc(ctx.sem(EV4), 1)
                    for i in range(4):
                        ti = c * 4 + i
                        e.wait_ge(ctx.sem(f"A4T{ti % 2}"), bTO4[ti % 2] + 16 * (ti // 2 + 1))
                        e.wait_ge(ctx.sem(f"A4L{ti % 2}"), bLD4[ti % 2] + 16 * (ti // 2 + 1))
                        e.tensor_tensor(out=orow[ti % 2][:, :], in0=orow[ti % 2][:, :],
                                        in1=otk[ti % 2][:, :], op=ALU.add).then_inc(ctx.sem(VA4), 1)
            ctx.bump(EV4, 32)
            ctx.bump(VA4, 16)

            def ph4_gpsimd(e):
                for ti in range(16):
                    e.wait_ge(ctx.sem(VA4), bVA4 + ti + 1)
                    e.dma_start(out=resid[row0 + ti * 128:row0 + (ti + 1) * 128, :],
                                in_=orow[ti % 2][:, :]).then_inc(ctx.sem(f"A4S{ti % 2}"), 16)
                e.wait_ge(ctx.sem("A4S0"), bST4[0] + 16 * 8)
                e.wait_ge(ctx.sem("A4S1"), bST4[1] + 16 * 8)
            ctx.bump("A4S0", 16 * 8)
            ctx.bump("A4S1", 16 * 8)

            block.sync(ph4_sync)
            block.tensor(ph4_tensor)
            block.vector(ph4_vector)
            block.gpsimd(ph4_gpsimd)
        barrier(ctx, block)




def emit_transposeT(ctx, block, es, jobs, tag):
    maxr = max(j[0].shape[0] for j in jobs)
    buf = [sbuf(ctx, f"tt{i}{tag}", [128, maxr], BF16, es) for i in range(2)]
    bTL = [ctx.val("TTL0"), ctx.val("TTL1")]
    bTS = [ctx.val("TTS0"), ctx.val("TTS1")]
    blocks = []
    for (s, d) in jobs:
        R, C = s.shape
        for cb in range(C // 128):
            blocks.append((s, d, cb, R))
    n = len(blocks)

    def f_sync(e):
        for i, (s, d, cb, R) in enumerate(blocks):
            if i >= 2:
                e.wait_ge(ctx.sem(f"TTS{i % 2}"), bTS[i % 2] + 16 * ((i - 2) // 2 + 1))
            e.dma_start_transpose(out=buf[i % 2][:, 0:R],
                                  in_=s[:, cb * 128:(cb + 1) * 128]).then_inc(ctx.sem(f"TTL{i % 2}"), 16)
    ctx.bump("TTL0", 16 * ((n + 1) // 2))
    ctx.bump("TTL1", 16 * (n // 2))

    def f_gpsimd(e):
        for i, (s, d, cb, R) in enumerate(blocks):
            e.wait_ge(ctx.sem(f"TTL{i % 2}"), bTL[i % 2] + 16 * (i // 2 + 1))
            e.dma_start(out=d[cb * 128:(cb + 1) * 128, :], in_=buf[i % 2][:, 0:R]).then_inc(ctx.sem(f"TTS{i % 2}"), 16)
        e.wait_ge(ctx.sem("TTS0"), bTS[0] + 16 * ((n + 1) // 2))
        e.wait_ge(ctx.sem("TTS1"), bTS[1] + 16 * (n // 2))
    ctx.bump("TTS0", 16 * ((n + 1) // 2))
    ctx.bump("TTS1", 16 * (n // 2))

    block.sync(f_sync)
    block.gpsimd(f_gpsimd)


def emit_ffn_int8(ctx, block, P, tag):
    nc = ctx.nc
    resid = P["resid"].ap()
    wgT = P["wgT"].ap()
    wuT = P["wuT"].ap()
    wdT = P["wdT"].ap()
    pa = P["psum"]
    CH = 1024
    with ExitStack() as es0:
        sb = lambda n, s, d: sbuf(ctx, n + tag, s, d, es0)
        wdb = sb("fWD", [128, 24 * CH], BF16)
        lnxT2 = [sb(f"fL{i}", [128, 8 * CH], BF16) for i in range(2)]
        hsb = sb("fH", [128, 24 * CH], BF16)
        wgb = [sb(f"fG{i}", [128, 8 * 128], BF16) for i in range(2)]
        wub = [sb(f"fU{i}", [128, 8 * 128], BF16) for i in range(2)]
        silu_t = sb("fS", [128, CH], F32)
        orow = [sb(f"fO{i}", [128, D], F32) for i in range(2)]
        bDL = ctx.val("FDL")

        def fw_sync(e):
            for kb in range(24):
                e.dma_start(out=wdb[:, kb * CH:(kb + 1) * CH],
                            in_=wdT[kb * 128:(kb + 1) * 128, :]).then_inc(ctx.sem("FDL"), 16)
        ctx.bump("FDL", 16 * 24)
        block.sync(fw_sync)

        for ch in range(4):
            lnxT = lnxT2[ch % 2]
            row0 = ch * CH
            PM5, SI5, HM5, EV5 = "FPM", "FSI", "FHM", "FEV"
            bPM5 = ctx.val(PM5)
            if ch >= 1:
                ltv, lts = ctx.val("LTV"), ctx.val("LTS")

                def pre_sync(e, ltv=ltv, lts=lts, bPM5=bPM5, ch=ch):
                    e.wait_ge(ctx.sem("LTV"), ltv)
                    e.wait_ge(ctx.sem("LTS"), lts)
                    if ch >= 2:
                        e.wait_ge(ctx.sem(PM5), bPM5 - 128)
                block.sync(pre_sync)
            tp0, tp1, _ = emit_lnT(ctx, block, f"fl{tag}c{ch}", resid, row0, CH // 128,
                                   lnxT, lambda i, db: db * CH + i * 128)
            bSI5, bHM5, bEV5 = ctx.val(SI5), ctx.val(HM5), ctx.val(EV5)
            bWL = [ctx.val("FWL0"), ctx.val("FWL1")]
            bSO = [ctx.val("FSO0"), ctx.val("FSO1")]
            bDW = [ctx.val("FDW0"), ctx.val("FDW1")]

            def f5_sync(e, ch=ch, bPM5=bPM5, bWL=bWL):
                for jb in range(24):
                    if jb >= 2:
                        e.wait_ge(ctx.sem(PM5), bPM5 + 4 * (jb - 1))
                    elif ch >= 1:
                        e.wait_ge(ctx.sem(PM5), bPM5 - 20 + 4 * jb)
                    for (wsrc, wb) in ((wgT, wgb), (wuT, wub)):
                        e.dma_start(out=wb[jb % 2][:, :].rearrange("p (k c) -> p k c", k=8),
                                    in_=bass.AP(wsrc.tensor, jb * 128,
                                                [[3072, 128], [128 * 3072, 8], [1, 128]])).then_inc(ctx.sem(f"FWL{jb % 2}"), 16)
            ctx.bump("FWL0", 16 * 24)
            ctx.bump("FWL1", 16 * 24)

            def f5_tensor(e, ch=ch, lnxT=lnxT, tp0=tp0, tp1=tp1, bPM5=bPM5, bSI5=bSI5,
                          bHM5=bHM5, bEV5=bEV5, bWL=bWL):
                e.wait_ge(ctx.sem("LTT0"), tp0)
                e.wait_ge(ctx.sem("LTT1"), tp1)
                for jb in range(24):
                    e.wait_ge(ctx.sem(f"FWL{jb % 2}"), bWL[jb % 2] + 16 * 2 * (jb // 2 + 1))
                    if jb >= 1:
                        e.wait_ge(ctx.sem(SI5), bSI5 + 2 * (jb - 1) + 2)
                    elif ch >= 1:
                        e.wait_ge(ctx.sem(SI5), bSI5)
                    for half in range(2):
                        for kb in range(8):
                            inst = e.matmul(pa[half][:, 0:512],
                                            wgb[jb % 2][:, kb * 128:(kb + 1) * 128],
                                            tmslice(lnxT, kb, half * 512, 512),
                                            start=(kb == 0), stop=(kb == 7))
                        inst.then_inc(ctx.sem(PM5), 1)
                    if jb >= 1:
                        e.wait_ge(ctx.sem(HM5), bHM5 + 2 * (jb - 1) + 2)
                    elif ch >= 1:
                        e.wait_ge(ctx.sem(HM5), bHM5)
                    for half in range(2):
                        for kb in range(8):
                            inst = e.matmul(pa[2 + half][:, 0:512],
                                            wub[jb % 2][:, kb * 128:(kb + 1) * 128],
                                            tmslice(lnxT, kb, half * 512, 512),
                                            start=(kb == 0), stop=(kb == 7))
                        inst.then_inc(ctx.sem(PM5), 1)
                e.wait_ge(ctx.sem("FDL"), bDL + 16 * 24)
                for i in range(8):
                    for half in range(2):
                        g = 2 * i + half
                        if g >= 2:
                            e.wait_ge(ctx.sem(EV5), bEV5 + g - 1)
                        elif ch >= 1:
                            e.wait_ge(ctx.sem(EV5), bEV5)
                        e.wait_ge(ctx.sem(HM5), bHM5 + 48)
                        for kb in range(24):
                            inst = e.matmul(pa[4 + g % 2][:, 0:512],
                                            hsb[:, kb * CH + i * 128:kb * CH + (i + 1) * 128],
                                            wdb[:, kb * CH + half * 512:kb * CH + (half + 1) * 512],
                                            start=(kb == 0), stop=(kb == 23))
                        inst.then_inc(ctx.sem(PM5), 1)
            ctx.bump(PM5, 96 + 16)

            def f5_scalar(e, ch=ch, bPM5=bPM5, bHM5=bHM5):
                for jb in range(24):
                    for half in range(2):
                        e.wait_ge(ctx.sem(PM5), bPM5 + 4 * jb + half + 1)
                        if jb >= 1:
                            e.wait_ge(ctx.sem(HM5), bHM5 + 2 * (jb - 1) + half + 1)
                        elif ch >= 1:
                            e.wait_ge(ctx.sem(HM5), bHM5)
                        e.activation(out=silu_t[:, half * 512:(half + 1) * 512],
                                     in_=pa[half][:, 0:512], func=AF.Silu,
                                     scale=1.0 / 127.0).then_inc(ctx.sem(SI5), 1)
            ctx.bump(SI5, 48)

            def f5_vector(e, ch=ch, bPM5=bPM5, bSI5=bSI5, bEV5=bEV5, bDW=bDW):
                for jb in range(24):
                    for half in range(2):
                        e.wait_ge(ctx.sem(SI5), bSI5 + 2 * jb + half + 1)
                        e.wait_ge(ctx.sem(PM5), bPM5 + 4 * jb + 2 + half + 1)
                        if jb == 0 and half == 0 and ch >= 1:
                            e.wait_ge(ctx.sem(PM5), bPM5)
                        e.scalar_tensor_tensor(out=hsb[:, jb * CH + half * 512:jb * CH + (half + 1) * 512],
                                               in0=pa[2 + half][:, 0:512], scalar=1.0 / 127.0,
                                               in1=silu_t[:, half * 512:(half + 1) * 512],
                                               op0=ALU.mult, op1=ALU.mult).then_inc(ctx.sem(HM5), 1)
                for i in range(8):
                    for half in range(2):
                        g = 2 * i + half
                        e.wait_ge(ctx.sem(PM5), bPM5 + 96 + g + 1)
                        e.wait_ge(ctx.sem(f"FDW{i % 2}"), bDW[i % 2] + 16 * (i // 2 + 1))
                        e.scalar_tensor_tensor(out=orow[i % 2][:, half * 512:(half + 1) * 512],
                                               in0=pa[4 + g % 2][:, 0:512], scalar=0.5 / 127.0,
                                               in1=orow[i % 2][:, half * 512:(half + 1) * 512],
                                               op0=ALU.mult, op1=ALU.add).then_inc(ctx.sem(EV5), 1)
            ctx.bump(HM5, 48)
            ctx.bump(EV5, 16)

            def f5_scalarq(e, ch=ch, row0=row0, bSO=bSO):
                for i in range(8):
                    if i >= 2:
                        e.wait_ge(ctx.sem(f"FSO{i % 2}"), bSO[i % 2] + 16 * ((i - 2) // 2 + 1))
                    elif ch >= 1:
                        e.wait_ge(ctx.sem(f"FSO{i % 2}"), bSO[i % 2])
                    e.dma_start(out=orow[i % 2][:, :],
                                in_=resid[row0 + i * 128:row0 + (i + 1) * 128, :]).then_inc(ctx.sem(f"FDW{i % 2}"), 16)
            ctx.bump("FDW0", 16 * 4)
            ctx.bump("FDW1", 16 * 4)

            def f5_gpsimd(e, row0=row0, bEV5=bEV5, bSO=bSO):
                for i in range(8):
                    e.wait_ge(ctx.sem(EV5), bEV5 + 2 * i + 2)
                    e.dma_start(out=resid[row0 + i * 128:row0 + (i + 1) * 128, :],
                                in_=orow[i % 2][:, :]).then_inc(ctx.sem(f"FSO{i % 2}"), 16)
                e.wait_ge(ctx.sem("FSO0"), bSO[0] + 16 * 4)
                e.wait_ge(ctx.sem("FSO1"), bSO[1] + 16 * 4)
            ctx.bump("FSO0", 16 * 4)
            ctx.bump("FSO1", 16 * 4)

            block.sync(f5_sync)
            block.tensor(f5_tensor)
            block.scalar(f5_scalar)
            block.scalar(f5_scalarq)
            block.vector(f5_vector)
            block.gpsimd(f5_gpsimd)
        barrier(ctx, block)




def emit_serial(ctx, block, ops, sem="SER"):
    def semfor(eng, amt):
        if amt == 1:
            return "SER"
        return "SERD" if eng == "gpsimd" else "SERH"
    cums = {"SER": ctx.val("SER"), "SERH": ctx.val("SERH"), "SERD": ctx.val("SERD")}
    pre = []
    for (eng, _, amt) in ops:
        pre.append(dict(cums))
        cums[semfor(eng, amt)] += amt
    i = 0
    while i < len(ops):
        j = i
        while j < len(ops) and ops[j][0] == ops[i][0]:
            j += 1
        eng, lo, hi = ops[i][0], i, j

        def make(lo=lo, hi=hi):
            def f(e):
                for k in range(lo, hi):
                    for s in ("SER", "SERH", "SERD"):
                        e.wait_ge(ctx.sem(s), pre[k][s])
                    inst = ops[k][1](e)
                    inst.then_inc(ctx.sem(semfor(ops[k][0], ops[k][2])), ops[k][2])
            return f
        getattr(block, eng)(make())
        i = j

    def flush(e):
        for s in ("SER", "SERH", "SERD"):
            e.wait_ge(ctx.sem(s), cums[s])
    block.gpsimd(flush)
    for s in ("SER", "SERH", "SERD"):
        ctx.cnt[s] = cums[s]


def emit_lru(ctx, block, P, tag):
    nc = ctx.nc
    resid = P["resid"].ap()
    zln = P["xln"].ap()
    nu = P["l1_nu"].ap()
    th = P["l1_th"].ap()
    bTre = P["bTre"].ap()
    bTim = P["bTim"].ap()
    creT = P["creT"].ap()
    cimnT = P["cimnT"].ap()
    dT = P["dT"].ap()
    w1l1 = P["w1l1_bf"].ap()
    w2l1 = P["w2l1_bf"].ap()
    rlg = P["l1_res"].ap()
    rsc = P["lru_scratch"].ap()
    TRI = P["tri_sb"]
    ONESCOL = P["onescol_sb"]
    ONES128B = P["ones128b_sb"]
    ONES128F = P["ones128f_sb"]
    IDENT = P["ident_sb"]
    pa = P["psum"]
    PI = math.pi
    S = []

    def op(eng, amt=1):
        def deco(fn):
            S.append((eng, fn, amt))
        return deco

    with ExitStack() as es0:
        sb0 = lambda n, s, d: sbuf(ctx, n + tag, s, d, es0)
        lam78 = sb0("qL8", [1, 4 * DST], F32)
        bnT = sb0("qBT", [128, 16 * DST], BF16)
        sgv = sb0("qSG", [128, 1], F32)
        IV2 = sb0("qV2a", [128, 2 * DST], F32)
        IV2S = sb0("qV2b", [128, 2 * DST], F32)
        A2 = sb0("qA2a", [128, 2 * DST], F32)
        A2S = sb0("qA2b", [128, 2 * DST], F32)
        A12 = sb0("qA2c", [128, 2 * DST], F32)
        A12S = sb0("qA2d", [128, 2 * DST], F32)
        lamS = sb0("qLS", [1, 4 * DST], F32)
        est = es0.enter_context(ExitStack())
        sbt = lambda n, s, d: sbuf(ctx, n + tag, s, d, est)
        Are = sbt("qAe", [128, DST], F32)
        Aim = sbt("qAi", [128, DST], F32)
        A1re = sbt("qBe", [128, DST], F32)
        A1im = sbt("qBi", [128, DST], F32)
        Ivre = sbt("qIe", [128, DST], F32)
        Ivim = sbt("qIi", [128, DST], F32)
        btl = sbt("qBL", [128, 16 * DST], BF16)
        row1 = sbt("qR1", [1, DST], F32)
        row2 = sbt("qR2", [1, DST], F32)
        row3 = sbt("qR3", [1, 1], F32)
        enu_b = sbt("qEN", [128, DST], F32)
        eth_b = sbt("qET", [128, DST], F32)
        texp = sbt("qTX", [128, DST], F32)
        ta = sbt("qTA", [128, DST], F32)
        w1a = sbt("qW1", [128, DST], F32)
        w2a = sbt("qW2", [128, DST], F32)
        w3a = sbt("qW3", [128, DST], F32)
        scq = sbt("qSQ", [128, DST], I32)
        iot0 = sbt("qI0", [128, 1], F32)

        def rr(dst, srcv):
            op("vector")(lambda e, s=srcv: e.tensor_scalar(out=w3a[:, :], in0=s[:, :], scalar1=1.0 / (2 * PI), scalar2=None, op0=ALU.mult))
            op("vector")(lambda e: e.tensor_copy(out=scq[:, :], in_=w3a[:, :]))
            op("vector")(lambda e: e.tensor_copy(out=w3a[:, :], in_=scq[:, :]))
            op("vector")(lambda e, d=dst, s=srcv: e.scalar_tensor_tensor(out=d[:, :], in0=w3a[:, :], scalar=-2 * PI, in1=s[:, :], op0=ALU.mult, op1=ALU.add))
            op("vector")(lambda e, d=dst: e.tensor_scalar(out=w3a[:, :], in0=d[:, :], scalar1=PI, scalar2=None, op0=ALU.is_gt))
            op("vector")(lambda e, d=dst: e.scalar_tensor_tensor(out=d[:, :], in0=w3a[:, :], scalar=-2 * PI, in1=d[:, :], op0=ALU.mult, op1=ALU.add))
            op("vector")(lambda e, d=dst: e.tensor_scalar(out=w3a[:, :], in0=d[:, :], scalar1=-PI, scalar2=None, op0=ALU.is_lt))
            op("vector")(lambda e, d=dst: e.scalar_tensor_tensor(out=d[:, :], in0=w3a[:, :], scalar=2 * PI, in1=d[:, :], op0=ALU.mult, op1=ALU.add))

        op("sync", 16)(lambda e: e.dma_start(out=row1[:, :], in_=nu[:, :]))
        op("sync", 16)(lambda e: e.dma_start(out=row2[:, :], in_=th[:, :]))
        op("sync", 16)(lambda e: e.dma_start(out=row3[:, :], in_=rlg[:, :]))
        for pl, bsrc in enumerate((bTre, bTim)):
            for kb in range(8):
                op("sync", 16)(lambda e, pl=pl, kb=kb, bsrc=bsrc: e.dma_start(
                    out=btl[:, (pl * 8 + kb) * DST:(pl * 8 + kb + 1) * DST],
                    in_=bsrc[kb * 128:(kb + 1) * 128, :]))
        op("gpsimd")(lambda e: e.iota(iot0[:, :], [[0, 1]], base=0, channel_multiplier=1,
                                      allow_small_or_imprecise_dtypes=True))
        op("scalar")(lambda e: e.activation(out=row1[:, :], in_=row1[:, :], func=AF.Exp))
        op("scalar")(lambda e: e.activation(out=row2[:, :], in_=row2[:, :], func=AF.Exp))
        op("tensor")(lambda e: e.matmul(pa[0][:, 0:DST], ONES128F[:, :], row1[:, :], start=True, stop=True))
        op("tensor")(lambda e: e.matmul(pa[1][:, 0:DST], ONES128F[:, :], row2[:, :], start=True, stop=True))
        op("vector")(lambda e: e.tensor_copy(out=enu_b[:, :], in_=pa[0][:, 0:DST]))
        op("vector")(lambda e: e.tensor_copy(out=eth_b[:, :], in_=pa[1][:, 0:DST]))
        op("vector")(lambda e: e.tensor_scalar(out=texp[:, :], in0=enu_b[:, :], scalar1=iot0[:, :], scalar2=None, op0=ALU.mult))
        op("vector")(lambda e: e.tensor_scalar(out=ta[:, :], in0=eth_b[:, :], scalar1=iot0[:, :], scalar2=None, op0=ALU.mult))
        rr(w1a, ta)
        op("scalar")(lambda e: e.activation(out=w1a[:, :], in_=w1a[:, :], func=AF.Sin))
        op("vector")(lambda e: e.tensor_scalar(out=w2a[:, :], in0=ta[:, :], scalar1=PI / 2, scalar2=None, op0=ALU.add))
        rr(w2a, w2a)
        op("scalar")(lambda e: e.activation(out=w2a[:, :], in_=w2a[:, :], func=AF.Sin))
        op("scalar")(lambda e: e.activation(out=Ivre[:, :], in_=texp[:, :], func=AF.Exp))
        op("vector")(lambda e: e.scalar_tensor_tensor(out=Ivim[:, :], in0=Ivre[:, :], scalar=-1.0, in1=w1a[:, :], op0=ALU.mult, op1=ALU.mult))
        op("vector")(lambda e: e.tensor_tensor(out=Ivre[:, :], in0=Ivre[:, :], in1=w2a[:, :], op=ALU.mult))
        op("scalar")(lambda e: e.activation(out=Are[:, :], in_=texp[:, :], func=AF.Exp, scale=-1.0))
        op("vector")(lambda e: e.tensor_tensor(out=Aim[:, :], in0=Are[:, :], in1=w1a[:, :], op=ALU.mult))
        op("vector")(lambda e: e.tensor_tensor(out=Are[:, :], in0=Are[:, :], in1=w2a[:, :], op=ALU.mult))
        op("vector")(lambda e: e.tensor_tensor(out=texp[:, :], in0=texp[:, :], in1=enu_b[:, :], op=ALU.add))
        op("vector")(lambda e: e.tensor_tensor(out=ta[:, :], in0=ta[:, :], in1=eth_b[:, :], op=ALU.add))
        rr(w1a, ta)
        op("scalar")(lambda e: e.activation(out=w1a[:, :], in_=w1a[:, :], func=AF.Sin))
        op("vector")(lambda e: e.tensor_scalar(out=w2a[:, :], in0=ta[:, :], scalar1=PI / 2, scalar2=None, op0=ALU.add))
        rr(w2a, w2a)
        op("scalar")(lambda e: e.activation(out=w2a[:, :], in_=w2a[:, :], func=AF.Sin))
        op("scalar")(lambda e: e.activation(out=A1re[:, :], in_=texp[:, :], func=AF.Exp, scale=-1.0))
        op("vector")(lambda e: e.tensor_tensor(out=A1im[:, :], in0=A1re[:, :], in1=w1a[:, :], op=ALU.mult))
        op("vector")(lambda e: e.tensor_tensor(out=A1re[:, :], in0=A1re[:, :], in1=w2a[:, :], op=ALU.mult))
        op("scalar")(lambda e: e.activation(out=w1a[:, :], in_=enu_b[:, :], func=AF.Exp, scale=-2.0))
        op("vector")(lambda e: e.tensor_scalar(out=w1a[:, :], in0=w1a[:, :], scalar1=-1.0, scalar2=1.0, op0=ALU.mult, op1=ALU.add))
        op("scalar")(lambda e: e.activation(out=w1a[:, :], in_=w1a[:, :], func=AF.Ln))
        op("scalar")(lambda e: e.activation(out=w1a[:, :], in_=w1a[:, :], func=AF.Exp, scale=0.5))
        for s_ in range(16):
            op("vector")(lambda e, s_=s_: e.tensor_tensor(out=bnT[:, s_ * DST:(s_ + 1) * DST],
                                                          in0=btl[:, s_ * DST:(s_ + 1) * DST], in1=w1a[:, :], op=ALU.mult))
        op("scalar")(lambda e: e.activation(out=row3[:, :], in_=row3[:, :], func=AF.Sigmoid))
        op("tensor")(lambda e: e.matmul(pa[0][:, 0:1], ONES128F[:, :], row3[:, :], start=True, stop=True))
        op("vector")(lambda e: e.tensor_copy(out=sgv[:, :], in_=pa[0][:, 0:1]))
        op("gpsimd", 16)(lambda e: e.dma_start(out=rsc[0:1, 0:DST], in_=Are[127:128, :]))
        op("gpsimd", 16)(lambda e: e.dma_start(out=rsc[0:1, DST:2 * DST], in_=Aim[127:128, :]))
        op("gpsimd", 16)(lambda e: e.dma_start(out=rsc[1:2, 0:DST], in_=A1re[127:128, :]))
        op("gpsimd", 16)(lambda e: e.dma_start(out=rsc[1:2, DST:2 * DST], in_=A1im[127:128, :]))
        op("gpsimd", 16)(lambda e: e.dma_start(out=lam78[:, :], in_=rsc[0:2, :].rearrange("a b -> (a b)")[None, :]))
        for (dstt, lo, hi) in ((IV2, Ivre, Ivim), (IV2S, Ivim, Ivre),
                               (A2, Are, Aim), (A2S, Aim, Are),
                               (A12, A1re, A1im), (A12S, A1im, A1re)):
            op("vector")(lambda e, d=dstt, s=lo: e.tensor_copy(out=d[:, 0:DST], in_=s[:, :]))
            op("vector")(lambda e, d=dstt, s=hi: e.tensor_copy(out=d[:, DST:2 * DST], in_=s[:, :]))
        for (do_, so_) in ((0, DST), (DST, 0), (2 * DST, 3 * DST), (3 * DST, 2 * DST)):
            op("vector")(lambda e, do_=do_, so_=so_: e.tensor_copy(
                out=lamS[:, do_:do_ + DST], in_=lam78[:, so_:so_ + DST]))
        emit_serial(ctx, block, S)
        S = []
        barrier(ctx, block)
        est.close()

        serF = ctx.val("SER")
        serhF = ctx.val("SERH")
        serdF = ctx.val("SERD")
        for b in range(NB):
            btag = f"{tag}b{b}"
            row0 = b * SEQ
            with ExitStack() as esb:
                sbb = lambda n, s, d: sbuf(ctx, n + btag, s, d, esb)
                yT = sbb("qY", [128, 8 * SEQ], BF16)
                carry_f = [sbb(f"qCF{i}", [1, 2 * DST], F32) for i in range(2)]
                carry_bf = sbb("qCB", [1, 2 * DST], BF16)
                with ExitStack() as esc:
                    sbc = lambda n, s, d: sbuf(ctx, n + btag, s, d, esc)
                    lnxT = sbc("qX", [128, 8 * SEQ], BF16)
                    stT = sbc("qST", [128, 4 * SEQ], BF16)
                    bu = sbc("qBU", [128, 2 * 16 * DST], BF16)
                    scbf = sbc("qSB", [128, 2 * DST], BF16)
                    stbf = sbc("qSS", [128, 2 * DST], BF16)
                    t1 = sbc("qM1", [128, 2 * DST], F32)
                    t2 = sbc("qM2", [128, 2 * DST], F32)
                    r1 = sbc("qC1", [1, 2 * DST], F32)
                    r2 = sbc("qC2", [1, 2 * DST], F32)
                    cre_sb = sbc("qCR", [128, 2 * 1024], BF16)
                    cim_sb = sbc("qCI", [128, 2 * 1024], BF16)
                    dT_sb = sbc("qDT", [128, 8 * 1024], BF16)
                    bLT = [ctx.val("LTT0"), ctx.val("LTT1")]
                    emit_lnT(ctx, block, "ql" + btag, resid, row0, SEQ // 128,
                             lnxT, lambda i, db: db * SEQ + i * 128,
                             lnrow_extra=(zln, row0))
                    xs0, xs1 = ctx.val("LTX0"), ctx.val("LTX1")
                    bBUM, bBUV = ctx.val("M31"), ctx.val("Y3V")
                    bSCM, bSCV, bSCA = ctx.val("M3Q"), ctx.val("V3F"), ctx.val("S3S")
                    bYL, bYM, bYV = ctx.val("A4W"), ctx.val("M3T"), ctx.val("V3C")

                    def bu_tensor(e):
                        e.wait_ge(ctx.sem("SER"), serF)
                        e.wait_ge(ctx.sem("SERH"), serhF)
                        e.wait_ge(ctx.sem("SERD"), serdF)
                        for i in range(16):
                            e.wait_ge(ctx.sem(f"LTT{i % 2}"), bLT[i % 2] + 16 * (i // 2 + 1))
                            for pl in range(2):
                                g = i * 2 + pl
                                if g >= 2:
                                    e.wait_ge(ctx.sem("Y3V"), bBUV + g - 1)
                                for kb in range(8):
                                    inst = e.matmul(pa[g % 2][:, 0:DST],
                                                    lnxT[:, i * D + kb * 128:i * D + (kb + 1) * 128],
                                                    bnT[:, (pl * 8 + kb) * DST:(pl * 8 + kb + 1) * DST],
                                                    start=(kb == 0), stop=(kb == 7))
                                inst.then_inc(ctx.sem("M31"), 1)
                    ctx.bump("M31", 32)

                    def bu_vector(e):
                        for g in range(32):
                            e.wait_ge(ctx.sem("M31"), bBUM + g + 1)
                            e.tensor_copy(out=bu[:, g * DST:(g + 1) * DST],
                                          in_=pa[g % 2][:, 0:DST]).then_inc(ctx.sem("Y3V"), 1)
                    ctx.bump("Y3V", 32)

                    def sc_vector(e):
                        k = [bSCV]

                        def step(inst):
                            inst.then_inc(ctx.sem("V3F"), 1)
                            k[0] += 1
                            e.wait_ge(ctx.sem("V3F"), k[0])
                        step(e.memset(carry_f[0][:, :], 0.0))
                        step(e.memset(carry_bf[:, :], 0.0))
                        for c in range(16):
                            cur, nxt = c % 2, (c + 1) % 2
                            bu2 = bu[:, 2 * c * DST:(2 * c + 2) * DST]
                            e.wait_ge(ctx.sem("Y3V"), bBUV + 2 * c + 2)
                            if c >= 1:
                                e.wait_ge(ctx.sem("M3Q"), bSCM + 7 * (c - 1) + 3)
                            step(e.tensor_tensor(out=t1[:, :], in0=IV2[:, :], in1=bu2, op=ALU.mult))
                            step(e.tensor_tensor(out=scbf[:, 0:DST], in0=t1[:, 0:DST], in1=t1[:, DST:2 * DST], op=ALU.subtract))
                            step(e.tensor_tensor(out=t2[:, :], in0=IV2S[:, :], in1=bu2, op=ALU.mult))
                            step(e.tensor_tensor(out=scbf[:, DST:2 * DST], in0=t2[:, 0:DST], in1=t2[:, DST:2 * DST], op=ALU.add))
                            e.wait_ge(ctx.sem("M3Q"), bSCM + 7 * c + 2)
                            step(e.tensor_tensor(out=t1[:, :], in0=A2[:, :], in1=pa[2][:, 0:2 * DST], op=ALU.mult))
                            step(e.tensor_tensor(out=t2[:, :], in0=A12[:, :], in1=pa[4][:, 0:2 * DST], op=ALU.mult))
                            step(e.tensor_tensor(out=t1[:, :], in0=t1[:, :], in1=t2[:, :], op=ALU.add))
                            step(e.tensor_tensor(out=stbf[:, 0:DST], in0=t1[:, 0:DST], in1=t1[:, DST:2 * DST], op=ALU.subtract))
                            step(e.tensor_tensor(out=t1[:, :], in0=A2S[:, :], in1=pa[2][:, 0:2 * DST], op=ALU.mult))
                            step(e.tensor_tensor(out=t2[:, :], in0=A12S[:, :], in1=pa[4][:, 0:2 * DST], op=ALU.mult))
                            step(e.tensor_tensor(out=t1[:, :], in0=t1[:, :], in1=t2[:, :], op=ALU.add))
                            step(e.tensor_tensor(out=stbf[:, DST:2 * DST], in0=t1[:, 0:DST], in1=t1[:, DST:2 * DST], op=ALU.add))
                            e.wait_ge(ctx.sem("M3Q"), bSCM + 7 * c + 3)
                            step(e.tensor_tensor(out=r1[:, :], in0=lam78[:, 0:2 * DST], in1=pa[3][0:1, 0:2 * DST], op=ALU.mult))
                            step(e.tensor_tensor(out=r2[:, :], in0=lam78[:, 2 * DST:4 * DST], in1=carry_f[cur][:, :], op=ALU.mult))
                            step(e.tensor_tensor(out=r1[:, :], in0=r1[:, :], in1=r2[:, :], op=ALU.add))
                            step(e.tensor_tensor(out=carry_f[nxt][:, 0:DST], in0=r1[:, 0:DST], in1=r1[:, DST:2 * DST], op=ALU.subtract))
                            step(e.tensor_tensor(out=r1[:, :], in0=lamS[:, 0:2 * DST], in1=pa[3][0:1, 0:2 * DST], op=ALU.mult))
                            step(e.tensor_tensor(out=r2[:, :], in0=lamS[:, 2 * DST:4 * DST], in1=carry_f[cur][:, :], op=ALU.mult))
                            step(e.tensor_tensor(out=r1[:, :], in0=r1[:, :], in1=r2[:, :], op=ALU.add))
                            step(e.tensor_tensor(out=carry_f[nxt][:, DST:2 * DST], in0=r1[:, 0:DST], in1=r1[:, DST:2 * DST], op=ALU.add))
                            step(e.tensor_copy(out=carry_bf[:, :], in_=carry_f[nxt][:, :]))
                    ctx.bump("V3F", 2 + 21 * 16)

                    def sc_tensor(e):
                        for c in range(16):
                            e.wait_ge(ctx.sem("V3F"), bSCV + 2 + 21 * c)
                            e.matmul(pa[4][:, 0:2 * DST], ONES128B[:, :], carry_bf[:, :],
                                     start=True, stop=True).then_inc(ctx.sem("M3Q"), 1)
                            e.wait_ge(ctx.sem("V3F"), bSCV + 2 + 21 * c + 4)
                            e.matmul(pa[2][:, 0:2 * DST], TRI[:, :], scbf[:, :],
                                     start=True, stop=True).then_inc(ctx.sem("M3Q"), 1)
                            e.matmul(pa[3][0:1, 0:2 * DST], ONESCOL[:, :], scbf[:, :],
                                     start=True, stop=True).then_inc(ctx.sem("M3Q"), 1)
                            e.wait_ge(ctx.sem("V3F"), bSCV + 2 + 21 * c + 12)
                            for t in range(4):
                                if 4 * c + t - 1 > 0:
                                    e.wait_ge(ctx.sem("S3S"), bSCA + 4 * c + t - 1)
                                e.transpose(pa[5 + t % 2][:, :].bitcast(BF16)[:, 0:128],
                                            stbf[:, t * 128:(t + 1) * 128],
                                            IDENT[:, :]).then_inc(ctx.sem("M3Q"), 1)
                    ctx.bump("M3Q", 7 * 16)

                    def sc_scalar(e):
                        for c in range(16):
                            for t in range(4):
                                e.wait_ge(ctx.sem("M3Q"), bSCM + 7 * c + 3 + t + 1)
                                e.activation(out=stT[:, t * SEQ + c * 128:t * SEQ + (c + 1) * 128],
                                             in_=pa[5 + t % 2][:, :].bitcast(BF16)[:, 0:128],
                                             func=AF.Copy).then_inc(ctx.sem("S3S"), 1)
                    ctx.bump("S3S", 4 * 16)

                    def y_sync(e):
                        e.dma_start(out=cre_sb[:, 0:1024], in_=creT[0:128, :]).then_inc(ctx.sem("A4W"), 16)
                        e.dma_start(out=cre_sb[:, 1024:2048], in_=creT[128:256, :]).then_inc(ctx.sem("A4W"), 16)
                        e.dma_start(out=cim_sb[:, 0:1024], in_=cimnT[0:128, :]).then_inc(ctx.sem("A4W"), 16)
                        e.dma_start(out=cim_sb[:, 1024:2048], in_=cimnT[128:256, :]).then_inc(ctx.sem("A4W"), 16)
                        for kb in range(8):
                            e.dma_start(out=dT_sb[:, kb * 1024:(kb + 1) * 1024],
                                        in_=dT[kb * 128:(kb + 1) * 128, :]).then_inc(ctx.sem("A4W"), 16)
                    ctx.bump("A4W", 16 * 12)

                    def y_tensor(e):
                        e.wait_ge(ctx.sem("A4W"), bYL + 16 * 12)
                        e.wait_ge(ctx.sem("S3S"), bSCA + 64)
                        for mb in range(8):
                            for tc in range(4):
                                g = mb * 4 + tc
                                if g >= 2:
                                    e.wait_ge(ctx.sem("V3C"), bYV + g - 1)
                                bank = pa[6 + g % 2]
                                first = True
                                for nb in range(2):
                                    e.matmul(bank[:, 0:512], cre_sb[:, nb * 1024 + mb * 128:nb * 1024 + (mb + 1) * 128],
                                             stT[:, nb * SEQ + tc * 512:nb * SEQ + (tc + 1) * 512], start=first, stop=False)
                                    first = False
                                    e.matmul(bank[:, 0:512], cim_sb[:, nb * 1024 + mb * 128:nb * 1024 + (mb + 1) * 128],
                                             stT[:, (2 + nb) * SEQ + tc * 512:(2 + nb) * SEQ + (tc + 1) * 512], start=False, stop=False)
                                for kb in range(8):
                                    inst = e.matmul(bank[:, 0:512], dT_sb[:, kb * 1024 + mb * 128:kb * 1024 + (mb + 1) * 128],
                                                    tmslice(lnxT, kb, tc * 512, 512),
                                                    start=False, stop=(kb == 7))
                                inst.then_inc(ctx.sem("M3T"), 1)
                    ctx.bump("M3T", 32)

                    def y_vector(e):
                        for g in range(32):
                            mb, tc = g // 4, g % 4
                            e.wait_ge(ctx.sem("M3T"), bYM + g + 1)
                            e.tensor_copy(out=yT[:, mb * SEQ + tc * 512:mb * SEQ + (tc + 1) * 512],
                                          in_=pa[6 + g % 2][:, 0:512]).then_inc(ctx.sem("V3C"), 1)
                    ctx.bump("V3C", 32)

                    block.tensor(bu_tensor)
                    block.vector(bu_vector)
                    block.vector(sc_vector)
                    block.tensor(sc_tensor)
                    block.scalar(sc_scalar)
                    block.sync(y_sync)
                    block.tensor(y_tensor)
                    block.vector(y_vector)
                with ExitStack() as esz:
                    sbz = lambda n, s, d: sbuf(ctx, n + btag, s, d, esz)
                    hgl = sbz("qH", [128, 32 * 512], BF16)
                    w1b = [sbz(f"qV1{i}", [128, 8 * 128], BF16) for i in range(2)]
                    w2b = [sbz(f"qV2{i}", [128, 32 * 128], BF16) for i in range(2)]
                    zbf = sbz("qZ", [128, 8 * 512], BF16)
                    ztok = [sbz(f"qZT{i}", [128, D], BF16) for i in range(2)]
                    zlr = [sbz(f"qZL{i}", [128, D], BF16) for i in range(2)]
                    orow = [sbz(f"qO{i}", [128, D], F32) for i in range(2)]
                    bZM, bZA, bZV = ctx.val("A3P"), ctx.val("A3A"), ctx.val("A3D")
                    bZW = [ctx.val("A4T0"), ctx.val("A4T1")]
                    bZU = [ctx.val("A4L0"), ctx.val("A4L1")]
                    bZT = [ctx.val("A3X0"), ctx.val("A3X1")]
                    bZL = [ctx.val("CVL0"), ctx.val("CVL1")]
                    bZS = [ctx.val("FSO0"), ctx.val("FSO1")]
                    mmh_inc = lambda gi: 40 * (gi // 32) + (gi % 32) + 1
                    mmz_inc = lambda gj: 40 * (gj // 8) + 32 + (gj % 8) + 1
                    evac_inc = lambda tcq, mbq: 16 * tcq + mbq + 1
                    add2_inc = lambda ti: 16 * (ti // 4) + 8 + 2 * (ti % 4) + 2

                    def z_sync(e):
                        e.wait_ge(ctx.sem("M3T"), bYM + 32)
                        e.wait_ge(ctx.sem("LTX0"), xs0)
                        e.wait_ge(ctx.sem("LTX1"), xs1)
                        for tc in range(4):
                            for hb in range(32):
                                gi = tc * 32 + hb
                                if gi >= 2:
                                    e.wait_ge(ctx.sem("A3P"), bZM + mmh_inc(gi - 2))
                                e.dma_start(out=w1b[gi % 2][:, :].rearrange("p (k c) -> p k c", k=8),
                                            in_=bass.AP(w1l1.tensor, hb * 128,
                                                        [[4096, 128], [128 * 4096, 8], [1, 128]])).then_inc(ctx.sem(f"A4T{gi % 2}"), 16)
                            for mb in range(8):
                                gj = tc * 8 + mb
                                if gj >= 2:
                                    e.wait_ge(ctx.sem("A3P"), bZM + mmz_inc(gj - 2))
                                e.dma_start(out=w2b[gj % 2][:, :].rearrange("p (k c) -> p k c", k=32),
                                            in_=bass.AP(w2l1.tensor, mb * 128,
                                                        [[1024, 128], [128 * 1024, 32], [1, 128]])).then_inc(ctx.sem(f"A4L{gj % 2}"), 16)
                            for i in range(4):
                                ti = tc * 4 + i
                                e.wait_ge(ctx.sem("A3D"), bZV + 16 * tc + 8)
                                if ti >= 2:
                                    e.wait_ge(ctx.sem("A3D"), bZV + add2_inc(ti - 2))
                                e.dma_start_transpose(out=b3(ztok[ti % 2][:, :]),
                                                      in_=b3(zbf[:, i * 1024:(i + 1) * 1024])).then_inc(ctx.sem(f"A3X{ti % 2}"), 16)
                                if ti >= 2:
                                    e.wait_ge(ctx.sem(f"FSO{ti % 2}"), bZS[ti % 2] + 16 * ((ti - 2) // 2 + 1))
                                e.dma_start(out=orow[ti % 2][:, :],
                                            in_=resid[row0 + ti * 128:row0 + (ti + 1) * 128, :]).then_inc(ctx.sem(f"CVL{ti % 2}"), 16)
                                e.dma_start(out=zlr[ti % 2][:, :],
                                            in_=zln[row0 + ti * 128:row0 + (ti + 1) * 128, :]).then_inc(ctx.sem(f"CVL{ti % 2}"), 16)
                    ctx.bump("A4T0", 16 * 64)
                    ctx.bump("A4T1", 16 * 64)
                    ctx.bump("A4L0", 16 * 16)
                    ctx.bump("A4L1", 16 * 16)
                    ctx.bump("A3X0", 16 * 8)
                    ctx.bump("A3X1", 16 * 8)
                    ctx.bump("CVL0", 32 * 8)
                    ctx.bump("CVL1", 32 * 8)

                    def z_tensor(e):
                        e.wait_ge(ctx.sem("V3C"), bYV + 32)
                        for tc in range(4):
                            for hb in range(32):
                                gi = tc * 32 + hb
                                e.wait_ge(ctx.sem(f"A4T{gi % 2}"), bZW[gi % 2] + 16 * (gi // 2 + 1))
                                if gi >= 2:
                                    e.wait_ge(ctx.sem("A3A"), bZA + 32 * ((gi - 2) // 32) + ((gi - 2) % 32) + 1)
                                for kb in range(8):
                                    inst = e.matmul(pa[gi % 2][:, 0:512], w1b[gi % 2][:, kb * 128:(kb + 1) * 128],
                                                    yT[:, kb * SEQ + tc * 512:kb * SEQ + (tc + 1) * 512],
                                                    start=(kb == 0), stop=(kb == 7))
                                inst.then_inc(ctx.sem("A3P"), 1)
                            for mb in range(8):
                                gj = tc * 8 + mb
                                e.wait_ge(ctx.sem("A3A"), bZA + 32 * (tc + 1))
                                e.wait_ge(ctx.sem(f"A4L{gj % 2}"), bZU[gj % 2] + 16 * (gj // 2 + 1))
                                if mb >= 2:
                                    e.wait_ge(ctx.sem("A3D"), bZV + evac_inc(tc, mb - 2))
                                elif tc >= 1:
                                    e.wait_ge(ctx.sem("A3D"), bZV + evac_inc(tc - 1, 6 + mb))
                                for hb in range(32):
                                    inst = e.matmul(pa[2 + mb % 2][:, 0:512], w2b[gj % 2][:, hb * 128:(hb + 1) * 128],
                                                    hgl[:, hb * 512:(hb + 1) * 512],
                                                    start=(hb == 0), stop=(hb == 31))
                                inst.then_inc(ctx.sem("A3P"), 1)
                    ctx.bump("A3P", 40 * 4)

                    def z_scalar(e):
                        for tc in range(4):
                            for hb in range(32):
                                gi = tc * 32 + hb
                                e.wait_ge(ctx.sem("A3P"), bZM + mmh_inc(gi))
                                e.activation(out=hgl[:, hb * 512:(hb + 1) * 512], in_=pa[gi % 2][:, 0:512],
                                             func=AF.Gelu_apprx_tanh).then_inc(ctx.sem("A3A"), 1)
                    ctx.bump("A3A", 32 * 4)

                    def z_vector(e):
                        k = [bZV]

                        def step(inst):
                            inst.then_inc(ctx.sem("A3D"), 1)
                            k[0] += 1
                        for tc in range(4):
                            for mb in range(8):
                                gj = tc * 8 + mb
                                e.wait_ge(ctx.sem("A3P"), bZM + mmz_inc(gj))
                                if mb == 0 and tc >= 1:
                                    e.wait_ge(ctx.sem("A3X0"), bZT[0] + 16 * 2 * tc)
                                    e.wait_ge(ctx.sem("A3X1"), bZT[1] + 16 * 2 * tc)
                                step(e.tensor_scalar(
                                    out=zbf[:, :].rearrange("p (i c) -> p i c", c=1024)[:, :, mb * 128:(mb + 1) * 128],
                                    in0=pa[2 + mb % 2][:, 0:512],
                                    scalar1=sgv[:, :], scalar2=None, op0=ALU.mult))
                            for i in range(4):
                                ti = tc * 4 + i
                                e.wait_ge(ctx.sem(f"A3X{ti % 2}"), bZT[ti % 2] + 16 * (ti // 2 + 1))
                                e.wait_ge(ctx.sem(f"CVL{ti % 2}"), bZL[ti % 2] + 32 * (ti // 2 + 1))
                                step(e.tensor_tensor(out=orow[ti % 2][:, :], in0=orow[ti % 2][:, :],
                                                     in1=zlr[ti % 2][:, :], op=ALU.add))
                                e.wait_ge(ctx.sem("A3D"), k[0])
                                step(e.tensor_tensor(out=orow[ti % 2][:, :], in0=orow[ti % 2][:, :],
                                                     in1=ztok[ti % 2][:, :], op=ALU.add))
                    ctx.bump("A3D", 16 * 4)

                    def z_gpsimd(e):
                        for ti in range(16):
                            e.wait_ge(ctx.sem("A3D"), bZV + add2_inc(ti))
                            e.dma_start(out=resid[row0 + ti * 128:row0 + (ti + 1) * 128, :],
                                        in_=orow[ti % 2][:, :]).then_inc(ctx.sem(f"FSO{ti % 2}"), 16)
                        e.wait_ge(ctx.sem("FSO0"), bZS[0] + 16 * 8)
                        e.wait_ge(ctx.sem("FSO1"), bZS[1] + 16 * 8)
                    ctx.bump("FSO0", 16 * 8)
                    ctx.bump("FSO1", 16 * 8)

                    block.sync(z_sync)
                    block.tensor(z_tensor)
                    block.scalar(z_scalar)
                    block.vector(z_vector)
                    block.gpsimd(z_gpsimd)
            barrier(ctx, block)


def host_constants():
    import ml_dtypes
    bf = ml_dtypes.bfloat16
    return {
        "tri": np.triu(np.ones((128, 128), np.float32)).astype(bf),
        "tri32": np.triu(np.ones((32, 32), np.float32), 1).astype(np.float32),
        "ones32": np.ones((32, 1), np.float32),
        "onescol": np.ones((128, 1)).astype(bf),
        "ones128b": np.ones((1, 128)).astype(bf),
        "ones128f": np.ones((1, 128), np.float32),
        "ones1x32": np.ones((1, 32), np.float32),
        "ident": np.eye(128).astype(bf),
        "amask": _attn_mask(),
    }


def _attn_mask():
    m = np.zeros((128, 640), np.float32)
    i = np.arange(128)[:, None]
    j = np.arange(128)[None, :]
    m[:, 0:128] = np.where(j > i, 0.0, -1e30)
    m[:, 512:640] = np.where(j <= i, 0.0, -1e30)
    return m


CONST_SPEC = [("tri", [128, 128], BF16), ("tri32", [32, 32], F32), ("ones32", [32, 1], F32),
              ("onescol", [128, 1], BF16), ("ones128b", [1, 128], BF16),
              ("ones128f", [1, 128], F32), ("ones1x32", [1, 32], F32),
              ("ident", [128, 128], BF16), ("amask", [128, 640], F32)]


def declare_common(nc, ctx, es):
    P = {}
    consts = []
    for (nm, shp, dt) in CONST_SPEC:
        dram = nc.dram_tensor("cst_" + nm, shp, dt, kind="ExternalInput")
        t = sbuf(ctx, nm + "_c", shp, dt, es)
        P[nm + "_sb"] = t
        consts.append((dram, t))
    P["iota16_sb"] = sbuf(ctx, "iota16_c", [128, NEXP], F32, es)
    P["eps_sb"] = sbuf(ctx, "eps_c", [128, 1], F32, es)
    P_EPS[0] = P["eps_sb"]
    P["psum"] = [es.enter_context(nc.psum_tensor(f"pbank{i}", [128, 512], F32)) for i in range(8)]
    P["xln"] = nc.dram_tensor("scr_xln", [NTOK, D], BF16, kind="Internal")
    P["perm"] = nc.dram_tensor("scr_perm", [NSLOT, 1], U32, kind="Internal")
    P["wslot"] = nc.dram_tensor("scr_wslot", [NSLOT, 1], F32, kind="Internal")
    P["ttot"] = nc.dram_tensor("scr_ttot", [32, NEXP], F32, kind="Internal")
    P["ctab"] = nc.dram_tensor("scr_ctab", [32, NEXP], F32, kind="Internal")
    P["etab"] = nc.dram_tensor("scr_etab", [1, NCT], I32, kind="Internal")
    return P, consts


def emit_prelude(ctx, block, P, consts):
    total = ctx.val("CONST") + 16 * len(consts) + 2

    def f_sync(e):
        for (dram, t) in consts:
            e.dma_start(out=t[:, :], in_=dram.ap()).then_inc(ctx.sem("CONST"), 16)
        e.wait_ge(ctx.sem("CONST"), total)

    def f_gpsimd(e):
        e.iota(P["iota16_sb"][:, :], [[1, NEXP]], base=0, channel_multiplier=0,
               allow_small_or_imprecise_dtypes=True).then_inc(ctx.sem("CONST"), 1)
        e.memset(P["eps_sb"][:, :], float(LN_EPS)).then_inc(ctx.sem("CONST"), 1)
        e.wait_ge(ctx.sem("CONST"), total)
    ctx.bump("CONST", 16 * len(consts) + 2)
    block.sync(f_sync)
    block.gpsimd(f_gpsimd)


def const_inputs():
    return {"cst_" + k: v for k, v in host_constants().items()}


def emit_dbg_dump(ctx, block, P, pairs, tag):
    with ExitStack() as es:
        dts = sorted({str(p[4]) for p in pairs})
        bufs = {k: sbuf(ctx, f"db{k}{tag}", [128, 2048],
                        next(p[4] for p in pairs if str(p[4]) == k), es) for k in dts}
        ld, st = "DBL", "DBS"
        bld, bst = ctx.val(ld), ctx.val(st)
        tiles = []
        for (s, d, rows, cols, dt) in pairs:
            for r0 in range(0, rows, 128):
                for c0 in range(0, cols, 2048):
                    tiles.append((s, d, r0, min(128, rows - r0), c0, min(2048, cols - c0), dt))

        def f_sync(e):
            for i, (s, d, r0, rr, c0, cc, dt) in enumerate(tiles):
                b = bufs[str(dt)]
                if i >= 1:
                    e.wait_ge(ctx.sem(st), bst + 16 * i)
                e.dma_start(out=b[:rr, :cc], in_=s[r0:r0 + rr, c0:c0 + cc]).then_inc(ctx.sem(ld), 16)
                e.wait_ge(ctx.sem(ld), bld + 16 * (i + 1))
                e.dma_start(out=d[r0:r0 + rr, c0:c0 + cc], in_=b[:rr, :cc]).then_inc(ctx.sem(st), 16)
            e.wait_ge(ctx.sem(st), bst + 16 * len(tiles))
        ctx.bump(ld, 16 * len(tiles))
        ctx.bump(st, 16 * len(tiles))
        block.sync(f_sync)


def build_unipool_test(nlayers, dbg=False):
    nc = bass.Bass("TRN2", target_bir_lowering=False)
    es = ExitStack()
    ctx = Ctx(nc, es)
    x0 = nc.dram_tensor("x0", [NTOK, D], F32, kind="ExternalInput").ap()
    w1 = nc.dram_tensor("l0_w1", [NEXP * D, FF], F32, kind="ExternalInput")
    w2 = nc.dram_tensor("l0_w2", [NEXP * FF, D], F32, kind="ExternalInput")
    routers = nc.dram_tensor("l0_routers", [4 * D, NEXP], F32, kind="ExternalInput")
    out = nc.dram_tensor("out", [NTOK, D], F32, kind="ExternalOutput").ap()
    resid = nc.dram_tensor("scr_resid", [NTOK, D], F32, kind="Internal").ap()

    P, consts = declare_common(nc, ctx, es)
    P["w1bf"] = nc.dram_tensor("scr_w1bf", [NEXP * 4 * 128, 8 * 1024], BF16, kind="Internal")
    P["w2bf"] = nc.dram_tensor("scr_w2bf", [NEXP * 4 * 128, 8 * 1024], BF16, kind="Internal")
    P["routers_bf"] = nc.dram_tensor("scr_rbf", [4 * D, NEXP], BF16, kind="Internal")

    if dbg:
        P["dbg_ex"] = nc.dram_tensor("dbg_ex", [128, NEXP * NTILE], F32, kind="ExternalOutput")
        P["dbg_idx"] = nc.dram_tensor("dbg_idx", [128, NTILE], F32, kind="ExternalOutput")
        P["dbg_xt"] = nc.dram_tensor("dbg_xt", [128, 2 * D], BF16, kind="ExternalOutput")
    with nc.Block() as block:
        emit_prelude(ctx, block, P, consts)
        with ExitStack() as es2:
            emit_convert_pretile(ctx, block, es2, w1.ap(), w2.ap(),
                                 P["w1bf"].ap(), P["w2bf"].ap(), "wp0")
            emit_convert(ctx, block, es2, [
                (routers.ap(), P["routers_bf"].ap(), 4 * D, NEXP, None),
            ], "w0")
            barrier(ctx, block)
        for r in range(nlayers):
            if nlayers == 1:
                mode = "single"
            elif r == 0:
                mode = "first"
            elif r == nlayers - 1:
                mode = "last"
            else:
                mode = "mid"
            emit_unipool(ctx, block, P, r, f"r{r}", mode, xsrc=x0, xdst=out)
            barrier(ctx, block)
        if dbg:
            dperm = nc.dram_tensor("dbg_perm", [NSLOT, 1], U32, kind="ExternalOutput").ap()
            dwslot = nc.dram_tensor("dbg_wslot", [NSLOT, 1], F32, kind="ExternalOutput").ap()
            dttot = nc.dram_tensor("dbg_ttot", [32, NEXP], F32, kind="ExternalOutput").ap()
            dctab = nc.dram_tensor("dbg_ctab", [32, NEXP], F32, kind="ExternalOutput").ap()
            detab = nc.dram_tensor("dbg_etab", [1, NCT], I32, kind="ExternalOutput").ap()
            dzbuf = nc.dram_tensor("dbg_zbuf", [NTOK, D], BF16, kind="ExternalOutput").ap()
            dw1 = nc.dram_tensor("dbg_w1pre", [256, 8192], BF16, kind="ExternalOutput").ap()
            dw2 = nc.dram_tensor("dbg_w2pre", [256, 8192], BF16, kind="ExternalOutput").ap()
            emit_dbg_dump(ctx, block, P, [
                (P["w1bf"].ap()[0:256, :], dw1, 256, 8192, BF16),
                (P["w2bf"].ap()[640:896, :], dw2, 256, 8192, BF16),
                (P["perm"].ap().rearrange("(p c) o -> p (c o)", p=128), dperm.rearrange("(p c) o -> p (c o)", p=128), 128, 64, U32),
                (P["wslot"].ap().rearrange("(p c) o -> p (c o)", p=128), dwslot.rearrange("(p c) o -> p (c o)", p=128), 128, 64, F32),
                (P["ttot"].ap(), dttot, 32, NEXP, F32),
                (P["ctab"].ap(), dctab, 32, NEXP, F32),
                (P["etab"].ap(), detab, 1, NCT, I32),
                (P["xln"].ap(), dzbuf, NTOK, D, BF16),
            ], "dbg")
            barrier(ctx, block)
    es.close()
    return nc




def build_attn_test():
    nc = bass.Bass("TRN2", target_bir_lowering=False)
    es = ExitStack()
    ctx = Ctx(nc, es)
    x0 = nc.dram_tensor("x0", [NTOK, D], F32, kind="ExternalInput").ap()
    wq = nc.dram_tensor("wq", [D, D], F32, kind="ExternalInput")
    wk = nc.dram_tensor("wk", [D, D], F32, kind="ExternalInput")
    wv = nc.dram_tensor("wv", [D, D], F32, kind="ExternalInput")
    wo = nc.dram_tensor("wo", [D, D], F32, kind="ExternalInput")
    out = nc.dram_tensor("out", [NTOK, D], F32, kind="ExternalOutput")
    P, consts = declare_common(nc, ctx, es)
    P["resid"] = out
    for nm in ("wq", "wk", "wv", "wo"):
        P[nm + "_bf"] = nc.dram_tensor("scr_" + nm, [D, D], BF16, kind="Internal")
    with nc.Block() as block:
        emit_prelude(ctx, block, P, consts)
        with ExitStack() as es2:
            emit_convert(ctx, block, es2, [
                (wq.ap(), P["wq_bf"].ap(), D, D, None),
                (wk.ap(), P["wk_bf"].ap(), D, D, None),
                (wv.ap(), P["wv_bf"].ap(), D, D, None),
                (wo.ap(), P["wo_bf"].ap(), D, D, None),
            ], "aw")
            barrier(ctx, block)
        emit_dbg_dump(ctx, block, P, [(x0, out.ap(), NTOK, D, F32)], "ini")
        barrier(ctx, block)
        emit_attn(ctx, block, P, "at")
    es.close()
    return nc




def build_ffn_test():
    nc = bass.Bass("TRN2", target_bir_lowering=False)
    es = ExitStack()
    ctx = Ctx(nc, es)
    x0 = nc.dram_tensor("x0", [NTOK, D], F32, kind="ExternalInput").ap()
    wg = nc.dram_tensor("wg", [FFI, D], I32, kind="ExternalInput")
    wu = nc.dram_tensor("wu", [FFI, D], I32, kind="ExternalInput")
    wd = nc.dram_tensor("wd", [D, FFI], I32, kind="ExternalInput")
    out = nc.dram_tensor("out", [NTOK, D], F32, kind="ExternalOutput")
    P, consts = declare_common(nc, ctx, es)
    P["resid"] = out
    wg_bf = nc.dram_tensor("scr_wg", [FFI, D], BF16, kind="Internal")
    wu_bf = nc.dram_tensor("scr_wu", [FFI, D], BF16, kind="Internal")
    wd_bf = nc.dram_tensor("scr_wd", [D, FFI], BF16, kind="Internal")
    P["wgT"] = nc.dram_tensor("scr_wgT", [D, FFI], BF16, kind="Internal")
    P["wuT"] = nc.dram_tensor("scr_wuT", [D, FFI], BF16, kind="Internal")
    P["wdT"] = nc.dram_tensor("scr_wdT", [FFI, D], BF16, kind="Internal")
    with nc.Block() as block:
        emit_prelude(ctx, block, P, consts)
        with ExitStack() as es2:
            emit_convert(ctx, block, es2, [
                (wg.ap(), wg_bf.ap(), FFI, D, None),
                (wu.ap(), wu_bf.ap(), FFI, D, None),
                (wd.ap(), wd_bf.ap(), D, FFI, None),
            ], "fw")
            barrier(ctx, block)
        with ExitStack() as es3:
            emit_transposeT(ctx, block, es3, [
                (wg_bf.ap(), P["wgT"].ap()),
                (wu_bf.ap(), P["wuT"].ap()),
                (wd_bf.ap(), P["wdT"].ap()),
            ], "ft")
            barrier(ctx, block)
        emit_dbg_dump(ctx, block, P, [(x0, out.ap(), NTOK, D, F32)], "ini")
        barrier(ctx, block)
        emit_ffn_int8(ctx, block, P, "ff")
    es.close()
    return nc




def build_lru_test():
    nc = bass.Bass("TRN2", target_bir_lowering=False)
    es = ExitStack()
    ctx = Ctx(nc, es)
    x0 = nc.dram_tensor("x0", [NTOK, D], F32, kind="ExternalInput").ap()
    nu = nc.dram_tensor("nu", [1, DST], F32, kind="ExternalInput")
    th = nc.dram_tensor("th", [1, DST], F32, kind="ExternalInput")
    bre = nc.dram_tensor("bre", [DST, D], F32, kind="ExternalInput")
    bim = nc.dram_tensor("bim", [DST, D], F32, kind="ExternalInput")
    cre = nc.dram_tensor("cre", [D, DST], F32, kind="ExternalInput")
    cim = nc.dram_tensor("cim", [D, DST], F32, kind="ExternalInput")
    dm = nc.dram_tensor("dm", [D, D], F32, kind="ExternalInput")
    w1 = nc.dram_tensor("w1", [D, 4096], F32, kind="ExternalInput")
    w2 = nc.dram_tensor("w2", [4096, D], F32, kind="ExternalInput")
    res = nc.dram_tensor("res", [1, 1], F32, kind="ExternalInput")
    out = nc.dram_tensor("out", [NTOK, D], F32, kind="ExternalOutput")
    P, consts = declare_common(nc, ctx, es)
    P["resid"] = out
    P["l1_nu"], P["l1_th"], P["l1_res"] = nu, th, res
    bre_bf = nc.dram_tensor("scr_bre", [DST, D], BF16, kind="Internal")
    bim_bf = nc.dram_tensor("scr_bim", [DST, D], BF16, kind="Internal")
    cre_bf = nc.dram_tensor("scr_cre", [D, DST], BF16, kind="Internal")
    cimn_bf = nc.dram_tensor("scr_cim", [D, DST], BF16, kind="Internal")
    d_bf = nc.dram_tensor("scr_dm", [D, D], BF16, kind="Internal")
    P["bTre"] = nc.dram_tensor("scr_bTre", [D, DST], BF16, kind="Internal")
    P["bTim"] = nc.dram_tensor("scr_bTim", [D, DST], BF16, kind="Internal")
    P["creT"] = nc.dram_tensor("scr_creT", [DST, D], BF16, kind="Internal")
    P["cimnT"] = nc.dram_tensor("scr_cimnT", [DST, D], BF16, kind="Internal")
    P["dT"] = nc.dram_tensor("scr_dT", [D, D], BF16, kind="Internal")
    P["w1l1_bf"] = nc.dram_tensor("scr_w1l1", [D, 4096], BF16, kind="Internal")
    P["w2l1_bf"] = nc.dram_tensor("scr_w2l1", [4096, D], BF16, kind="Internal")
    P["lru_scratch"] = nc.dram_tensor("scr_lrs", [8, 512], F32, kind="Internal")
    with nc.Block() as block:
        emit_prelude(ctx, block, P, consts)
        with ExitStack() as es2:
            emit_convert(ctx, block, es2, [
                (bre.ap(), bre_bf.ap(), DST, D, None),
                (bim.ap(), bim_bf.ap(), DST, D, None),
                (cre.ap(), cre_bf.ap(), D, DST, None),
                (cim.ap(), cimn_bf.ap(), D, DST, -1.0),
                (dm.ap(), d_bf.ap(), D, D, None),
                (w1.ap(), P["w1l1_bf"].ap(), D, 4096, None),
                (w2.ap(), P["w2l1_bf"].ap(), 4096, D, None),
            ], "lw")
            barrier(ctx, block)
        with ExitStack() as es3:
            emit_transposeT(ctx, block, es3, [
                (bre_bf.ap(), P["bTre"].ap()),
                (bim_bf.ap(), P["bTim"].ap()),
                (cre_bf.ap(), P["creT"].ap()),
                (cimn_bf.ap(), P["cimnT"].ap()),
                (d_bf.ap(), P["dT"].ap()),
            ], "lt")
            barrier(ctx, block)
        emit_dbg_dump(ctx, block, P, [(x0, out.ap(), NTOK, D, F32)], "ini")
        barrier(ctx, block)
        emit_lru(ctx, block, P, "lr")
    es.close()
    return nc




def build_full():
    nc = bass.Bass("TRN2", target_bir_lowering=False)
    es = ExitStack()
    ctx = Ctx(nc, es)
    x0 = nc.dram_tensor("x0", [NTOK, D], F32, kind="ExternalInput").ap()
    l0w1 = nc.dram_tensor("l0_w1", [NEXP * D, FF], F32, kind="ExternalInput")
    l0w2 = nc.dram_tensor("l0_w2", [NEXP * FF, D], F32, kind="ExternalInput")
    routers = nc.dram_tensor("l0_routers", [4 * D, NEXP], F32, kind="ExternalInput")
    wq = nc.dram_tensor("wq", [D, D], F32, kind="ExternalInput")
    wk = nc.dram_tensor("wk", [D, D], F32, kind="ExternalInput")
    wv = nc.dram_tensor("wv", [D, D], F32, kind="ExternalInput")
    wo = nc.dram_tensor("wo", [D, D], F32, kind="ExternalInput")
    nu = nc.dram_tensor("nu", [1, DST], F32, kind="ExternalInput")
    th = nc.dram_tensor("th", [1, DST], F32, kind="ExternalInput")
    bre = nc.dram_tensor("bre", [DST, D], F32, kind="ExternalInput")
    bim = nc.dram_tensor("bim", [DST, D], F32, kind="ExternalInput")
    cre = nc.dram_tensor("cre", [D, DST], F32, kind="ExternalInput")
    cim = nc.dram_tensor("cim", [D, DST], F32, kind="ExternalInput")
    dm = nc.dram_tensor("dm", [D, D], F32, kind="ExternalInput")
    l1w1 = nc.dram_tensor("l1_w1", [D, 4096], F32, kind="ExternalInput")
    l1w2 = nc.dram_tensor("l1_w2", [4096, D], F32, kind="ExternalInput")
    res = nc.dram_tensor("res", [1, 1], F32, kind="ExternalInput")
    wg = nc.dram_tensor("wg", [FFI, D], I32, kind="ExternalInput")
    wu = nc.dram_tensor("wu", [FFI, D], I32, kind="ExternalInput")
    wd = nc.dram_tensor("wd", [D, FFI], I32, kind="ExternalInput")
    y = nc.dram_tensor("y", [NTOK, D], F32, kind="ExternalOutput")

    P, consts = declare_common(nc, ctx, es)
    P["resid"] = y
    P["l1_nu"], P["l1_th"], P["l1_res"] = nu, th, res
    P["w1bf"] = nc.dram_tensor("scr_w1bf", [NEXP * 4 * 128, 8 * 1024], BF16, kind="Internal")
    P["w2bf"] = nc.dram_tensor("scr_w2bf", [NEXP * 4 * 128, 8 * 1024], BF16, kind="Internal")
    P["routers_bf"] = nc.dram_tensor("scr_rbf", [4 * D, NEXP], BF16, kind="Internal")
    for nm in ("wq", "wk", "wv", "wo"):
        P[nm + "_bf"] = nc.dram_tensor("scr_" + nm, [D, D], BF16, kind="Internal")
    wg_bf = nc.dram_tensor("scr_wg", [FFI, D], BF16, kind="Internal")
    wu_bf = nc.dram_tensor("scr_wu", [FFI, D], BF16, kind="Internal")
    wd_bf = nc.dram_tensor("scr_wd", [D, FFI], BF16, kind="Internal")
    P["wgT"] = nc.dram_tensor("scr_wgT", [D, FFI], BF16, kind="Internal")
    P["wuT"] = nc.dram_tensor("scr_wuT", [D, FFI], BF16, kind="Internal")
    P["wdT"] = nc.dram_tensor("scr_wdT", [FFI, D], BF16, kind="Internal")
    bre_bf = nc.dram_tensor("scr_bre", [DST, D], BF16, kind="Internal")
    bim_bf = nc.dram_tensor("scr_bim", [DST, D], BF16, kind="Internal")
    cre_bf = nc.dram_tensor("scr_cre", [D, DST], BF16, kind="Internal")
    cimn_bf = nc.dram_tensor("scr_cim", [D, DST], BF16, kind="Internal")
    d_bf = nc.dram_tensor("scr_dm", [D, D], BF16, kind="Internal")
    P["bTre"] = nc.dram_tensor("scr_bTre", [D, DST], BF16, kind="Internal")
    P["bTim"] = nc.dram_tensor("scr_bTim", [D, DST], BF16, kind="Internal")
    P["creT"] = nc.dram_tensor("scr_creT", [DST, D], BF16, kind="Internal")
    P["cimnT"] = nc.dram_tensor("scr_cimnT", [DST, D], BF16, kind="Internal")
    P["dT"] = nc.dram_tensor("scr_dT", [D, D], BF16, kind="Internal")
    P["w1l1_bf"] = nc.dram_tensor("scr_w1l1", [D, 4096], BF16, kind="Internal")
    P["w2l1_bf"] = nc.dram_tensor("scr_w2l1", [4096, D], BF16, kind="Internal")
    P["lru_scratch"] = nc.dram_tensor("scr_lrs", [8, 512], F32, kind="Internal")

    with nc.Block() as block:
        emit_prelude(ctx, block, P, consts)
        with ExitStack() as es2p:
            emit_convert_pretile(ctx, block, es2p, l0w1.ap(), l0w2.ap(),
                                 P["w1bf"].ap(), P["w2bf"].ap(), "wp0")
            barrier(ctx, block)
        with ExitStack() as es2:
            emit_convert(ctx, block, es2, [
                (routers.ap(), P["routers_bf"].ap(), 4 * D, NEXP, None),
                (wq.ap(), P["wq_bf"].ap(), D, D, None),
                (wk.ap(), P["wk_bf"].ap(), D, D, None),
                (wv.ap(), P["wv_bf"].ap(), D, D, None),
                (wo.ap(), P["wo_bf"].ap(), D, D, None),
                (wg.ap(), wg_bf.ap(), FFI, D, None),
                (wu.ap(), wu_bf.ap(), FFI, D, None),
                (wd.ap(), wd_bf.ap(), D, FFI, None),
                (bre.ap(), bre_bf.ap(), DST, D, None),
                (bim.ap(), bim_bf.ap(), DST, D, None),
                (cre.ap(), cre_bf.ap(), D, DST, None),
                (cim.ap(), cimn_bf.ap(), D, DST, -1.0),
                (dm.ap(), d_bf.ap(), D, D, None),
                (l1w1.ap(), P["w1l1_bf"].ap(), D, 4096, None),
                (l1w2.ap(), P["w2l1_bf"].ap(), 4096, D, None),
            ], "cv")
            barrier(ctx, block)
        with ExitStack() as es3:
            emit_transposeT(ctx, block, es3, [
                (wg_bf.ap(), P["wgT"].ap()),
                (wu_bf.ap(), P["wuT"].ap()),
                (wd_bf.ap(), P["wdT"].ap()),
                (bre_bf.ap(), P["bTre"].ap()),
                (bim_bf.ap(), P["bTim"].ap()),
                (cre_bf.ap(), P["creT"].ap()),
                (cimn_bf.ap(), P["cimnT"].ap()),
                (d_bf.ap(), P["dT"].ap()),
            ], "tt")
            barrier(ctx, block)
        resid = y.ap()
        for r in range(4):
            mode = "first" if r == 0 else ("last" if r == 3 else "mid")
            emit_unipool(ctx, block, P, r, f"ua{r}", mode,
                         xsrc=(x0 if True else None), xdst=resid)
            barrier(ctx, block)
        emit_attn(ctx, block, P, "at")
        for r in range(4):
            mode = "first" if r == 0 else ("last" if r == 3 else "mid")
            emit_unipool(ctx, block, P, r, f"ub{r}", mode, xsrc=resid, xdst=resid)
            barrier(ctx, block)
        emit_ffn_int8(ctx, block, P, "f1")
        emit_lru(ctx, block, P, "lr")
        emit_ffn_int8(ctx, block, P, "f2")
    es.close()
    return nc


def shard_inputs(inputs, core_id):
    x = np.ascontiguousarray(inputs["L0_attn_x"]).reshape(16, SEQ, D)
    m = {
        "x0": np.ascontiguousarray(x[2 * core_id:2 * core_id + 2].reshape(NTOK, D)),
        "l0_w1": inputs["L0_ffn_W1"].reshape(NEXP * D, FF),
        "l0_w2": inputs["L0_ffn_W2"].reshape(NEXP * FF, D),
        "l0_routers": inputs["L0_ffn_routers"].reshape(4 * D, NEXP),
        "wq": inputs["L0_attn_Wq"], "wk": inputs["L0_attn_Wk"],
        "wv": inputs["L0_attn_Wv"], "wo": inputs["L0_attn_Wo"],
        "nu": inputs["L1_attn_nu_log"].reshape(1, DST),
        "th": inputs["L1_attn_theta_log"].reshape(1, DST),
        "bre": inputs["L1_attn_B_re"], "bim": inputs["L1_attn_B_im"],
        "cre": inputs["L1_attn_C_re"], "cim": inputs["L1_attn_C_im"],
        "dm": inputs["L1_attn_D_mat"],
        "l1_w1": inputs["L1_attn_W1"], "l1_w2": inputs["L1_attn_W2"],
        "res": np.asarray(inputs["L1_attn_res_logit"], dtype=np.float32).reshape(1, 1),
        "wg": inputs["L1_ffn_w_gate"], "wu": inputs["L1_ffn_w_up"],
        "wd": inputs["L1_ffn_w_down"],
    }
    m = {k: np.ascontiguousarray(v) for k, v in m.items()}
    m.update(const_inputs())
    return m


def kernel(**inputs):
    nc = build_full()
    in_maps = [shard_inputs(inputs, c) for c in range(8)]
    res = run_bass_kernel_spmd(nc, in_maps, core_ids=list(range(8)))
    out = np.concatenate([res.results[c]["y"].reshape(2, SEQ, D) for c in range(8)], axis=0)
    return out.astype(np.float32)




# version 08
# speedup vs baseline: 1.0352x; 1.0352x over previous
import math
import numpy as np
from contextlib import ExitStack

import concourse.bass as bass
import concourse.mybir as mybir
from concourse.bass import IndirectOffsetOnAxis
from concourse.bass_utils import run_bass_kernel_spmd

F32 = mybir.dt.float32
BF16 = mybir.dt.bfloat16
I32 = mybir.dt.int32
U32 = mybir.dt.uint32
AF = mybir.ActivationFunctionType
ALU = mybir.AluOpType
AX = mybir.AxisListType

D = 1024
NTOK = 4096
SEQ = 2048
NB = 2
NTILE = 32
ZROW = False
NEXP = 16
FF = 4096
TT = 256
NCT = 32
NSLOT = NCT * TT
WINDOW = 512
HEADS = 16
HDIM = 64
DST = 256
FFI = 3072
LN_EPS = 1e-5


class Ctx:
    def __init__(self, nc, es):
        self.nc = nc
        self.es = es
        self.sems = {}
        self.cnt = {}
        self.nbar = 0

    def sem(self, name):
        if name not in self.sems:
            self.sems[name] = self.es.enter_context(self.nc.semaphore(name))
            self.cnt[name] = 0
        return self.sems[name]

    def val(self, name):
        self.sem(name)
        return self.cnt[name]

    def bump(self, name, amount):
        self.sem(name)
        self.cnt[name] += amount


def sbuf(ctx, name, shape, dtype, es=None):
    return (es or ctx.es).enter_context(ctx.nc.sbuf_tensor(name, shape, dtype))


def barrier(ctx, block):
    ctx.nbar += 1
    b = ctx.sem("BAR")
    tgt = 5 * ctx.nbar

    def f(e):
        e.sem_inc(b, 1)
        e.wait_ge(b, tgt)
    for eng in (block.sync, block.scalar, block.vector, block.tensor, block.gpsimd):
        eng(f)
    ctx.cnt["BAR"] = tgt


def emit_convert(ctx, block, es, jobs, tag):
    tiles = []
    for (src, dst, rows, cols, scale) in jobs:
        isint = src.dtype == I32
        g = 4
        while g > 1 and (rows % (128 * g) != 0 or cols * 4 * g > 16384):
            g //= 2
        for r0 in range(0, rows, 128 * g):
            tiles.append((src, dst, r0, g, cols, scale, isint))
    maxc = max(g * cols for (_, _, _, g, cols, _, _) in tiles)
    tin = [sbuf(ctx, f"cvi{i}{tag}", [128, maxc], F32, es) for i in range(2)]
    has_int = any(t[6] for t in tiles)
    tini = [sbuf(ctx, f"cvj{i}{tag}", [128, maxc], I32, es) for i in range(2)] if has_int else None
    tout = [sbuf(ctx, f"cvo{i}{tag}", [128, maxc], BF16, es) for i in range(2)]
    cv = "CVC"
    n = len(tiles)
    bcv = ctx.val(cv)
    bld = [ctx.val("CVL0"), ctx.val("CVL1")]
    bst = [ctx.val("CVS0"), ctx.val("CVS1")]

    def f_load(par):
        def f(e):
            for i, (src, dst, r0, g, cols, scale, isint) in enumerate(tiles):
                if i % 2 != par:
                    continue
                if i >= 2:
                    e.wait_ge(ctx.sem(cv), bcv + i - 1)
                buf = tini[i % 2] if isint else tin[i % 2]
                e.dma_start(out=buf[:, 0:g * cols],
                            in_=src[r0:r0 + 128 * g, :].rearrange("(p g) c -> p (g c)", g=g)).then_inc(ctx.sem(f"CVL{i % 2}"), 16)
        return f
    ctx.bump("CVL0", 16 * ((n + 1) // 2))
    ctx.bump("CVL1", 16 * (n // 2))

    def f_vector(e):
        for i, (src, dst, r0, g, cols, scale, isint) in enumerate(tiles):
            e.wait_ge(ctx.sem(f"CVL{i % 2}"), bld[i % 2] + 16 * (i // 2 + 1))
            if i >= 2:
                e.wait_ge(ctx.sem(f"CVS{i % 2}"), bst[i % 2] + 16 * ((i - 2) // 2 + 1))
            buf = tini[i % 2] if isint else tin[i % 2]
            if scale is None:
                inst = e.tensor_copy(out=tout[i % 2][:, 0:g * cols], in_=buf[:, 0:g * cols])
            else:
                inst = e.tensor_scalar(out=tout[i % 2][:, 0:g * cols], in0=buf[:, 0:g * cols],
                                       scalar1=float(scale), scalar2=None, op0=ALU.mult)
            inst.then_inc(ctx.sem(cv), 1)
    ctx.bump(cv, n)

    def f_gpsimd(e):
        for i, (src, dst, r0, g, cols, scale, isint) in enumerate(tiles):
            e.wait_ge(ctx.sem(cv), bcv + i + 1)
            e.dma_start(out=dst[r0:r0 + 128 * g, :].rearrange("(p g) c -> p (g c)", g=g),
                        in_=tout[i % 2][:, 0:g * cols]).then_inc(ctx.sem(f"CVS{i % 2}"), 16)
        e.wait_ge(ctx.sem("CVS0"), bst[0] + 16 * ((n + 1) // 2))
        e.wait_ge(ctx.sem("CVS1"), bst[1] + 16 * (n // 2))
    ctx.bump("CVS0", 16 * ((n + 1) // 2))
    ctx.bump("CVS1", 16 * (n // 2))

    block.sync(f_load(0))
    block.scalar(f_load(1))
    block.vector(f_vector)
    block.gpsimd(f_gpsimd)


def emit_convert_pretile(ctx, block, es, w1_ap, w2_ap, w1pre_ap, w2pre_ap, tag):
    L = [sbuf(ctx, f"cpi{i}{tag}", [128, 4096], F32, es) for i in range(2)]
    wbuf = [[sbuf(ctx, f"cpw{q}{h}{tag}", [128, 4096], BF16, es) for h in range(2)] for q in range(4)]
    w2b = [sbuf(ctx, f"cpz{h}{tag}", [128, 8192], BF16, es) for h in range(2)]
    ld, cv, st = "CPL", "CPC", "CPS"
    bcv = ctx.val(cv)
    bld = [ctx.val("CPL0"), ctx.val("CPL1")]
    bst = [ctx.val("CPS0"), ctx.val("CPS1")]
    recs = []
    for e_ in range(NEXP):
        for kb in range(8):
            i = e_ * 8 + kb
            h = i // 4
            casts = [(q * 1024, wbuf[q][h % 2], (kb % 4) * 1024, 1024) for q in range(4)]
            war = (h % 2, 64 * (h // 2)) if (kb % 4 == 0 and h >= 2) else None
            stores = []
            if kb % 4 == 3:
                stores = [(w1pre_ap, (4 * e_ + q) * 128, (h % 2) * 4096, wbuf[q][h % 2], 4096, h % 2)
                          for q in range(4)]
            recs.append((w1_ap, e_ * 1024 + kb * 128, 4096, casts, war, stores))
    for e_ in range(NEXP):
        for q in range(4):
            jj = e_ * 4 + q
            for hb in range(8):
                casts = [(0, w2b[jj % 2], hb * 1024, 1024)]
                war = (jj % 2, 1024 + 16 * (jj // 2)) if (hb == 0 and jj >= 2) else None
                stores = []
                if hb == 7:
                    stores = [(w2pre_ap, (4 * e_ + q) * 128, 0, w2b[jj % 2], 8192, jj % 2)]
                recs.append((w2_ap, e_ * 4096 + q * 1024 + hb * 128, 1024, casts, war, stores))
    n = len(recs)
    cpre = [0]
    for r in recs:
        cpre.append(cpre[-1] + len(r[3]))

    def f_load(par):
        def f(e):
            for i, (srcap, r0, w, casts, war, stores) in enumerate(recs):
                if i % 2 != par:
                    continue
                if i >= 2:
                    e.wait_ge(ctx.sem(cv), bcv + cpre[i - 1])
                e.dma_start(out=L[i % 2][:, :w], in_=srcap[r0:r0 + 128, :]).then_inc(ctx.sem(f"CPL{i % 2}"), 16)
        return f
    ctx.bump("CPL0", 16 * ((n + 1) // 2))
    ctx.bump("CPL1", 16 * (n // 2))

    def f_vector(e):
        for i, (srcap, r0, w, casts, war, stores) in enumerate(recs):
            e.wait_ge(ctx.sem(f"CPL{i % 2}"), bld[i % 2] + 16 * (i // 2 + 1))
            if war is not None:
                e.wait_ge(ctx.sem(f"CPS{war[0]}"), bst[war[0]] + war[1])
            for (so, ob, oo, cw) in casts:
                e.tensor_copy(out=ob[:, oo:oo + cw], in_=L[i % 2][:, so:so + cw]).then_inc(ctx.sem(cv), 1)
    ctx.bump(cv, cpre[-1])

    stot = [0, 0]

    def f_gpsimd(e):
        cnt = [0, 0]
        for i, (srcap, r0, w, casts, war, stores) in enumerate(recs):
            if not stores:
                continue
            e.wait_ge(ctx.sem(cv), bcv + cpre[i + 1])
            for (dap, dr, dc, ob, sw, p) in stores:
                e.dma_start(out=dap[dr:dr + 128, dc:dc + sw],
                            in_=ob[:, 0:sw]).then_inc(ctx.sem(f"CPS{p}"), 16)
                cnt[p] += 16
        e.wait_ge(ctx.sem("CPS0"), bst[0] + cnt[0])
        e.wait_ge(ctx.sem("CPS1"), bst[1] + cnt[1])
    for (srcap, r0, w, casts, war, stores) in recs:
        for s in stores:
            stot[s[5]] += 16
    ctx.bump("CPS0", stot[0])
    ctx.bump("CPS1", stot[1])

    block.sync(f_load(0))
    block.scalar(f_load(1))
    block.vector(f_vector)
    block.gpsimd(f_gpsimd)


def emit_unipool(ctx, block, P, router_idx, tag, mode, xsrc=None, xdst=None):
    nc = ctx.nc
    do_ln = mode in ("first", "single")
    do_res = mode in ("last", "single")
    wscale = 0.5 if do_res else 1.0
    with ExitStack() as es:
        sb = lambda n, s, d: sbuf(ctx, n + tag, s, d, es)

        router = P["routers_bf"].ap()[router_idx * D:(router_idx + 1) * D, :]
        zbuf = P["xln"].ap()
        src = xsrc
        dst = xdst
        perm_d = P["perm"].ap()
        wslot_d = P["wslot"].ap()
        ttot_d = P["ttot"].ap()
        ctab_d = P["ctab"].ap()
        etab_d = P["etab"].ap()
        w1pre = P["w1bf"].ap()
        w2pre = P["w2bf"].ap()
        TRI = P["tri_sb"]
        TRI32 = P["tri32_sb"]
        ONES32 = P["ones32_sb"]
        ONESCOL = P["onescol_sb"]
        ONES128B = P["ones128b_sb"]
        ONES128F = P["ones128f_sb"]
        ONES1x32 = P["ones1x32_sb"]
        IOTA16 = P["iota16_sb"]
        pa = P["psum"]

        use_skip = not do_res
        use_zrow = (not do_ln) and ZROW
        xrow = [sb(f"u1x{i}", [128, D], F32) for i in range(2)] if do_ln else None
        xc = [sb(f"u1c{i}", [128, D], F32) for i in range(2)] if do_ln else None
        zrow = [sb(f"u1z{i}", [128, D], BF16) for i in range(2)] if use_zrow else None
        sq = sb("u1sq", [128, D], F32) if do_ln else None
        lnrow = [sb(f"u1l{i}", [128, D], BF16) for i in range(2)] if do_ln else None
        xT1 = sb("u1xT", [128, 2 * D], BF16)
        ssum = sb("u1sm", [128, 1], F32)
        negmean = sb("u1nm", [128, 1], F32)
        varD = sb("u1vd", [128, 1], F32)
        lnv = sb("u1lv", [128, 1], F32)
        rstd = sb("u1rs", [128, 1], F32)
        nmx = sb("u1mx", [128, 1], F32)
        exbuf_all = sb("u1ex", [128, NEXP * NTILE], F32)
        sexp = sb("u1se", [128, 1], F32)
        recw = sb("u1rw", [128, 1], F32)
        mx8 = sb("u1m8", [128, 8], F32)
        mi8 = sb("u1i8", [128, 8], U32)
        idxf_all = sb("u1if", [128, NTILE], F32)
        w_all = sb("u1wa", [128, NTILE], F32)
        rank_all = sb("u1ra", [128, NTILE], F32)
        oh_all = sb("u1oh", [128, NEXP * NTILE], BF16)
        cumsel = sb("u1cs", [128, NEXP], F32)
        ttrow = sb("u1tr", [1, NEXP], F32)
        router_sb = sb("u1rt", [128, 8 * NEXP], BF16)

        ps_log, ps_cum, ps_tot = pa[0], pa[1], pa[2]
        LD, TP, XS, TS = "ULD", "UTP", "UXS", "UTS"
        V, S, M = "UV", "US", "UM"
        bTP, bTS = ctx.val(TP), ctx.val(TS)
        bLD = [ctx.val("ULD0"), ctx.val("ULD1")]
        bXS = [ctx.val("UXS0"), ctx.val("UXS1")]
        bV, bS, bM = ctx.val(V), ctx.val(S), ctx.val(M)
        VP = 15 if do_ln else 10
        SP = 4 if do_ln else 1
        VR = 5 if do_ln else 0

        RT = "URT"
        bRT = ctx.val(RT)

        def s1_sync(e):
            for kb in range(8):
                e.dma_start(out=router_sb[:, kb * NEXP:(kb + 1) * NEXP],
                            in_=router[kb * 128:(kb + 1) * 128, :]).then_inc(ctx.sem(RT), 16)
            for i in range(NTILE):
                if do_ln:
                    if i >= 2:
                        e.wait_ge(ctx.sem(V), bV + VP * (i - 2) + 3)
                    e.dma_start(out=xrow[i % 2][:, :], in_=src[i * 128:(i + 1) * 128, :]).then_inc(ctx.sem(f"ULD{i % 2}"), 16)
                    e.wait_ge(ctx.sem(V), bV + VP * i + 5)
                elif use_zrow:
                    e.wait_ge(ctx.sem(f"UZG{i % 2}"), bUZG[i % 2] + 16 * (i // 2 + 1))
                if i >= 1:
                    e.wait_ge(ctx.sem(M), bM + 3 * i - 2)
                srcap = lnrow[i % 2][:, :] if do_ln else (
                    zrow[i % 2][:, :] if use_zrow else zbuf[i * 128:(i + 1) * 128, :])
                e.dma_start_transpose(out=b3(xT1[:, (i % 2) * D:(i % 2 + 1) * D]),
                                      in_=b3(srcap)).then_inc(ctx.sem(TP), 16)
        if do_ln:
            ctx.bump("ULD0", 16 * (NTILE // 2))
            ctx.bump("ULD1", 16 * (NTILE // 2))
        bUZG = [ctx.val("UZG0"), ctx.val("UZG1")]
        if use_zrow:
            ctx.bump("UZG0", 16 * (NTILE // 2))
            ctx.bump("UZG1", 16 * (NTILE // 2))

        def s1_scalar(e):
            for i in range(NTILE):
                if do_ln:
                    e.wait_ge(ctx.sem(V), bV + VP * i + 1)
                    e.activation(out=negmean[:], in_=ssum[:], func=AF.Copy, scale=-1.0 / D).then_inc(ctx.sem(S), 1)
                    e.wait_ge(ctx.sem(V), bV + VP * i + 4)
                    e.activation(out=lnv[:], in_=varD[:], func=AF.Ln, bias=P["eps_sb"][:, :], scale=1.0 / D).then_inc(ctx.sem(S), 1)
                    e.wait_ge(ctx.sem(S), bS + SP * i + 2)
                    e.activation(out=rstd[:], in_=lnv[:], func=AF.Exp, scale=-0.5).then_inc(ctx.sem(S), 1)
                    e.wait_ge(ctx.sem(V), bV + VP * i + 5)
                e.wait_ge(ctx.sem(M), bM + 3 * i + 1)
                e.wait_ge(ctx.sem(V), bV + VP * i + VR + 1)
                e.activation(out=exbuf_all[:, i * NEXP:(i + 1) * NEXP], in_=ps_log[:, 0:NEXP], func=AF.Exp, bias=nmx[:],
                             scale=1.0, accum_out=sexp[:]).then_inc(ctx.sem(S), 1)
        ctx.bump(S, SP * NTILE)
        ctx.bump(TP, 16 * NTILE)

        def s1_gpsimd(e):
            if do_ln:
                for i in range(NTILE):
                    e.wait_ge(ctx.sem(V), bV + VP * i + 5)
                    e.dma_start(out=zbuf[i * 128:(i + 1) * 128, :], in_=lnrow[i % 2][:, :]).then_inc(ctx.sem(f"UXS{i % 2}"), 16)
                    e.wait_ge(ctx.sem(V), bV + VP * i + VR + 10)
                    e.dma_start(out=ttot_d[i:i + 1, :], in_=ttrow[:, :]).then_inc(ctx.sem(TS), 16)
            elif use_zrow:
                K = 4
                for j in range(NTILE + K):
                    if j < NTILE:
                        if j >= 2:
                            e.wait_ge(ctx.sem(TP), bTP + 16 * (j - 1))
                        e.dma_start(out=zrow[j % 2][:, :],
                                    in_=zbuf[j * 128:(j + 1) * 128, :]).then_inc(ctx.sem(f"UZG{j % 2}"), 16)
                    if j >= K:
                        i = j - K
                        e.wait_ge(ctx.sem(V), bV + VP * i + VR + 10)
                        e.dma_start(out=ttot_d[i:i + 1, :], in_=ttrow[:, :]).then_inc(ctx.sem(TS), 16)
            else:
                for i in range(NTILE):
                    e.wait_ge(ctx.sem(V), bV + VP * i + VR + 10)
                    e.dma_start(out=ttot_d[i:i + 1, :], in_=ttrow[:, :]).then_inc(ctx.sem(TS), 16)
            if "dbg_ex" in P:
                e.wait_ge(ctx.sem(M), bM + 3 * NTILE)
                bdbg = ctx.val("DBGS")
                e.dma_start(out=P["dbg_ex"].ap(), in_=exbuf_all[:, :]).then_inc(ctx.sem("DBGS"), 16)
                e.dma_start(out=P["dbg_idx"].ap(), in_=idxf_all[:, :]).then_inc(ctx.sem("DBGS"), 16)
                e.dma_start(out=P["dbg_xt"].ap(), in_=xT1[:, :]).then_inc(ctx.sem("DBGS"), 16)
                e.wait_ge(ctx.sem("DBGS"), bdbg + 48)
        if do_ln:
            ctx.bump("UXS0", 16 * (NTILE // 2))
            ctx.bump("UXS1", 16 * (NTILE // 2))
        ctx.bump(TS, 16 * NTILE)
        if "dbg_ex" in P:
            ctx.bump("DBGS", 48)

        def s1_vector(e):
            for i in range(NTILE):
                b = i % 2
                vb = bV + VP * i

                def step(k, inst):
                    inst.then_inc(ctx.sem(V), 1)
                    e.wait_ge(ctx.sem(V), vb + k)
                if do_ln:
                    e.wait_ge(ctx.sem(f"ULD{i % 2}"), bLD[i % 2] + 16 * (i // 2 + 1))
                    if i >= 1:
                        e.wait_ge(ctx.sem(S), bS + SP * (i - 1) + SP)
                    step(1, e.reduce_sum(ssum[:], xrow[b][:, :], axis=AX.X))
                    e.wait_ge(ctx.sem(S), bS + SP * i + 1)
                    step(2, e.tensor_scalar(out=xc[b][:, :], in0=xrow[b][:, :], scalar1=negmean[:],
                                            scalar2=None, op0=ALU.add))
                    step(3, e.tensor_tensor(out=sq[:, :], in0=xc[b][:, :], in1=xc[b][:, :], op=ALU.mult))
                    step(4, e.reduce_sum(varD[:], sq[:, :], axis=AX.X))
                    e.wait_ge(ctx.sem(S), bS + SP * i + 3)
                    if i >= 2:
                        e.wait_ge(ctx.sem(TP), bTP + 16 * (i - 1))
                        e.wait_ge(ctx.sem(f"UXS{i % 2}"), bXS[i % 2] + 16 * ((i - 2) // 2 + 1))
                    step(5, e.tensor_scalar(out=lnrow[b][:, :], in0=xc[b][:, :], scalar1=rstd[:],
                                            scalar2=None, op0=ALU.mult))
                e.wait_ge(ctx.sem(M), bM + 3 * i + 1)
                if i >= 1:
                    e.wait_ge(ctx.sem(S), bS + SP * (i - 1) + SP)
                step(VR + 1, e.reduce_max(nmx[:], ps_log[:, 0:NEXP], axis=AX.X, negate=True))
                e.wait_ge(ctx.sem(S), bS + SP * i + SP)
                step(VR + 2, e.reciprocal(recw[:], sexp[:]))
                step(VR + 3, e.tensor_scalar(out=w_all[:, i:i + 1], in0=recw[:], scalar1=wscale,
                                             scalar2=None, op0=ALU.mult))
                step(VR + 4, e.max(mx8[:], exbuf_all[:, i * NEXP:(i + 1) * NEXP]))
                step(VR + 5, e.max_index(mi8[:], mx8[:], exbuf_all[:, i * NEXP:(i + 1) * NEXP]))
                step(VR + 6, e.tensor_copy(out=idxf_all[:, i:i + 1], in_=mi8[:, 0:1]))
                step(VR + 7, e.tensor_scalar(out=oh_all[:, i * NEXP:(i + 1) * NEXP], in0=IOTA16[:, :],
                                             scalar1=idxf_all[:, i:i + 1], scalar2=None, op0=ALU.is_equal))
                e.wait_ge(ctx.sem(M), bM + 3 * i + 2)
                step(VR + 8, e.tensor_tensor(out=cumsel[:, :], in0=ps_cum[:, 0:NEXP],
                                             in1=oh_all[:, i * NEXP:(i + 1) * NEXP], op=ALU.mult))
                step(VR + 9, e.reduce_sum(rank_all[:, i:i + 1], cumsel[:, :], axis=AX.X))
                e.wait_ge(ctx.sem(M), bM + 3 * i + 3)
                if i >= 1:
                    e.wait_ge(ctx.sem(TS), bTS + 16 * i)
                step(VR + 10, e.tensor_copy(out=ttrow[:, :], in_=ps_tot[0:1, 0:NEXP]))
        ctx.bump(V, VP * NTILE)

        ctx.bump(RT, 16 * 8)

        def s1_tensor(e):
            e.wait_ge(ctx.sem(RT), bRT + 16 * 8)
            for i in range(NTILE):
                e.wait_ge(ctx.sem(TP), bTP + 16 * (i + 1))
                if i >= 1:
                    e.wait_ge(ctx.sem(S), bS + SP * (i - 1) + SP)
                    e.wait_ge(ctx.sem(V), bV + VP * (i - 1) + VR + 1)
                for db in range(8):
                    inst = e.matmul(ps_log[:, 0:NEXP],
                                    xT1[:, (i % 2) * D + db * 128:(i % 2) * D + (db + 1) * 128],
                                    router_sb[:, db * NEXP:(db + 1) * NEXP],
                                    start=(db == 0), stop=(db == 7))
                inst.then_inc(ctx.sem(M), 1)
                e.wait_ge(ctx.sem(V), bV + VP * i + VR + 7)
                if i >= 1:
                    e.wait_ge(ctx.sem(V), bV + VP * (i - 1) + VR + 8)
                e.matmul(ps_cum[:, 0:NEXP], TRI[:, :], oh_all[:, i * NEXP:(i + 1) * NEXP],
                         start=True, stop=True).then_inc(ctx.sem(M), 1)
                if i >= 1:
                    e.wait_ge(ctx.sem(V), bV + VP * (i - 1) + VR + 10)
                e.matmul(ps_tot[0:1, 0:NEXP], ONESCOL[:, :], oh_all[:, i * NEXP:(i + 1) * NEXP],
                         start=True, stop=True).then_inc(ctx.sem(M), 1)
        ctx.bump(M, 3 * NTILE)

        block.gpsimd(s1_gpsimd)
        block.sync(s1_sync)
        block.scalar(s1_scalar)
        block.vector(s1_vector)
        block.tensor(s1_tensor)

        tt32 = sb("u2tt", [32, NEXP], F32)
        offs_sb = sb("u2of", [32, NEXP], F32)
        c1 = sb("u2c1", [32, NEXP], F32)
        ctab_sb = sb("u2ct", [32, NEXP], F32)
        pcnt = sb("u2pc", [1, NEXP], I32)
        pcs = [sb(f"u2s{i}", [1, 48], I32) for i in range(2)]
        pbasef = sb("u2pb", [1, NEXP], F32)
        pb256 = sb("u2p2", [32, NEXP], F32)
        iota32 = sb("u2i32", [32, 1], F32)
        cmpm = sb("u2cm", [32, NEXP], F32)
        ecnt = sb("u2ec", [32, 1], F32)
        etab_i = sb("u2ei", [32, 1], I32)
        invbuf = sb("u2iv", [128, 64], U32)
        ps_off, ps_bc = pa[3], pa[4]
        G2, V2, M2, D2 = "UG2", "UV2", "UM2", "UD2"
        bG2, bV2, bM2, bD2 = ctx.val(G2), ctx.val(V2), ctx.val(M2), ctx.val(D2)
        bD2G = ctx.val("UD2G")

        def s2_gpsimd(e):
            e.iota(iota32[0:32, :], [[0, 1]], base=0, channel_multiplier=1,
                   allow_small_or_imprecise_dtypes=True).then_inc(ctx.sem(G2), 1)
            e.memset(invbuf[:, :], 0xFFFFFFFF).then_inc(ctx.sem(G2), 1)
            e.wait_ge(ctx.sem(G2), bG2 + 2)
            e.dma_start(out=perm_d.rearrange("(p c) o -> p (c o)", p=128), in_=invbuf[:, :]).then_inc(ctx.sem("UD2G"), 16)
            e.wait_ge(ctx.sem(TS), bTS + 16 * NTILE)
            e.dma_start(out=tt32[:, :], in_=ttot_d[:, :]).then_inc(ctx.sem("UD2G"), 16)
        ctx.bump(G2, 2)
        ctx.bump("UD2G", 32)

        def s2_tensor(e):
            e.wait_ge(ctx.sem("UD2G"), bD2G + 32)
            e.wait_ge(ctx.sem(V), bV + VP * NTILE)
            e.matmul(ps_off[0:32, 0:NEXP], TRI32[:, :], tt32[:, :], start=True, stop=True).then_inc(ctx.sem(M2), 1)
            e.matmul(ps_tot[0:1, 0:NEXP], ONES32[:, :], tt32[:, :], start=True, stop=True).then_inc(ctx.sem(M2), 1)
            e.wait_ge(ctx.sem(V2), bV2 + 11)
            e.matmul(ps_bc[0:32, 0:NEXP], ONES1x32[:, :], pbasef[:, :], start=True, stop=True).then_inc(ctx.sem(M2), 1)
        ctx.bump(M2, 3)

        def s2_vector(e):
            vv = [bV2]

            def step(inst):
                inst.then_inc(ctx.sem(V2), 1)
                vv[0] += 1
                e.wait_ge(ctx.sem(V2), vv[0])
            e.wait_ge(ctx.sem(G2), bG2 + 1)
            e.wait_ge(ctx.sem(M2), bM2 + 2)
            step(e.tensor_copy(out=pcnt[:, :], in_=ps_tot[0:1, 0:NEXP]))
            step(e.tensor_scalar(out=pcnt[:, :], in0=pcnt[:, :], scalar1=TT - 1, scalar2=None, op0=ALU.add))
            step(e.tensor_scalar(out=pcnt[:, :], in0=pcnt[:, :], scalar1=-TT, scalar2=None, op0=ALU.bitwise_and))
            step(e.memset(pcs[0][:, :], 0))
            step(e.memset(pcs[1][:, :], 0))
            step(e.tensor_copy(out=pcs[0][:, 16:32], in_=pcnt[:, :]))
            srcb = 0
            for sh in (1, 2, 4, 8):
                step(e.tensor_tensor(out=pcs[1 - srcb][:, 16:32], in0=pcs[srcb][:, 16:32],
                                     in1=pcs[srcb][:, 16 - sh:32 - sh], op=ALU.add))
                srcb = 1 - srcb
            step(e.tensor_copy(out=pbasef[:, :], in_=pcs[srcb][:, 15:31]))
            e.wait_ge(ctx.sem(M2), bM2 + 1)
            step(e.tensor_copy(out=offs_sb[:, :], in_=ps_off[0:32, 0:NEXP]))
            e.wait_ge(ctx.sem(M2), bM2 + 3)
            step(e.tensor_tensor(out=c1[:, :], in0=offs_sb[:, :], in1=ps_bc[0:32, 0:NEXP], op=ALU.add))
            step(e.tensor_scalar(out=ctab_sb[:, :], in0=c1[:, :], scalar1=-1.0, scalar2=None, op0=ALU.add))
            step(e.tensor_scalar(out=pb256[:, :], in0=ps_bc[0:32, 0:NEXP], scalar1=1.0 / TT,
                                 scalar2=None, op0=ALU.mult))
            step(e.tensor_scalar(out=cmpm[:, :], in0=pb256[:, :], scalar1=iota32[:, :],
                                 scalar2=None, op0=ALU.is_le))
            step(e.reduce_sum(ecnt[:, :], cmpm[:, :], axis=AX.X))
            step(e.tensor_scalar(out=ecnt[:, :], in0=ecnt[:, :], scalar1=-1.0, scalar2=None, op0=ALU.add))
            step(e.tensor_copy(out=etab_i[:, :], in_=ecnt[:, :]))
        ctx.bump(V2, 19)

        def s2_sync(e):
            e.wait_ge(ctx.sem(V2), bV2 + 19)
            e.dma_start(out=ctab_d[:, :], in_=ctab_sb[:, :]).then_inc(ctx.sem(D2), 16)
            e.dma_start(out=etab_d.rearrange("a b -> b a"), in_=etab_i[0:32, :],
                        allow_slow_non_contiguous=True).then_inc(ctx.sem(D2), 16)
        ctx.bump(D2, 32)

        block.gpsimd(s2_gpsimd)
        block.tensor(s2_tensor)
        block.vector(s2_vector)
        block.sync(s2_sync)
        D2_s2 = ctx.val(D2)

        tokid = sb("u2tk", [128, 1], U32)
        crow = sb("u2cr", [1, NEXP], F32)
        cbsel = sb("u2cb", [128, NEXP], F32)
        slotf = sb("u2sf", [128, 1], F32)
        slot_i = sb("u2si", [128, 1], I32)
        slot_u = sb("u2su", [128, 1], U32)
        fa = sb("u2fa", [128, 1], I32)
        fb = sb("u2fb", [128, 1], I32)
        fc = sb("u2fc", [128, 1], U32)
        perm_sb = sb("u2pm", [128, 64], U32)
        etab_sb = sb("u2et", [1, NCT], I32)
        etab_f = sb("u2ef", [1, NCT], F32)
        iota_p = sb("u2ip", [128, 1], F32)
        idx_all = sb("u2ia", [128, NCT], U32)
        idx_f = sb("u2if2", [128, NCT], F32)
        eqr = sb("u2eq", [1, NCT], F32) if use_skip else None
        idx_skp = sb("u2is", [128, NCT], U32) if use_skip else None
        bbV, bbM, bbG, bbD = ctx.val(V), ctx.val(M), ctx.val(G2), ctx.val(D2)
        bbGS = ctx.val("G2S")
        V2B = 10

        def s2b_sync(e):
            for i in range(NTILE):
                e.wait_ge(ctx.sem(D2), D2_s2)
                if i >= 1:
                    e.wait_ge(ctx.sem(M), bbM + i)
                e.dma_start(out=crow[:, :], in_=ctab_d[i:i + 1, :]).then_inc(ctx.sem(D2), 16)
        ctx.bump(D2, 16 * NTILE)

        def s2b_tensor(e):
            for i in range(NTILE):
                e.wait_ge(ctx.sem(D2), bbD + 16 * (i + 1))
                if i >= 1:
                    e.wait_ge(ctx.sem(V), bbV + V2B * (i - 1) + 1)
                e.matmul(ps_bc[:, 0:NEXP], ONES128F[:, :], crow[:, :], start=True, stop=True).then_inc(ctx.sem(M), 1)
        ctx.bump(M, NTILE)

        def s2b_vector(e):
            for i in range(NTILE):
                vb = bbV + V2B * i

                def step(k, inst):
                    inst.then_inc(ctx.sem(V), 1)
                    e.wait_ge(ctx.sem(V), vb + k)
                e.wait_ge(ctx.sem(M), bbM + i + 1)
                if i >= 1:
                    e.wait_ge(ctx.sem("G2S"), bbGS + 32 * i)
                step(1, e.tensor_tensor(out=cbsel[:, :], in0=ps_bc[:, 0:NEXP],
                                        in1=oh_all[:, i * NEXP:(i + 1) * NEXP], op=ALU.mult))
                step(2, e.reduce_sum(slotf[:, :], cbsel[:, :], axis=AX.X))
                step(3, e.tensor_tensor(out=slotf[:, :], in0=slotf[:, :], in1=rank_all[:, i:i + 1], op=ALU.add))
                step(4, e.tensor_copy(out=slot_i[:, :], in_=slotf[:, :]))
                step(5, e.tensor_copy(out=slot_u[:, :], in_=slot_i[:, :]))
                step(6, e.tensor_scalar(out=fa[:, :], in0=slot_i[:, :], scalar1=127, scalar2=None, op0=ALU.bitwise_and))
                step(7, e.tensor_scalar(out=fa[:, :], in0=fa[:, :], scalar1=64, scalar2=None, op0=ALU.mult))
                step(8, e.tensor_scalar(out=fb[:, :], in0=slot_i[:, :], scalar1=7, scalar2=None, op0=ALU.arith_shift_right))
                step(9, e.tensor_tensor(out=fb[:, :], in0=fa[:, :], in1=fb[:, :], op=ALU.add))
                step(10, e.tensor_copy(out=fc[:, :], in_=fb[:, :]))
        ctx.bump(V, V2B * NTILE)

        def s2b_gpsimd(e):
            for i in range(NTILE):
                e.wait_ge(ctx.sem(V), bbV + V2B * i + V2B)
                if i >= 1:
                    e.wait_ge(ctx.sem("G2S"), bbGS + 32 * i)
                e.iota(tokid[:, :], [[0, 1]], base=128 * i, channel_multiplier=1,
                       allow_small_or_imprecise_dtypes=True).then_inc(ctx.sem(G2), 1)
                e.wait_ge(ctx.sem(G2), bbG + i + 1)
                e.indirect_dma_start(out=perm_d[:, :],
                                     out_offset=IndirectOffsetOnAxis(fc[:, :], 0),
                                     in_=tokid[:, :], in_offset=None).then_inc(ctx.sem("G2S"), 16)
                e.indirect_dma_start(out=wslot_d[:, :],
                                     out_offset=IndirectOffsetOnAxis(slot_u[:, :], 0),
                                     in_=w_all[:, i:i + 1], in_offset=None).then_inc(ctx.sem("G2S"), 16)
            e.wait_ge(ctx.sem("G2S"), bbGS + 32 * NTILE)
            e.wait_ge(ctx.sem(D2), D2_s2)
            e.dma_start(out=perm_sb[:, :], in_=perm_d.rearrange("(p c) o -> p (c o)", p=128)).then_inc(ctx.sem("G2S"), 16)
            e.dma_start(out=etab_sb[:, :], in_=etab_d[:, :]).then_inc(ctx.sem("G2S"), 16)
            e.iota(iota_p[:, :], [[0, 1]], base=0, channel_multiplier=1,
                   allow_small_or_imprecise_dtypes=True).then_inc(ctx.sem(G2), 1)
            e.wait_ge(ctx.sem("G2S"), bbGS + 32 * NTILE + 32)
        ctx.bump(G2, NTILE + 1)
        ctx.bump("G2S", 32 * NTILE + 32)

        block.sync(s2b_sync)
        block.tensor(s2b_tensor)
        block.vector(s2b_vector)
        block.gpsimd(s2b_gpsimd)

        ps_eb = pa[7]
        bIV, bIM = ctx.val(V), ctx.val(M)

        def s2c_vector(e):
            e.wait_ge(ctx.sem(G2), ctx_G2_final)
            e.wait_ge(ctx.sem("G2S"), ctx_G2S_final)
            e.tensor_copy(out=etab_f[:, :], in_=etab_sb[:, :]).then_inc(ctx.sem(V), 1)
            if use_skip:
                e.memset(eqr[:, :], 0.0).then_inc(ctx.sem(V), 1)
                e.wait_ge(ctx.sem(V), bIV + 2)
                e.tensor_tensor(out=eqr[:, 1:NCT], in0=etab_f[:, 1:NCT], in1=etab_f[:, 0:NCT - 1],
                                op=ALU.is_equal).then_inc(ctx.sem(V), 1)
            nv = 3 if use_skip else 1
            e.wait_ge(ctx.sem(M), bIM + 1)
            e.tensor_scalar(out=idx_f[:, :], in0=ps_eb[:, 0:NCT], scalar1=512.0, scalar2=iota_p[:, :],
                            op0=ALU.mult, op1=ALU.add).then_inc(ctx.sem(V), 1)
            e.wait_ge(ctx.sem(V), bIV + nv + 1)
            e.tensor_copy(out=idx_all[:, :], in_=idx_f[:, :]).then_inc(ctx.sem(V), 1)
            e.wait_ge(ctx.sem(V), bIV + nv + 2)
            if use_skip:
                e.wait_ge(ctx.sem(M), bIM + 2)
                e.scalar_tensor_tensor(out=idx_f[:, :], in0=ps_eb[:, 0:NCT], scalar=float(2 ** 14),
                                       in1=idx_f[:, :], op0=ALU.mult, op1=ALU.add).then_inc(ctx.sem(V), 1)
                e.wait_ge(ctx.sem(V), bIV + nv + 3)
                e.tensor_copy(out=idx_skp[:, :], in_=idx_f[:, :]).then_inc(ctx.sem(V), 1)
                e.wait_ge(ctx.sem(V), bIV + nv + 4)

        def s2c_tensor(e):
            e.wait_ge(ctx.sem(V), bIV + 1)
            e.matmul(ps_eb[:, 0:NCT], ONES128F[:, :], etab_f[:, :], start=True, stop=True).then_inc(ctx.sem(M), 1)
            if use_skip:
                e.wait_ge(ctx.sem(V), bIV + 5)
                e.matmul(ps_eb[:, 0:NCT], ONES128F[:, :], eqr[:, :], start=True, stop=True).then_inc(ctx.sem(M), 1)
        ctx_G2_final = ctx.val(G2)
        ctx_G2S_final = ctx.val("G2S")
        block.vector(s2c_vector)
        block.tensor(s2c_tensor)
        ctx.bump(V, 7 if use_skip else 3)
        ctx.bump(M, 2 if use_skip else 1)

        xg = [sb(f"u3x{i}", [128, 2 * D], BF16) for i in range(2)]
        rg = [sb(f"u3r{i}", [128, 2 * D], F32) for i in range(2)] if do_res else None
        xT = [sb(f"u3t{i}", [128, 8 * TT], BF16) for i in range(2)]
        NWB = 4 if use_skip else 2
        wq1 = [sb(f"u3w{i}", [128, 8 * 1024], BF16) for i in range(NWB)]
        wq2 = [sb(f"u3v{i}", [128, 8 * 1024], BF16) for i in range(NWB)]
        hsb = [sb(f"u3h{i}", [128, 8 * TT], BF16) for i in range(2)]
        wrow = sb("u3wr", [1, TT], F32)
        wrowb = sb("u3wb", [1, TT], BF16)
        wbc = sb("u3wc", [128, TT], F32)
        ybf = sb("u3yb", [128, 8 * TT], BF16)
        ytok = [sb(f"u3y{i}", [128, 2 * D], BF16) for i in range(2)]
        og = [sb(f"u3o{i}", [128, 2 * D], F32) for i in range(2)] if do_res else None
        psY = pa[0:4]
        psH = pa[4:6]
        psW = pa[6]

        TI, TO = "T3I", "T3O"
        M1, PQ, PM, SIL = "M31", "M3Q", "M3T", "S3S"
        YV, VF, VC = "Y3V", "V3F", "V3C"
        bTI, bTO = ctx.val(TI), ctx.val(TO)
        bM1, bPQ, bPM, bSIL = ctx.val(M1), ctx.val(PQ), ctx.val(PM), ctx.val(SIL)
        bYV, bVF, bVC = ctx.val(YV), ctx.val(VF), ctx.val(VC)
        idx_ready = ctx.val(V)
        bGX = ctx.val("GXG")
        bGR = [ctx.val("GRG0"), ctx.val("GRG1")]
        bGS = [ctx.val("GSC0"), ctx.val("GSC1")]
        bWG = [ctx.val(f"WG{i}") for i in range(4 if use_skip else 2)]

        def s3_gpsimd(e):
          with e.register("bc" + tag) as bcreg, e.register("b2" + tag) as bc2:
            e.reg_mov(bcreg, NTOK - 1)
            e.reg_mov(bc2, NEXP * 512 - 1)
            e.wait_ge(ctx.sem(V), idx_ready)
            if do_ln:
                e.wait_ge(ctx.sem("UXS0"), bXS[0] + 16 * (NTILE // 2))
                e.wait_ge(ctx.sem("UXS1"), bXS[1] + 16 * (NTILE // 2))
            for t in range(NCT + 1):
                if t < NCT:
                    if t >= 2:
                        e.wait_ge(ctx.sem(TI), bTI + 16 * 2 * (t - 1))
                        if do_res:
                            e.wait_ge(ctx.sem(VF), bVF + (t - 1))
                    elif t == 1 and use_skip:
                        e.wait_ge(ctx.sem("GXG"), bGX + 32)
                    for g in range(2):
                        e.indirect_dma_start(out=xg[t % 2][:, g * D:(g + 1) * D], out_offset=None,
                                             in_=zbuf[:, :],
                                             in_offset=IndirectOffsetOnAxis(perm_sb[:, 2 * t + g:2 * t + g + 1], 0),
                                             bounds_check=bcreg, oob_is_err=False).then_inc(ctx.sem("GXG"), 16)
                    if do_res:
                        for g in range(2):
                            e.indirect_dma_start(out=rg[t % 2][:, g * D:(g + 1) * D], out_offset=None,
                                                 in_=src[:, :],
                                                 in_offset=IndirectOffsetOnAxis(perm_sb[:, 2 * t + g:2 * t + g + 1], 0),
                                                 bounds_check=bcreg, oob_is_err=False).then_inc(ctx.sem(f"GRG{t % 2}"), 16)
                    for q in range(4):
                        qg = 4 * t + q
                        if use_skip:
                            buf = q
                            wsem = f"WG{q}"
                            if t >= 1:
                                e.wait_ge(ctx.sem(PQ), bPQ + 4 * (t - 1) + q + 1)
                            idxsrc = idx_skp
                            kw = dict(bounds_check=bc2, oob_is_err=False)
                        else:
                            buf = qg % 2
                            wsem = f"WG{qg % 2}"
                            if qg >= 2:
                                e.wait_ge(ctx.sem(PQ), bPQ + qg - 1)
                            idxsrc = idx_all
                            kw = {}
                        e.indirect_dma_start(out=wq1[buf][:, :], out_offset=None,
                                             in_=w1pre[:, :],
                                             in_offset=IndirectOffsetOnAxis(idxsrc[:, t:t + 1], 0),
                                             element_offset=q * 128 * 8192, **kw).then_inc(ctx.sem(wsem), 16)
                        e.indirect_dma_start(out=wq2[buf][:, :], out_offset=None,
                                             in_=w2pre[:, :],
                                             in_offset=IndirectOffsetOnAxis(idxsrc[:, t:t + 1], 0),
                                             element_offset=q * 128 * 8192, **kw).then_inc(ctx.sem(wsem), 16)
                if t >= 1:
                    if do_res:
                        e.wait_ge(ctx.sem(VF), bVF + t)
                    else:
                        e.wait_ge(ctx.sem(TO), bTO + 16 * t)
                    for g in range(2):
                        sap = og[(t - 1) % 2][:, g * D:(g + 1) * D] if do_res else \
                            ytok[(t - 1) % 2][:, g * D:(g + 1) * D]
                        dap = dst[:, :] if do_res else zbuf[:, :]
                        e.indirect_dma_start(out=dap,
                                             out_offset=IndirectOffsetOnAxis(perm_sb[:, 2 * (t - 1) + g:2 * (t - 1) + g + 1], 0),
                                             in_=sap, in_offset=None,
                                             bounds_check=bcreg, oob_is_err=False).then_inc(ctx.sem(f"GSC{(t - 1) % 2}"), 16)
            e.wait_ge(ctx.sem("GXG"), bGX + 16 * 2 * NCT)
            for par in range(2):
                if do_res:
                    e.wait_ge(ctx.sem(f"GRG{par}"), bGR[par] + 32 * (NCT // 2))
                e.wait_ge(ctx.sem(f"GSC{par}"), bGS[par] + 32 * (NCT // 2))
            if use_skip:
                for q in range(4):
                    e.wait_ge(ctx.sem(f"WG{q}"), bWG[q] + 32 * NCT)
            else:
                for par in range(2):
                    e.wait_ge(ctx.sem(f"WG{par}"), bWG[par] + 16 * 4 * NCT)
        ctx.bump("GXG", 16 * 2 * NCT)
        if do_res:
            ctx.bump("GRG0", 32 * (NCT // 2))
            ctx.bump("GRG1", 32 * (NCT // 2))
        ctx.bump("GSC0", 32 * (NCT // 2))
        ctx.bump("GSC1", 32 * (NCT // 2))
        if use_skip:
            for q in range(4):
                ctx.bump(f"WG{q}", 32 * NCT)
        else:
            ctx.bump("WG0", 16 * 4 * NCT)
            ctx.bump("WG1", 16 * 4 * NCT)

        def s3_sync(e):
            if True:
                for t in range(NCT + 1):
                    if t < NCT:
                        e.wait_ge(ctx.sem("GXG"), bGX + 16 * 2 * (t + 1))
                        if t >= 2:
                            e.wait_ge(ctx.sem(PM), bPM + (t - 1))
                        e.wait_ge(ctx.sem("G2S"), bbGS + 32 * NTILE)
                        if t >= 1:
                            e.wait_ge(ctx.sem(VC), bVC + 2 * (t - 1) + 1)
                        e.dma_start(out=wrow[:, :],
                                    in_=wslot_d[t * TT:(t + 1) * TT, :].rearrange("a b -> b a")).then_inc(ctx.sem(TI), 16)
                        e.dma_start_transpose(out=b3(xT[t % 2][:, :]),
                                              in_=b3(xg[t % 2][:, :])).then_inc(ctx.sem(TI), 16)
                    if t >= 1:
                        e.wait_ge(ctx.sem(YV), bYV + t)
                        if t >= 3:
                            if do_res:
                                e.wait_ge(ctx.sem(VF), bVF + (t - 2))
                            else:
                                e.wait_ge(ctx.sem(f"GSC{(t - 3) % 2}"), bGS[(t - 3) % 2] + 32 * ((t - 3) // 2 + 1))
                        e.dma_start_transpose(out=b3(ytok[(t - 1) % 2][:, :]),
                                              in_=b3(ybf[:, :])).then_inc(ctx.sem(TO), 16)
        ctx.bump(TI, 16 * 2 * NCT)
        ctx.bump(TO, 16 * NCT)

        def s3_tensor(e):
            for t in range(NCT):
                e.wait_ge(ctx.sem(TI), bTI + 16 * 2 * (t + 1))
                e.wait_ge(ctx.sem(YV), bYV + t)
                for q in range(4):
                    qg = 4 * t + q
                    buf = q if use_skip else qg % 2
                    if use_skip:
                        e.wait_ge(ctx.sem(f"WG{q}"), bWG[q] + 32 * (t + 1))
                    else:
                        e.wait_ge(ctx.sem(f"WG{qg % 2}"), bWG[qg % 2] + 16 * 2 * (qg // 2 + 1))
                    for hb in range(8):
                        gidx = qg * 8 + hb
                        if gidx >= 2:
                            e.wait_ge(ctx.sem(SIL), bSIL + gidx - 1)
                        bank = psH[gidx % 2]
                        for kb in range(8):
                            inst = e.matmul(bank[:, 0:TT],
                                            wq1[buf][:, kb * 1024 + hb * 128:kb * 1024 + (hb + 1) * 128],
                                            xT[t % 2][:, :].rearrange("p (g c) -> p g c", c=D)[:, :, kb * 128:(kb + 1) * 128],
                                            start=(kb == 0), stop=(kb == 7))
                        inst.then_inc(ctx.sem(M1), 1)
                    e.wait_ge(ctx.sem(SIL), bSIL + qg * 8 + 8)
                    for mb in range(8):
                        for hb in range(8):
                            inst = e.matmul(psY[mb // 2][:, (mb % 2) * TT:(mb % 2 + 1) * TT],
                                            wq2[buf][:, hb * 1024 + mb * 128:hb * 1024 + (mb + 1) * 128],
                                            hsb[q % 2][:, hb * TT:(hb + 1) * TT],
                                            start=(q == 0 and hb == 0 and mb % 2 == 0), stop=(q == 3 and hb == 7), skip_group_check=True)
                    inst.then_inc(ctx.sem(PQ), 1)
                e.wait_ge(ctx.sem(VC), bVC + 2 * t + 1)
                e.matmul(psW[:, 0:TT], ONES128B[:, :], wrowb[:, :], start=True, stop=True).then_inc(ctx.sem(PM), 1)
        ctx.bump(M1, 32 * NCT)
        ctx.bump(PQ, 4 * NCT)
        ctx.bump(PM, NCT)

        def s3_scalar(e):
            for t in range(NCT):
                for q in range(4):
                    qg = 4 * t + q
                    if qg >= 2:
                        e.wait_ge(ctx.sem(PQ), bPQ + qg - 1)
                    for hb in range(8):
                        gidx = qg * 8 + hb
                        e.wait_ge(ctx.sem(M1), bM1 + gidx + 1)
                        e.activation(out=hsb[q % 2][:, hb * TT:(hb + 1) * TT],
                                     in_=psH[gidx % 2][:, 0:TT], func=AF.Silu).then_inc(ctx.sem(SIL), 1)
        ctx.bump(SIL, 32 * NCT)

        def s3_vector(e):
            for t in range(NCT):
                e.wait_ge(ctx.sem(TI), bTI + 16 * 2 * (t + 1))
                e.tensor_copy(out=wrowb[:, :], in_=wrow[:, :]).then_inc(ctx.sem(VC), 1)
                e.wait_ge(ctx.sem(VC), bVC + 2 * t + 1)
                e.wait_ge(ctx.sem(PM), bPM + t + 1)
                if t >= 1:
                    e.wait_ge(ctx.sem(TO), bTO + 16 * t)
                e.tensor_copy(out=wbc[:, :], in_=psW[:, 0:TT]).then_inc(ctx.sem(VC), 1)
                e.wait_ge(ctx.sem(VC), bVC + 2 * t + 2)
                for mb in range(8):
                    inst = e.tensor_tensor(out=ybf[:, :].rearrange("p (g c) -> p g c", c=D)[:, :, mb * 128:(mb + 1) * 128],
                                           in0=psY[mb // 2][:, (mb % 2) * TT:(mb % 2 + 1) * TT],
                                           in1=wbc[:, :], op=ALU.mult)
                inst.then_inc(ctx.sem(YV), 1)
                if do_res:
                    e.wait_ge(ctx.sem(TO), bTO + 16 * (t + 1))
                    e.wait_ge(ctx.sem(f"GRG{t % 2}"), bGR[t % 2] + 32 * (t // 2 + 1))
                    if t >= 2:
                        e.wait_ge(ctx.sem(f"GSC{t % 2}"), bGS[t % 2] + 32 * ((t - 2) // 2 + 1))
                    for g in range(2):
                        inst = e.tensor_tensor(out=og[t % 2][:, g * D:(g + 1) * D],
                                               in0=rg[t % 2][:, g * D:(g + 1) * D],
                                               in1=ytok[t % 2][:, g * D:(g + 1) * D], op=ALU.add)
                    inst.then_inc(ctx.sem(VF), 1)
        ctx.bump(VC, 2 * NCT)
        ctx.bump(YV, NCT)
        if do_res:
            ctx.bump(VF, NCT)

        block.gpsimd(s3_gpsimd)
        block.sync(s3_sync)
        block.tensor(s3_tensor)
        block.scalar(s3_scalar)
        block.vector(s3_vector)






def b3(ap):
    return ap.rearrange("p (k c) -> p k c", c=128)


def tmslice(xT_sb, kb, c0, w):
    return xT_sb[:, :].rearrange("p (i c) -> p i c", c=D)[:, c0 // 128:(c0 + w) // 128, kb * 128:(kb + 1) * 128]

def emit_lnT(ctx, block, tag, src_rows, row0, ntile, xT_sb, col_of, lnrow_extra=None):
    nc = ctx.nc
    with ExitStack() as es:
        sb = lambda n, s, d: sbuf(ctx, n + tag, s, d, es)
        xrow = [sb(f"lt{i}", [128, D], F32) for i in range(2)]
        xc = [sb(f"lc{i}", [128, D], F32) for i in range(2)]
        sq = sb("lsq", [128, D], F32)
        lnrow = [sb(f"ll{i}", [128, D], BF16) for i in range(2)]
        ssum = sb("lsm", [128, 1], F32)
        negmean = sb("lnm", [128, 1], F32)
        varD = sb("lvd", [128, 1], F32)
        lnv = sb("llv", [128, 1], F32)
        rstd = sb("lrs", [128, 1], F32)
        V, S = "LTV", "LTS"
        bV, bS = ctx.val(V), ctx.val(S)
        bLD = [ctx.val("LTL0"), ctx.val("LTL1")]
        bTP = [ctx.val("LTT0"), ctx.val("LTT1")]
        bXS = [ctx.val("LTX0"), ctx.val("LTX1")]

        def f_sync(e):
            for i in range(ntile):
                if i >= 2:
                    e.wait_ge(ctx.sem(V), bV + 5 * (i - 2) + 3)
                e.dma_start(out=xrow[i % 2][:, :], in_=src_rows[row0 + i * 128:row0 + (i + 1) * 128, :]).then_inc(ctx.sem(f"LTL{i % 2}"), 16)
                e.wait_ge(ctx.sem(V), bV + 5 * i + 5)
                e.dma_start_transpose(out=b3(xT_sb[:, i * D:(i + 1) * D]),
                                      in_=b3(lnrow[i % 2][:, :])).then_inc(ctx.sem(f"LTT{i % 2}"), 16)
        ctx.bump("LTL0", 16 * ((ntile + 1) // 2))
        ctx.bump("LTL1", 16 * (ntile // 2))
        ctx.bump("LTT0", 16 * ((ntile + 1) // 2))
        ctx.bump("LTT1", 16 * (ntile // 2))

        def f_scalar(e):
            for i in range(ntile):
                e.wait_ge(ctx.sem(V), bV + 5 * i + 1)
                e.activation(out=negmean[:], in_=ssum[:], func=AF.Copy, scale=-1.0 / D).then_inc(ctx.sem(S), 1)
                e.wait_ge(ctx.sem(V), bV + 5 * i + 4)
                e.activation(out=lnv[:], in_=varD[:], func=AF.Ln, bias=P_EPS[0][:, :], scale=1.0 / D).then_inc(ctx.sem(S), 1)
                e.wait_ge(ctx.sem(S), bS + 3 * i + 2)
                e.activation(out=rstd[:], in_=lnv[:], func=AF.Exp, scale=-0.5).then_inc(ctx.sem(S), 1)
        ctx.bump(S, 3 * ntile)

        def f_vector(e):
            for i in range(ntile):
                b = i % 2
                vb = bV + 5 * i

                def step(k, inst):
                    inst.then_inc(ctx.sem(V), 1)
                    e.wait_ge(ctx.sem(V), vb + k)
                e.wait_ge(ctx.sem(f"LTL{i % 2}"), bLD[i % 2] + 16 * (i // 2 + 1))
                if i >= 1:
                    e.wait_ge(ctx.sem(S), bS + 3 * (i - 1) + 3)
                step(1, e.reduce_sum(ssum[:], xrow[b][:, :], axis=AX.X))
                e.wait_ge(ctx.sem(S), bS + 3 * i + 1)
                step(2, e.tensor_scalar(out=xc[b][:, :], in0=xrow[b][:, :], scalar1=negmean[:],
                                        scalar2=None, op0=ALU.add))
                step(3, e.tensor_tensor(out=sq[:, :], in0=xc[b][:, :], in1=xc[b][:, :], op=ALU.mult))
                step(4, e.reduce_sum(varD[:], sq[:, :], axis=AX.X))
                e.wait_ge(ctx.sem(S), bS + 3 * i + 3)
                if i >= 2:
                    e.wait_ge(ctx.sem(f"LTT{i % 2}"), bTP[i % 2] + 16 * ((i - 2) // 2 + 1))
                    if lnrow_extra is not None:
                        e.wait_ge(ctx.sem(f"LTX{i % 2}"), bXS[i % 2] + 16 * ((i - 2) // 2 + 1))
                step(5, e.tensor_scalar(out=lnrow[b][:, :], in0=xc[b][:, :], scalar1=rstd[:],
                                        scalar2=None, op0=ALU.mult))
        ctx.bump(V, 5 * ntile)

        if lnrow_extra is not None:
            dap, drow0 = lnrow_extra

            def f_gpsimd(e):
                for i in range(ntile):
                    e.wait_ge(ctx.sem(V), bV + 5 * i + 5)
                    e.dma_start(out=dap[drow0 + i * 128:drow0 + (i + 1) * 128, :],
                                in_=lnrow[i % 2][:, :]).then_inc(ctx.sem(f"LTX{i % 2}"), 16)
            ctx.bump("LTX0", 16 * ((ntile + 1) // 2))
            ctx.bump("LTX1", 16 * (ntile // 2))
            block.gpsimd(f_gpsimd)

        block.sync(f_sync)
        block.scalar(f_scalar)
        block.vector(f_vector)
    return (ctx.val("LTT0"), ctx.val("LTT1"), ctx.val(V))


P_EPS = [None]


def emit_attn(ctx, block, P, tag):
    nc = ctx.nc
    resid = P["resid"].ap()
    AMASK = P["amask_sb"]
    IDENT = None
    pa = P["psum"]
    for b in range(NB):
        btag = f"{tag}b{b}"
        with ExitStack() as es:
            sb = lambda n, s, d: sbuf(ctx, n + btag, s, d, es)
            QT = sb("aQ", [128, 8 * SEQ], BF16)
            KT = sb("aK", [128, 8 * SEQ], BF16)
            VTOK = sb("aV", [128, 16 * 1024], BF16)
            AVT = sb("aA", [128, 8 * SEQ], BF16)
            wt = sb("aW", [128, 8 * 1024], BF16)
            es3 = es.enter_context(ExitStack())
            sb3 = lambda n, s, d: sbuf(ctx, n + btag, s, d, es3)
            row0 = b * SEQ
            with ExitStack() as es2:
                lnxT = sbuf(ctx, "aL" + btag, [128, 8 * SEQ], BF16, es2)
                tp0, tp1, _ = emit_lnT(ctx, block, "al" + btag, resid, row0, SEQ // 128,
                                       lnxT, lambda i, db: db * SEQ + i * 128)

                WT, PM2, EV2 = "AWT", "APM", "AEV"
                bWT, bPM2, bEV2 = ctx.val(WT), ctx.val(PM2), ctx.val(EV2)
                wnames = [P["wq_bf"].ap(), P["wk_bf"].ap(), P["wv_bf"].ap()]

                def ph2_sync(e):
                    for m in range(3):
                        if m >= 1:
                            e.wait_ge(ctx.sem(PM2), bPM2 + m * 32)
                        for jb in range(8):
                            e.dma_start_transpose(out=b3(wt[:, jb * 1024:(jb + 1) * 1024]),
                                                  in_=b3(wnames[m][jb * 128:(jb + 1) * 128, :])).then_inc(ctx.sem(WT), 16)
                ctx.bump(WT, 16 * 8 * 3)

                def ph2_tensor(e):
                    e.wait_ge(ctx.sem("LTT0"), tp0)
                    e.wait_ge(ctx.sem("LTT1"), tp1)
                    for m in range(3):
                        e.wait_ge(ctx.sem(WT), bWT + 16 * 8 * (m + 1))
                        if m < 2:
                            for jb in range(8):
                                for c in range(4):
                                    g = m * 32 + jb * 4 + c
                                    if g >= 2:
                                        e.wait_ge(ctx.sem(EV2), bEV2 + g - 1)
                                    bank = pa[g % 2]
                                    for kb in range(8):
                                        inst = e.matmul(bank[:, 0:512],
                                                        wt[:, jb * 1024 + kb * 128:jb * 1024 + (kb + 1) * 128],
                                                        tmslice(lnxT, kb, c * 512, 512),
                                                        start=(kb == 0), stop=(kb == 7))
                                    inst.then_inc(ctx.sem(PM2), 1)
                        else:
                            for i in range(16):
                                for jh in range(2):
                                    g = 64 + i * 2 + jh
                                    if g >= 2:
                                        e.wait_ge(ctx.sem(EV2), bEV2 + g - 1)
                                    bank = pa[g % 2]
                                    for kb in range(8):
                                        inst = e.matmul(bank[:, 0:512],
                                                        lnxT[:, i * D + kb * 128:i * D + (kb + 1) * 128],
                                                        wt[:, :].rearrange("p (j c) -> p j c", c=1024)[:, 4 * jh:4 * (jh + 1), kb * 128:(kb + 1) * 128],
                                                        start=(kb == 0), stop=(kb == 7))
                                    inst.then_inc(ctx.sem(PM2), 1)
                ctx.bump(PM2, 96)

                def ph2_vector(e):
                    for g in range(96):
                        e.wait_ge(ctx.sem(PM2), bPM2 + g + 1)
                        if g < 32:
                            jb, c = g // 4, g % 4
                            inst = e.tensor_copy(out=QT[:, jb * SEQ + c * 512:jb * SEQ + (c + 1) * 512],
                                                 in_=pa[g % 2][:, 0:512])
                        elif g < 64:
                            jb, c = (g - 32) // 4, (g - 32) % 4
                            inst = e.tensor_copy(out=KT[:, jb * SEQ + c * 512:jb * SEQ + (c + 1) * 512],
                                                 in_=pa[g % 2][:, 0:512])
                        else:
                            i, jh = (g - 64) // 2, (g - 64) % 2
                            inst = e.tensor_copy(out=VTOK[:, i * 1024 + jh * 512:i * 1024 + (jh + 1) * 512],
                                                 in_=pa[g % 2][:, 0:512])
                        inst.then_inc(ctx.sem(EV2), 1)
                ctx.bump(EV2, 96)

                block.sync(ph2_sync)
                block.tensor(ph2_tensor)
                block.vector(ph2_vector)

            S_sb = [sb3(f"aS{i}", [128, 640], F32) for i in range(2)]
            Pexp = [sb3(f"aP{i}", [128, 640], F32) for i in range(2)]
            Pbf = [sb3(f"aB{i}", [128, 640], BF16) for i in range(2)]
            PT = [sb3(f"aT{i}", [128, 5 * 128], BF16) for i in range(3)]
            nmx = [sb3(f"aM{i}", [128, 1], F32) for i in range(2)]
            sexp = [sb3(f"aX{i}", [128, 1], F32) for i in range(2)]
            rv = [sb3(f"aR{i}", [128, 1], F32) for i in range(2)]
            orow = [sb3(f"aO{i}", [128, D], F32) for i in range(2)]
            obf = sb3("aF", [128, 8 * 512], BF16)
            otk = [sb3(f"aU{i}", [128, D], BF16) for i in range(2)]
            pairs = []
            for h in range(HEADS):
                for qb in range(SEQ // 128):
                    pairs.append((h, qb, min(qb + 1, 5)))
            NP = len(pairs)
            SA = [pa[0], pa[1], pa[4]]
            SBB = [pa[2], pa[3], pa[5]]
            AVB = [pa[6], pa[7]]
            LEAD = 6
            PE3, DV3, AC3 = "A3P", "A3D", "A3A"
            bPE3, bDV3, bAC3 = ctx.val(PE3), ctx.val(DV3), ctx.val(AC3)
            bTP3 = [ctx.val("A3X0"), ctx.val("A3X1")]
            dv_mask, dv_rowmax, dv_recip = [0] * NP, [0] * NP, [0] * NP
            c = 0
            for p in range(NP):
                c += (2 if pairs[p][2] == 5 else 1); dv_mask[p] = c
                c += 1; dv_rowmax[p] = c
                if p >= 1:
                    c += 1; dv_recip[p - 1] = c
            c += 1; dv_recip[NP - 1] = c
            DV3_total = c
            ac_exp, ac_pbf, ac_evac = [0] * NP, [0] * NP, [0] * NP
            c = 0
            for p in range(NP):
                c += 1; ac_exp[p] = c
                if p >= 1:
                    c += 1; ac_pbf[p - 1] = c
                if p >= 2:
                    c += 1; ac_evac[p - 2] = c
            c += 1; ac_pbf[NP - 1] = c
            c += 1; ac_evac[NP - 2] = c
            c += 1; ac_evac[NP - 1] = c
            AC3_total = c
            pe_sc, pe_av = [0] * NP, [0] * NP
            c = 0
            for q in range(min(LEAD, NP)):
                c += 1; pe_sc[q] = c
            for p in range(NP):
                c += 1; pe_av[p] = c
                if p + LEAD < NP:
                    c += 1; pe_sc[p + LEAD] = c
            PE3_total = c

            def ph3_tensor(e):
                e.wait_ge(ctx.sem(EV2), bEV2 + 96)

                def scores(q):
                    h, qb, nk = pairs[q]
                    hb2, hoff = h // 2, (h % 2) * 64
                    kb0 = qb - nk + 1
                    if q >= 3:
                        e.wait_ge(ctx.sem(DV3), bDV3 + dv_mask[q - 3])
                    q_ap = QT[hoff:hoff + 64, hb2 * SEQ + qb * 128:hb2 * SEQ + (qb + 1) * 128]
                    if nk == 5:
                        e.matmul(SA[q % 3][:, 0:512], q_ap,
                                 KT[hoff:hoff + 64, hb2 * SEQ + kb0 * 128:hb2 * SEQ + (kb0 + 4) * 128],
                                 start=True, stop=True)
                        e.matmul(SBB[q % 3][:, 0:128], q_ap,
                                 KT[hoff:hoff + 64, hb2 * SEQ + (kb0 + 4) * 128:hb2 * SEQ + (kb0 + 5) * 128],
                                 start=True, stop=True).then_inc(ctx.sem(PE3), 1)
                    else:
                        e.matmul(SA[q % 3][:, 0:128 * nk], q_ap,
                                 KT[hoff:hoff + 64, hb2 * SEQ + kb0 * 128:hb2 * SEQ + (kb0 + nk) * 128],
                                 start=True, stop=True).then_inc(ctx.sem(PE3), 1)
                for q in range(min(LEAD, NP)):
                    scores(q)
                for p, (h, qb, nk) in enumerate(pairs):
                    kb0 = qb - nk + 1
                    e.wait_ge(ctx.sem(f"A3X{p % 2}"), bTP3[p % 2] + 16 * (p // 2 + 1))
                    if p >= 2:
                        e.wait_ge(ctx.sem(AC3), bAC3 + ac_evac[p - 2])
                    for t in range(nk):
                        inst = e.matmul(AVB[p % 2][0:64, 0:128],
                                        VTOK[:, (kb0 + t) * 1024 + h * 64:(kb0 + t) * 1024 + (h + 1) * 64],
                                        PT[p % 3][:, t * 128:(t + 1) * 128],
                                        start=(t == 0), stop=(t == nk - 1))
                    inst.then_inc(ctx.sem(PE3), 1)
                    if p + LEAD < NP:
                        scores(p + LEAD)
            ctx.bump(PE3, PE3_total)

            def ph3_sync(e):
                for p, (h, qb, nk) in enumerate(pairs):
                    e.wait_ge(ctx.sem(AC3), bAC3 + ac_pbf[p])
                    if p >= 3:
                        e.wait_ge(ctx.sem(PE3), bPE3 + pe_av[p - 3])
                    e.dma_start_transpose(out=b3(PT[p % 3][:, 0:128 * nk]),
                                          in_=b3(Pbf[p % 2][:, 0:128 * nk])).then_inc(ctx.sem(f"A3X{p % 2}"), 16)
            ctx.bump("A3X0", 16 * ((NP + 1) // 2))
            ctx.bump("A3X1", 16 * (NP // 2))

            def ph3_vector(e):
                for p, (h, qb, nk) in enumerate(pairs):
                    e.wait_ge(ctx.sem(PE3), bPE3 + pe_sc[p])
                    if p >= 2:
                        e.wait_ge(ctx.sem(AC3), bAC3 + ac_exp[p - 2])
                        e.wait_ge(ctx.sem(DV3), bDV3 + dv_rowmax[p - 2])
                    if nk == 5:
                        e.scalar_tensor_tensor(out=S_sb[p % 2][:, 0:512], in0=SA[p % 3][:, 0:512],
                                               scalar=0.125, in1=AMASK[:, 0:512],
                                               op0=ALU.mult, op1=ALU.add).then_inc(ctx.sem(DV3), 1)
                        e.scalar_tensor_tensor(out=S_sb[p % 2][:, 512:640], in0=SBB[p % 3][:, 0:128],
                                               scalar=0.125, in1=AMASK[:, 512:640],
                                               op0=ALU.mult, op1=ALU.add).then_inc(ctx.sem(DV3), 1)
                    else:
                        e.scalar_tensor_tensor(out=S_sb[p % 2][:, 0:128 * nk], in0=SA[p % 3][:, 0:128 * nk],
                                               scalar=0.125, in1=AMASK[:, 640 - 128 * nk:640],
                                               op0=ALU.mult, op1=ALU.add).then_inc(ctx.sem(DV3), 1)
                    e.wait_ge(ctx.sem(DV3), bDV3 + dv_mask[p])
                    e.reduce_max(nmx[p % 2][:], S_sb[p % 2][:, 0:128 * nk], axis=AX.X,
                                 negate=True).then_inc(ctx.sem(DV3), 1)
                    if p >= 1:
                        e.wait_ge(ctx.sem(AC3), bAC3 + ac_exp[p - 1])
                        e.reciprocal(rv[(p - 1) % 2][:], sexp[(p - 1) % 2][:]).then_inc(ctx.sem(DV3), 1)
                e.wait_ge(ctx.sem(AC3), bAC3 + ac_exp[NP - 1])
                e.reciprocal(rv[(NP - 1) % 2][:], sexp[(NP - 1) % 2][:]).then_inc(ctx.sem(DV3), 1)
            ctx.bump(DV3, DV3_total)

            def ph3_scalar(e):
                def pbfcopy(q):
                    nkq = pairs[q][2]
                    e.wait_ge(ctx.sem(DV3), bDV3 + dv_recip[q])
                    if q >= 2:
                        e.wait_ge(ctx.sem(f"A3X{q % 2}"), bTP3[q % 2] + 16 * ((q - 2) // 2 + 1))
                    e.activation(out=Pbf[q % 2][:, 0:128 * nkq], in_=Pexp[q % 2][:, 0:128 * nkq],
                                 func=AF.Copy, scale=rv[q % 2][:]).then_inc(ctx.sem(AC3), 1)

                def avevac(q):
                    h2, qb2, _ = pairs[q]
                    hb2, hoff = h2 // 2, (h2 % 2) * 64
                    e.wait_ge(ctx.sem(PE3), bPE3 + pe_av[q])
                    e.activation(out=AVT[hoff:hoff + 64, hb2 * SEQ + qb2 * 128:hb2 * SEQ + (qb2 + 1) * 128],
                                 in_=AVB[q % 2][0:64, 0:128], func=AF.Copy).then_inc(ctx.sem(AC3), 1)
                for p, (h, qb, nk) in enumerate(pairs):
                    e.wait_ge(ctx.sem(DV3), bDV3 + dv_rowmax[p])
                    if p >= 2:
                        e.wait_ge(ctx.sem(AC3), bAC3 + ac_pbf[p - 2])
                    e.activation(out=Pexp[p % 2][:, 0:128 * nk], in_=S_sb[p % 2][:, 0:128 * nk], func=AF.Exp,
                                 bias=nmx[p % 2][:], scale=1.0, accum_out=sexp[p % 2][:]).then_inc(ctx.sem(AC3), 1)
                    if p >= 1:
                        pbfcopy(p - 1)
                    if p >= 2:
                        avevac(p - 2)
                pbfcopy(NP - 1)
                avevac(NP - 2)
                avevac(NP - 1)
            ctx.bump(AC3, AC3_total)

            block.tensor(ph3_tensor)
            block.sync(ph3_sync)
            block.vector(ph3_vector)
            block.scalar(ph3_scalar)

            WT4, PM4, EV4, TO4, LD4, VA4, ST4 = ("A4W", "A4P", "A4E", "A4T", "A4L", "A4V", "A4S")
            bWT4, bPM4, bEV4, bVA4 = ctx.val(WT4), ctx.val(PM4), ctx.val(EV4), ctx.val(VA4)
            bTO4 = [ctx.val("A4T0"), ctx.val("A4T1")]
            bLD4 = [ctx.val("A4L0"), ctx.val("A4L1")]
            bST4 = [ctx.val("A4S0"), ctx.val("A4S1")]

            def ph4_sync(e):
                e.wait_ge(ctx.sem(PM2), bPM2 + 96)
                for jb in range(8):
                    e.dma_start_transpose(out=b3(wt[:, jb * 1024:(jb + 1) * 1024]),
                                          in_=b3(P["wo_bf"].ap()[jb * 128:(jb + 1) * 128, :])).then_inc(ctx.sem(WT4), 16)
                for c in range(4):
                    for i in range(4):
                        ti = c * 4 + i
                        if ti >= 2:
                            e.wait_ge(ctx.sem(f"A4S{ti % 2}"), bST4[ti % 2] + 16 * ((ti - 2) // 2 + 1))
                        e.dma_start(out=orow[ti % 2][:, :], in_=resid[row0 + c * 512 + i * 128:row0 + c * 512 + (i + 1) * 128, :]).then_inc(ctx.sem(f"A4L{ti % 2}"), 16)
                        e.wait_ge(ctx.sem(EV4), bEV4 + c * 8 + 8)
                        if ti >= 2:
                            e.wait_ge(ctx.sem(VA4), bVA4 + ti - 1)
                        e.dma_start_transpose(out=b3(otk[ti % 2][:, :]),
                                              in_=b3(obf[:, i * 1024:(i + 1) * 1024])).then_inc(ctx.sem(f"A4T{ti % 2}"), 16)
            ctx.bump(WT4, 16 * 8)
            ctx.bump("A4L0", 16 * 8)
            ctx.bump("A4L1", 16 * 8)
            ctx.bump("A4T0", 16 * 8)
            ctx.bump("A4T1", 16 * 8)

            def ph4_tensor(e):
                e.wait_ge(ctx.sem(WT4), bWT4 + 16 * 8)
                e.wait_ge(ctx.sem(AC3), bAC3 + AC3_total)
                for c in range(4):
                    for jb in range(8):
                        g = c * 8 + jb
                        if g >= 2:
                            e.wait_ge(ctx.sem(EV4), bEV4 + g - 1)
                        bank = pa[g % 2]
                        for kb in range(8):
                            inst = e.matmul(bank[:, 0:512],
                                            wt[:, jb * 1024 + kb * 128:jb * 1024 + (kb + 1) * 128],
                                            AVT[:, kb * SEQ + c * 512:kb * SEQ + (c + 1) * 512],
                                            start=(kb == 0), stop=(kb == 7))
                        inst.then_inc(ctx.sem(PM4), 1)
            ctx.bump(PM4, 32)

            def ph4_vector(e):
                for c in range(4):
                    for jb in range(8):
                        g = c * 8 + jb
                        e.wait_ge(ctx.sem(PM4), bPM4 + g + 1)
                        if c >= 1:
                            e.wait_ge(ctx.sem("A4T0"), bTO4[0] + 16 * 2 * c)
                            e.wait_ge(ctx.sem("A4T1"), bTO4[1] + 16 * 2 * c)
                        e.tensor_copy(out=obf[:, :].rearrange("p (i c) -> p i c", c=1024)[:, :, jb * 128:(jb + 1) * 128],
                                      in_=pa[g % 2][:, 0:512]).then_inc(ctx.sem(EV4), 1)
                    for i in range(4):
                        ti = c * 4 + i
                        e.wait_ge(ctx.sem(f"A4T{ti % 2}"), bTO4[ti % 2] + 16 * (ti // 2 + 1))
                        e.wait_ge(ctx.sem(f"A4L{ti % 2}"), bLD4[ti % 2] + 16 * (ti // 2 + 1))
                        e.tensor_tensor(out=orow[ti % 2][:, :], in0=orow[ti % 2][:, :],
                                        in1=otk[ti % 2][:, :], op=ALU.add).then_inc(ctx.sem(VA4), 1)
            ctx.bump(EV4, 32)
            ctx.bump(VA4, 16)

            def ph4_gpsimd(e):
                for ti in range(16):
                    e.wait_ge(ctx.sem(VA4), bVA4 + ti + 1)
                    e.dma_start(out=resid[row0 + ti * 128:row0 + (ti + 1) * 128, :],
                                in_=orow[ti % 2][:, :]).then_inc(ctx.sem(f"A4S{ti % 2}"), 16)
                e.wait_ge(ctx.sem("A4S0"), bST4[0] + 16 * 8)
                e.wait_ge(ctx.sem("A4S1"), bST4[1] + 16 * 8)
            ctx.bump("A4S0", 16 * 8)
            ctx.bump("A4S1", 16 * 8)

            block.sync(ph4_sync)
            block.tensor(ph4_tensor)
            block.vector(ph4_vector)
            block.gpsimd(ph4_gpsimd)
        barrier(ctx, block)




def emit_transposeT(ctx, block, es, jobs, tag):
    maxr = max(j[0].shape[0] for j in jobs)
    buf = [sbuf(ctx, f"tt{i}{tag}", [128, maxr], BF16, es) for i in range(2)]
    bTL = [ctx.val("TTL0"), ctx.val("TTL1")]
    bTS = [ctx.val("TTS0"), ctx.val("TTS1")]
    blocks = []
    for (s, d) in jobs:
        R, C = s.shape
        for cb in range(C // 128):
            blocks.append((s, d, cb, R))
    n = len(blocks)

    def f_sync(e):
        for i, (s, d, cb, R) in enumerate(blocks):
            if i >= 2:
                e.wait_ge(ctx.sem(f"TTS{i % 2}"), bTS[i % 2] + 16 * ((i - 2) // 2 + 1))
            e.dma_start_transpose(out=buf[i % 2][:, 0:R],
                                  in_=s[:, cb * 128:(cb + 1) * 128]).then_inc(ctx.sem(f"TTL{i % 2}"), 16)
    ctx.bump("TTL0", 16 * ((n + 1) // 2))
    ctx.bump("TTL1", 16 * (n // 2))

    def f_gpsimd(e):
        for i, (s, d, cb, R) in enumerate(blocks):
            e.wait_ge(ctx.sem(f"TTL{i % 2}"), bTL[i % 2] + 16 * (i // 2 + 1))
            e.dma_start(out=d[cb * 128:(cb + 1) * 128, :], in_=buf[i % 2][:, 0:R]).then_inc(ctx.sem(f"TTS{i % 2}"), 16)
        e.wait_ge(ctx.sem("TTS0"), bTS[0] + 16 * ((n + 1) // 2))
        e.wait_ge(ctx.sem("TTS1"), bTS[1] + 16 * (n // 2))
    ctx.bump("TTS0", 16 * ((n + 1) // 2))
    ctx.bump("TTS1", 16 * (n // 2))

    block.sync(f_sync)
    block.gpsimd(f_gpsimd)


def emit_ffn_int8(ctx, block, P, tag):
    nc = ctx.nc
    resid = P["resid"].ap()
    wgT = P["wgT"].ap()
    wuT = P["wuT"].ap()
    wdT = P["wdT"].ap()
    pa = P["psum"]
    CH = 1024
    with ExitStack() as es0:
        sb = lambda n, s, d: sbuf(ctx, n + tag, s, d, es0)
        wdb = sb("fWD", [128, 24 * CH], BF16)
        lnxT2 = [sb(f"fL{i}", [128, 8 * CH], BF16) for i in range(2)]
        hsb = sb("fH", [128, 24 * CH], BF16)
        wgb = [sb(f"fG{i}", [128, 8 * 128], BF16) for i in range(2)]
        wub = [sb(f"fU{i}", [128, 8 * 128], BF16) for i in range(2)]
        silu_t = sb("fS", [128, CH], F32)
        orow = [sb(f"fO{i}", [128, D], F32) for i in range(2)]
        bDL = ctx.val("FDL")

        def fw_sync(e):
            for kb in range(24):
                e.dma_start(out=wdb[:, kb * CH:(kb + 1) * CH],
                            in_=wdT[kb * 128:(kb + 1) * 128, :]).then_inc(ctx.sem("FDL"), 16)
        ctx.bump("FDL", 16 * 24)
        block.sync(fw_sync)

        for ch in range(4):
            lnxT = lnxT2[ch % 2]
            row0 = ch * CH
            PM5, SI5, HM5, EV5 = "FPM", "FSI", "FHM", "FEV"
            bPM5 = ctx.val(PM5)
            if ch >= 1:
                ltv, lts = ctx.val("LTV"), ctx.val("LTS")

                def pre_sync(e, ltv=ltv, lts=lts, bPM5=bPM5, ch=ch):
                    e.wait_ge(ctx.sem("LTV"), ltv)
                    e.wait_ge(ctx.sem("LTS"), lts)
                    if ch >= 2:
                        e.wait_ge(ctx.sem(PM5), bPM5 - 128)
                block.sync(pre_sync)
            tp0, tp1, _ = emit_lnT(ctx, block, f"fl{tag}c{ch}", resid, row0, CH // 128,
                                   lnxT, lambda i, db: db * CH + i * 128)
            bSI5, bHM5, bEV5 = ctx.val(SI5), ctx.val(HM5), ctx.val(EV5)
            bWL = [ctx.val("FWL0"), ctx.val("FWL1")]
            bSO = [ctx.val("FSO0"), ctx.val("FSO1")]
            bDW = [ctx.val("FDW0"), ctx.val("FDW1")]

            def f5_sync(e, ch=ch, bPM5=bPM5, bWL=bWL):
                for jb in range(24):
                    if jb >= 2:
                        e.wait_ge(ctx.sem(PM5), bPM5 + 4 * (jb - 1))
                    elif ch >= 1:
                        e.wait_ge(ctx.sem(PM5), bPM5 - 20 + 4 * jb)
                    for (wsrc, wb) in ((wgT, wgb), (wuT, wub)):
                        e.dma_start(out=wb[jb % 2][:, :].rearrange("p (k c) -> p k c", k=8),
                                    in_=bass.AP(wsrc.tensor, jb * 128,
                                                [[3072, 128], [128 * 3072, 8], [1, 128]])).then_inc(ctx.sem(f"FWL{jb % 2}"), 16)
            ctx.bump("FWL0", 16 * 24)
            ctx.bump("FWL1", 16 * 24)

            def f5_tensor(e, ch=ch, lnxT=lnxT, tp0=tp0, tp1=tp1, bPM5=bPM5, bSI5=bSI5,
                          bHM5=bHM5, bEV5=bEV5, bWL=bWL):
                e.wait_ge(ctx.sem("LTT0"), tp0)
                e.wait_ge(ctx.sem("LTT1"), tp1)
                for jb in range(24):
                    e.wait_ge(ctx.sem(f"FWL{jb % 2}"), bWL[jb % 2] + 16 * 2 * (jb // 2 + 1))
                    if jb >= 1:
                        e.wait_ge(ctx.sem(SI5), bSI5 + 2 * (jb - 1) + 2)
                    elif ch >= 1:
                        e.wait_ge(ctx.sem(SI5), bSI5)
                    for half in range(2):
                        for kb in range(8):
                            inst = e.matmul(pa[half][:, 0:512],
                                            wgb[jb % 2][:, kb * 128:(kb + 1) * 128],
                                            tmslice(lnxT, kb, half * 512, 512),
                                            start=(kb == 0), stop=(kb == 7))
                        inst.then_inc(ctx.sem(PM5), 1)
                    if jb >= 1:
                        e.wait_ge(ctx.sem(HM5), bHM5 + 2 * (jb - 1) + 2)
                    elif ch >= 1:
                        e.wait_ge(ctx.sem(HM5), bHM5)
                    for half in range(2):
                        for kb in range(8):
                            inst = e.matmul(pa[2 + half][:, 0:512],
                                            wub[jb % 2][:, kb * 128:(kb + 1) * 128],
                                            tmslice(lnxT, kb, half * 512, 512),
                                            start=(kb == 0), stop=(kb == 7))
                        inst.then_inc(ctx.sem(PM5), 1)
                e.wait_ge(ctx.sem("FDL"), bDL + 16 * 24)
                for i in range(8):
                    for half in range(2):
                        g = 2 * i + half
                        if g >= 2:
                            e.wait_ge(ctx.sem(EV5), bEV5 + g - 1)
                        elif ch >= 1:
                            e.wait_ge(ctx.sem(EV5), bEV5)
                        e.wait_ge(ctx.sem(HM5), bHM5 + 48)
                        for kb in range(24):
                            inst = e.matmul(pa[4 + g % 2][:, 0:512],
                                            hsb[:, kb * CH + i * 128:kb * CH + (i + 1) * 128],
                                            wdb[:, kb * CH + half * 512:kb * CH + (half + 1) * 512],
                                            start=(kb == 0), stop=(kb == 23))
                        inst.then_inc(ctx.sem(PM5), 1)
            ctx.bump(PM5, 96 + 16)

            def f5_scalar(e, ch=ch, bPM5=bPM5, bHM5=bHM5):
                for jb in range(24):
                    for half in range(2):
                        e.wait_ge(ctx.sem(PM5), bPM5 + 4 * jb + half + 1)
                        if jb >= 1:
                            e.wait_ge(ctx.sem(HM5), bHM5 + 2 * (jb - 1) + half + 1)
                        elif ch >= 1:
                            e.wait_ge(ctx.sem(HM5), bHM5)
                        e.activation(out=silu_t[:, half * 512:(half + 1) * 512],
                                     in_=pa[half][:, 0:512], func=AF.Silu,
                                     scale=1.0 / 127.0).then_inc(ctx.sem(SI5), 1)
            ctx.bump(SI5, 48)

            def f5_vector(e, ch=ch, bPM5=bPM5, bSI5=bSI5, bEV5=bEV5, bDW=bDW):
                for jb in range(24):
                    for half in range(2):
                        e.wait_ge(ctx.sem(SI5), bSI5 + 2 * jb + half + 1)
                        e.wait_ge(ctx.sem(PM5), bPM5 + 4 * jb + 2 + half + 1)
                        if jb == 0 and half == 0 and ch >= 1:
                            e.wait_ge(ctx.sem(PM5), bPM5)
                        e.scalar_tensor_tensor(out=hsb[:, jb * CH + half * 512:jb * CH + (half + 1) * 512],
                                               in0=pa[2 + half][:, 0:512], scalar=1.0 / 127.0,
                                               in1=silu_t[:, half * 512:(half + 1) * 512],
                                               op0=ALU.mult, op1=ALU.mult).then_inc(ctx.sem(HM5), 1)
                for i in range(8):
                    for half in range(2):
                        g = 2 * i + half
                        e.wait_ge(ctx.sem(PM5), bPM5 + 96 + g + 1)
                        e.wait_ge(ctx.sem(f"FDW{i % 2}"), bDW[i % 2] + 16 * (i // 2 + 1))
                        e.scalar_tensor_tensor(out=orow[i % 2][:, half * 512:(half + 1) * 512],
                                               in0=pa[4 + g % 2][:, 0:512], scalar=0.5 / 127.0,
                                               in1=orow[i % 2][:, half * 512:(half + 1) * 512],
                                               op0=ALU.mult, op1=ALU.add).then_inc(ctx.sem(EV5), 1)
            ctx.bump(HM5, 48)
            ctx.bump(EV5, 16)

            def f5_scalarq(e, ch=ch, row0=row0, bSO=bSO):
                for i in range(8):
                    if i >= 2:
                        e.wait_ge(ctx.sem(f"FSO{i % 2}"), bSO[i % 2] + 16 * ((i - 2) // 2 + 1))
                    elif ch >= 1:
                        e.wait_ge(ctx.sem(f"FSO{i % 2}"), bSO[i % 2])
                    e.dma_start(out=orow[i % 2][:, :],
                                in_=resid[row0 + i * 128:row0 + (i + 1) * 128, :]).then_inc(ctx.sem(f"FDW{i % 2}"), 16)
            ctx.bump("FDW0", 16 * 4)
            ctx.bump("FDW1", 16 * 4)

            def f5_gpsimd(e, row0=row0, bEV5=bEV5, bSO=bSO):
                for i in range(8):
                    e.wait_ge(ctx.sem(EV5), bEV5 + 2 * i + 2)
                    e.dma_start(out=resid[row0 + i * 128:row0 + (i + 1) * 128, :],
                                in_=orow[i % 2][:, :]).then_inc(ctx.sem(f"FSO{i % 2}"), 16)
                e.wait_ge(ctx.sem("FSO0"), bSO[0] + 16 * 4)
                e.wait_ge(ctx.sem("FSO1"), bSO[1] + 16 * 4)
            ctx.bump("FSO0", 16 * 4)
            ctx.bump("FSO1", 16 * 4)

            block.sync(f5_sync)
            block.tensor(f5_tensor)
            block.scalar(f5_scalar)
            block.scalar(f5_scalarq)
            block.vector(f5_vector)
            block.gpsimd(f5_gpsimd)
        barrier(ctx, block)




def emit_serial(ctx, block, ops, sem="SER"):
    def semfor(eng, amt):
        if amt == 1:
            return "SER"
        return "SERD" if eng == "gpsimd" else "SERH"
    cums = {"SER": ctx.val("SER"), "SERH": ctx.val("SERH"), "SERD": ctx.val("SERD")}
    pre = []
    for (eng, _, amt) in ops:
        pre.append(dict(cums))
        cums[semfor(eng, amt)] += amt
    i = 0
    while i < len(ops):
        j = i
        while j < len(ops) and ops[j][0] == ops[i][0]:
            j += 1
        eng, lo, hi = ops[i][0], i, j

        def make(lo=lo, hi=hi):
            def f(e):
                for k in range(lo, hi):
                    for s in ("SER", "SERH", "SERD"):
                        e.wait_ge(ctx.sem(s), pre[k][s])
                    inst = ops[k][1](e)
                    inst.then_inc(ctx.sem(semfor(ops[k][0], ops[k][2])), ops[k][2])
            return f
        getattr(block, eng)(make())
        i = j

    def flush(e):
        for s in ("SER", "SERH", "SERD"):
            e.wait_ge(ctx.sem(s), cums[s])
    block.gpsimd(flush)
    for s in ("SER", "SERH", "SERD"):
        ctx.cnt[s] = cums[s]


def emit_lru(ctx, block, P, tag):
    nc = ctx.nc
    resid = P["resid"].ap()
    zln = P["xln"].ap()
    nu = P["l1_nu"].ap()
    th = P["l1_th"].ap()
    bTre = P["bTre"].ap()
    bTim = P["bTim"].ap()
    creT = P["creT"].ap()
    cimnT = P["cimnT"].ap()
    dT = P["dT"].ap()
    w1l1 = P["w1l1_bf"].ap()
    w2l1 = P["w2l1_bf"].ap()
    rlg = P["l1_res"].ap()
    rsc = P["lru_scratch"].ap()
    TRI = P["tri_sb"]
    ONESCOL = P["onescol_sb"]
    ONES128B = P["ones128b_sb"]
    ONES128F = P["ones128f_sb"]
    IDENT = P["ident_sb"]
    pa = P["psum"]
    PI = math.pi
    S = []

    def op(eng, amt=1):
        def deco(fn):
            S.append((eng, fn, amt))
        return deco

    with ExitStack() as es0:
        sb0 = lambda n, s, d: sbuf(ctx, n + tag, s, d, es0)
        lam78 = sb0("qL8", [1, 4 * DST], F32)
        bnT = sb0("qBT", [128, 16 * DST], BF16)
        sgv = sb0("qSG", [128, 1], F32)
        IV2 = sb0("qV2a", [128, 2 * DST], F32)
        IV2S = sb0("qV2b", [128, 2 * DST], F32)
        A2 = sb0("qA2a", [128, 2 * DST], F32)
        A2S = sb0("qA2b", [128, 2 * DST], F32)
        A12 = sb0("qA2c", [128, 2 * DST], F32)
        A12S = sb0("qA2d", [128, 2 * DST], F32)
        lamS = sb0("qLS", [1, 4 * DST], F32)
        est = es0.enter_context(ExitStack())
        sbt = lambda n, s, d: sbuf(ctx, n + tag, s, d, est)
        Are = sbt("qAe", [128, DST], F32)
        Aim = sbt("qAi", [128, DST], F32)
        A1re = sbt("qBe", [128, DST], F32)
        A1im = sbt("qBi", [128, DST], F32)
        Ivre = sbt("qIe", [128, DST], F32)
        Ivim = sbt("qIi", [128, DST], F32)
        btl = sbt("qBL", [128, 16 * DST], BF16)
        row1 = sbt("qR1", [1, DST], F32)
        row2 = sbt("qR2", [1, DST], F32)
        row3 = sbt("qR3", [1, 1], F32)
        enu_b = sbt("qEN", [128, DST], F32)
        eth_b = sbt("qET", [128, DST], F32)
        texp = sbt("qTX", [128, DST], F32)
        ta = sbt("qTA", [128, DST], F32)
        w1a = sbt("qW1", [128, DST], F32)
        w2a = sbt("qW2", [128, DST], F32)
        w3a = sbt("qW3", [128, DST], F32)
        scq = sbt("qSQ", [128, DST], I32)
        iot0 = sbt("qI0", [128, 1], F32)

        def rr(dst, srcv):
            op("vector")(lambda e, s=srcv: e.tensor_scalar(out=w3a[:, :], in0=s[:, :], scalar1=1.0 / (2 * PI), scalar2=None, op0=ALU.mult))
            op("vector")(lambda e: e.tensor_copy(out=scq[:, :], in_=w3a[:, :]))
            op("vector")(lambda e: e.tensor_copy(out=w3a[:, :], in_=scq[:, :]))
            op("vector")(lambda e, d=dst, s=srcv: e.scalar_tensor_tensor(out=d[:, :], in0=w3a[:, :], scalar=-2 * PI, in1=s[:, :], op0=ALU.mult, op1=ALU.add))
            op("vector")(lambda e, d=dst: e.tensor_scalar(out=w3a[:, :], in0=d[:, :], scalar1=PI, scalar2=None, op0=ALU.is_gt))
            op("vector")(lambda e, d=dst: e.scalar_tensor_tensor(out=d[:, :], in0=w3a[:, :], scalar=-2 * PI, in1=d[:, :], op0=ALU.mult, op1=ALU.add))
            op("vector")(lambda e, d=dst: e.tensor_scalar(out=w3a[:, :], in0=d[:, :], scalar1=-PI, scalar2=None, op0=ALU.is_lt))
            op("vector")(lambda e, d=dst: e.scalar_tensor_tensor(out=d[:, :], in0=w3a[:, :], scalar=2 * PI, in1=d[:, :], op0=ALU.mult, op1=ALU.add))

        op("sync", 16)(lambda e: e.dma_start(out=row1[:, :], in_=nu[:, :]))
        op("sync", 16)(lambda e: e.dma_start(out=row2[:, :], in_=th[:, :]))
        op("sync", 16)(lambda e: e.dma_start(out=row3[:, :], in_=rlg[:, :]))
        for pl, bsrc in enumerate((bTre, bTim)):
            for kb in range(8):
                op("sync", 16)(lambda e, pl=pl, kb=kb, bsrc=bsrc: e.dma_start(
                    out=btl[:, (pl * 8 + kb) * DST:(pl * 8 + kb + 1) * DST],
                    in_=bsrc[kb * 128:(kb + 1) * 128, :]))
        op("gpsimd")(lambda e: e.iota(iot0[:, :], [[0, 1]], base=0, channel_multiplier=1,
                                      allow_small_or_imprecise_dtypes=True))
        op("scalar")(lambda e: e.activation(out=row1[:, :], in_=row1[:, :], func=AF.Exp))
        op("scalar")(lambda e: e.activation(out=row2[:, :], in_=row2[:, :], func=AF.Exp))
        op("tensor")(lambda e: e.matmul(pa[0][:, 0:DST], ONES128F[:, :], row1[:, :], start=True, stop=True))
        op("tensor")(lambda e: e.matmul(pa[1][:, 0:DST], ONES128F[:, :], row2[:, :], start=True, stop=True))
        op("vector")(lambda e: e.tensor_copy(out=enu_b[:, :], in_=pa[0][:, 0:DST]))
        op("vector")(lambda e: e.tensor_copy(out=eth_b[:, :], in_=pa[1][:, 0:DST]))
        op("vector")(lambda e: e.tensor_scalar(out=texp[:, :], in0=enu_b[:, :], scalar1=iot0[:, :], scalar2=None, op0=ALU.mult))
        op("vector")(lambda e: e.tensor_scalar(out=ta[:, :], in0=eth_b[:, :], scalar1=iot0[:, :], scalar2=None, op0=ALU.mult))
        rr(w1a, ta)
        op("scalar")(lambda e: e.activation(out=w1a[:, :], in_=w1a[:, :], func=AF.Sin))
        op("vector")(lambda e: e.tensor_scalar(out=w2a[:, :], in0=ta[:, :], scalar1=PI / 2, scalar2=None, op0=ALU.add))
        rr(w2a, w2a)
        op("scalar")(lambda e: e.activation(out=w2a[:, :], in_=w2a[:, :], func=AF.Sin))
        op("scalar")(lambda e: e.activation(out=Ivre[:, :], in_=texp[:, :], func=AF.Exp))
        op("vector")(lambda e: e.scalar_tensor_tensor(out=Ivim[:, :], in0=Ivre[:, :], scalar=-1.0, in1=w1a[:, :], op0=ALU.mult, op1=ALU.mult))
        op("vector")(lambda e: e.tensor_tensor(out=Ivre[:, :], in0=Ivre[:, :], in1=w2a[:, :], op=ALU.mult))
        op("scalar")(lambda e: e.activation(out=Are[:, :], in_=texp[:, :], func=AF.Exp, scale=-1.0))
        op("vector")(lambda e: e.tensor_tensor(out=Aim[:, :], in0=Are[:, :], in1=w1a[:, :], op=ALU.mult))
        op("vector")(lambda e: e.tensor_tensor(out=Are[:, :], in0=Are[:, :], in1=w2a[:, :], op=ALU.mult))
        op("vector")(lambda e: e.tensor_tensor(out=texp[:, :], in0=texp[:, :], in1=enu_b[:, :], op=ALU.add))
        op("vector")(lambda e: e.tensor_tensor(out=ta[:, :], in0=ta[:, :], in1=eth_b[:, :], op=ALU.add))
        rr(w1a, ta)
        op("scalar")(lambda e: e.activation(out=w1a[:, :], in_=w1a[:, :], func=AF.Sin))
        op("vector")(lambda e: e.tensor_scalar(out=w2a[:, :], in0=ta[:, :], scalar1=PI / 2, scalar2=None, op0=ALU.add))
        rr(w2a, w2a)
        op("scalar")(lambda e: e.activation(out=w2a[:, :], in_=w2a[:, :], func=AF.Sin))
        op("scalar")(lambda e: e.activation(out=A1re[:, :], in_=texp[:, :], func=AF.Exp, scale=-1.0))
        op("vector")(lambda e: e.tensor_tensor(out=A1im[:, :], in0=A1re[:, :], in1=w1a[:, :], op=ALU.mult))
        op("vector")(lambda e: e.tensor_tensor(out=A1re[:, :], in0=A1re[:, :], in1=w2a[:, :], op=ALU.mult))
        op("scalar")(lambda e: e.activation(out=w1a[:, :], in_=enu_b[:, :], func=AF.Exp, scale=-2.0))
        op("vector")(lambda e: e.tensor_scalar(out=w1a[:, :], in0=w1a[:, :], scalar1=-1.0, scalar2=1.0, op0=ALU.mult, op1=ALU.add))
        op("scalar")(lambda e: e.activation(out=w1a[:, :], in_=w1a[:, :], func=AF.Ln))
        op("scalar")(lambda e: e.activation(out=w1a[:, :], in_=w1a[:, :], func=AF.Exp, scale=0.5))
        for s_ in range(16):
            op("vector")(lambda e, s_=s_: e.tensor_tensor(out=bnT[:, s_ * DST:(s_ + 1) * DST],
                                                          in0=btl[:, s_ * DST:(s_ + 1) * DST], in1=w1a[:, :], op=ALU.mult))
        op("scalar")(lambda e: e.activation(out=row3[:, :], in_=row3[:, :], func=AF.Sigmoid))
        op("tensor")(lambda e: e.matmul(pa[0][:, 0:1], ONES128F[:, :], row3[:, :], start=True, stop=True))
        op("vector")(lambda e: e.tensor_copy(out=sgv[:, :], in_=pa[0][:, 0:1]))
        op("gpsimd", 16)(lambda e: e.dma_start(out=rsc[0:1, 0:DST], in_=Are[127:128, :]))
        op("gpsimd", 16)(lambda e: e.dma_start(out=rsc[0:1, DST:2 * DST], in_=Aim[127:128, :]))
        op("gpsimd", 16)(lambda e: e.dma_start(out=rsc[1:2, 0:DST], in_=A1re[127:128, :]))
        op("gpsimd", 16)(lambda e: e.dma_start(out=rsc[1:2, DST:2 * DST], in_=A1im[127:128, :]))
        op("gpsimd", 16)(lambda e: e.dma_start(out=lam78[:, :], in_=rsc[0:2, :].rearrange("a b -> (a b)")[None, :]))
        for (dstt, lo, hi) in ((IV2, Ivre, Ivim), (IV2S, Ivim, Ivre),
                               (A2, Are, Aim), (A2S, Aim, Are),
                               (A12, A1re, A1im), (A12S, A1im, A1re)):
            op("vector")(lambda e, d=dstt, s=lo: e.tensor_copy(out=d[:, 0:DST], in_=s[:, :]))
            op("vector")(lambda e, d=dstt, s=hi: e.tensor_copy(out=d[:, DST:2 * DST], in_=s[:, :]))
        for (do_, so_) in ((0, DST), (DST, 0), (2 * DST, 3 * DST), (3 * DST, 2 * DST)):
            op("vector")(lambda e, do_=do_, so_=so_: e.tensor_copy(
                out=lamS[:, do_:do_ + DST], in_=lam78[:, so_:so_ + DST]))
        emit_serial(ctx, block, S)
        S = []
        barrier(ctx, block)
        est.close()

        serF = ctx.val("SER")
        serhF = ctx.val("SERH")
        serdF = ctx.val("SERD")
        for b in range(NB):
            btag = f"{tag}b{b}"
            row0 = b * SEQ
            with ExitStack() as esb:
                sbb = lambda n, s, d: sbuf(ctx, n + btag, s, d, esb)
                yT = sbb("qY", [128, 8 * SEQ], BF16)
                carry_f = [sbb(f"qCF{i}", [1, 2 * DST], F32) for i in range(2)]
                carry_bf = sbb("qCB", [1, 2 * DST], BF16)
                with ExitStack() as esc:
                    sbc = lambda n, s, d: sbuf(ctx, n + btag, s, d, esc)
                    lnxT = sbc("qX", [128, 8 * SEQ], BF16)
                    stT = sbc("qST", [128, 4 * SEQ], BF16)
                    bu = sbc("qBU", [128, 2 * 16 * DST], BF16)
                    scbf = sbc("qSB", [128, 2 * DST], BF16)
                    stbf = sbc("qSS", [128, 2 * DST], BF16)
                    t1 = sbc("qM1", [128, 2 * DST], F32)
                    t2 = sbc("qM2", [128, 2 * DST], F32)
                    r1 = sbc("qC1", [1, 2 * DST], F32)
                    r2 = sbc("qC2", [1, 2 * DST], F32)
                    cre_sb = sbc("qCR", [128, 2 * 1024], BF16)
                    cim_sb = sbc("qCI", [128, 2 * 1024], BF16)
                    dT_sb = sbc("qDT", [128, 8 * 1024], BF16)
                    bLT = [ctx.val("LTT0"), ctx.val("LTT1")]
                    emit_lnT(ctx, block, "ql" + btag, resid, row0, SEQ // 128,
                             lnxT, lambda i, db: db * SEQ + i * 128,
                             lnrow_extra=(zln, row0))
                    xs0, xs1 = ctx.val("LTX0"), ctx.val("LTX1")
                    bBUM, bBUV = ctx.val("M31"), ctx.val("Y3V")
                    bSCM, bSCV, bSCA = ctx.val("M3Q"), ctx.val("V3F"), ctx.val("S3S")
                    bYL, bYM, bYV = ctx.val("A4W"), ctx.val("M3T"), ctx.val("V3C")

                    def bu_tensor(e):
                        e.wait_ge(ctx.sem("SER"), serF)
                        e.wait_ge(ctx.sem("SERH"), serhF)
                        e.wait_ge(ctx.sem("SERD"), serdF)
                        for i in range(16):
                            e.wait_ge(ctx.sem(f"LTT{i % 2}"), bLT[i % 2] + 16 * (i // 2 + 1))
                            for pl in range(2):
                                g = i * 2 + pl
                                if g >= 2:
                                    e.wait_ge(ctx.sem("Y3V"), bBUV + g - 1)
                                for kb in range(8):
                                    inst = e.matmul(pa[g % 2][:, 0:DST],
                                                    lnxT[:, i * D + kb * 128:i * D + (kb + 1) * 128],
                                                    bnT[:, (pl * 8 + kb) * DST:(pl * 8 + kb + 1) * DST],
                                                    start=(kb == 0), stop=(kb == 7))
                                inst.then_inc(ctx.sem("M31"), 1)
                    ctx.bump("M31", 32)

                    def bu_vector(e):
                        for g in range(32):
                            e.wait_ge(ctx.sem("M31"), bBUM + g + 1)
                            e.tensor_copy(out=bu[:, g * DST:(g + 1) * DST],
                                          in_=pa[g % 2][:, 0:DST]).then_inc(ctx.sem("Y3V"), 1)
                    ctx.bump("Y3V", 32)

                    def sc_vector(e):
                        k = [bSCV]

                        def step(inst):
                            inst.then_inc(ctx.sem("V3F"), 1)
                            k[0] += 1
                            e.wait_ge(ctx.sem("V3F"), k[0])
                        step(e.memset(carry_f[0][:, :], 0.0))
                        step(e.memset(carry_bf[:, :], 0.0))
                        for c in range(16):
                            cur, nxt = c % 2, (c + 1) % 2
                            bu2 = bu[:, 2 * c * DST:(2 * c + 2) * DST]
                            e.wait_ge(ctx.sem("Y3V"), bBUV + 2 * c + 2)
                            if c >= 1:
                                e.wait_ge(ctx.sem("M3Q"), bSCM + 7 * (c - 1) + 3)
                            step(e.tensor_tensor(out=t1[:, :], in0=IV2[:, :], in1=bu2, op=ALU.mult))
                            step(e.tensor_tensor(out=scbf[:, 0:DST], in0=t1[:, 0:DST], in1=t1[:, DST:2 * DST], op=ALU.subtract))
                            step(e.tensor_tensor(out=t2[:, :], in0=IV2S[:, :], in1=bu2, op=ALU.mult))
                            step(e.tensor_tensor(out=scbf[:, DST:2 * DST], in0=t2[:, 0:DST], in1=t2[:, DST:2 * DST], op=ALU.add))
                            e.wait_ge(ctx.sem("M3Q"), bSCM + 7 * c + 2)
                            step(e.tensor_tensor(out=t1[:, :], in0=A2[:, :], in1=pa[2][:, 0:2 * DST], op=ALU.mult))
                            step(e.tensor_tensor(out=t2[:, :], in0=A12[:, :], in1=pa[4][:, 0:2 * DST], op=ALU.mult))
                            step(e.tensor_tensor(out=t1[:, :], in0=t1[:, :], in1=t2[:, :], op=ALU.add))
                            step(e.tensor_tensor(out=stbf[:, 0:DST], in0=t1[:, 0:DST], in1=t1[:, DST:2 * DST], op=ALU.subtract))
                            step(e.tensor_tensor(out=t1[:, :], in0=A2S[:, :], in1=pa[2][:, 0:2 * DST], op=ALU.mult))
                            step(e.tensor_tensor(out=t2[:, :], in0=A12S[:, :], in1=pa[4][:, 0:2 * DST], op=ALU.mult))
                            step(e.tensor_tensor(out=t1[:, :], in0=t1[:, :], in1=t2[:, :], op=ALU.add))
                            step(e.tensor_tensor(out=stbf[:, DST:2 * DST], in0=t1[:, 0:DST], in1=t1[:, DST:2 * DST], op=ALU.add))
                            e.wait_ge(ctx.sem("M3Q"), bSCM + 7 * c + 3)
                            step(e.tensor_tensor(out=r1[:, :], in0=lam78[:, 0:2 * DST], in1=pa[3][0:1, 0:2 * DST], op=ALU.mult))
                            step(e.tensor_tensor(out=r2[:, :], in0=lam78[:, 2 * DST:4 * DST], in1=carry_f[cur][:, :], op=ALU.mult))
                            step(e.tensor_tensor(out=r1[:, :], in0=r1[:, :], in1=r2[:, :], op=ALU.add))
                            step(e.tensor_tensor(out=carry_f[nxt][:, 0:DST], in0=r1[:, 0:DST], in1=r1[:, DST:2 * DST], op=ALU.subtract))
                            step(e.tensor_tensor(out=r1[:, :], in0=lamS[:, 0:2 * DST], in1=pa[3][0:1, 0:2 * DST], op=ALU.mult))
                            step(e.tensor_tensor(out=r2[:, :], in0=lamS[:, 2 * DST:4 * DST], in1=carry_f[cur][:, :], op=ALU.mult))
                            step(e.tensor_tensor(out=r1[:, :], in0=r1[:, :], in1=r2[:, :], op=ALU.add))
                            step(e.tensor_tensor(out=carry_f[nxt][:, DST:2 * DST], in0=r1[:, 0:DST], in1=r1[:, DST:2 * DST], op=ALU.add))
                            step(e.tensor_copy(out=carry_bf[:, :], in_=carry_f[nxt][:, :]))
                    ctx.bump("V3F", 2 + 21 * 16)

                    def sc_tensor(e):
                        for c in range(16):
                            e.wait_ge(ctx.sem("V3F"), bSCV + 2 + 21 * c)
                            e.matmul(pa[4][:, 0:2 * DST], ONES128B[:, :], carry_bf[:, :],
                                     start=True, stop=True).then_inc(ctx.sem("M3Q"), 1)
                            e.wait_ge(ctx.sem("V3F"), bSCV + 2 + 21 * c + 4)
                            e.matmul(pa[2][:, 0:2 * DST], TRI[:, :], scbf[:, :],
                                     start=True, stop=True).then_inc(ctx.sem("M3Q"), 1)
                            e.matmul(pa[3][0:1, 0:2 * DST], ONESCOL[:, :], scbf[:, :],
                                     start=True, stop=True).then_inc(ctx.sem("M3Q"), 1)
                            e.wait_ge(ctx.sem("V3F"), bSCV + 2 + 21 * c + 12)
                            for t in range(4):
                                if 4 * c + t - 1 > 0:
                                    e.wait_ge(ctx.sem("S3S"), bSCA + 4 * c + t - 1)
                                e.transpose(pa[5 + t % 2][:, :].bitcast(BF16)[:, 0:128],
                                            stbf[:, t * 128:(t + 1) * 128],
                                            IDENT[:, :]).then_inc(ctx.sem("M3Q"), 1)
                    ctx.bump("M3Q", 7 * 16)

                    def sc_scalar(e):
                        for c in range(16):
                            for t in range(4):
                                e.wait_ge(ctx.sem("M3Q"), bSCM + 7 * c + 3 + t + 1)
                                e.activation(out=stT[:, t * SEQ + c * 128:t * SEQ + (c + 1) * 128],
                                             in_=pa[5 + t % 2][:, :].bitcast(BF16)[:, 0:128],
                                             func=AF.Copy).then_inc(ctx.sem("S3S"), 1)
                    ctx.bump("S3S", 4 * 16)

                    def y_sync(e):
                        e.dma_start(out=cre_sb[:, 0:1024], in_=creT[0:128, :]).then_inc(ctx.sem("A4W"), 16)
                        e.dma_start(out=cre_sb[:, 1024:2048], in_=creT[128:256, :]).then_inc(ctx.sem("A4W"), 16)
                        e.dma_start(out=cim_sb[:, 0:1024], in_=cimnT[0:128, :]).then_inc(ctx.sem("A4W"), 16)
                        e.dma_start(out=cim_sb[:, 1024:2048], in_=cimnT[128:256, :]).then_inc(ctx.sem("A4W"), 16)
                        for kb in range(8):
                            e.dma_start(out=dT_sb[:, kb * 1024:(kb + 1) * 1024],
                                        in_=dT[kb * 128:(kb + 1) * 128, :]).then_inc(ctx.sem("A4W"), 16)
                    ctx.bump("A4W", 16 * 12)

                    def y_tensor(e):
                        e.wait_ge(ctx.sem("A4W"), bYL + 16 * 12)
                        e.wait_ge(ctx.sem("S3S"), bSCA + 64)
                        for mb in range(8):
                            for tc in range(4):
                                g = mb * 4 + tc
                                if g >= 2:
                                    e.wait_ge(ctx.sem("V3C"), bYV + g - 1)
                                bank = pa[6 + g % 2]
                                first = True
                                for nb in range(2):
                                    e.matmul(bank[:, 0:512], cre_sb[:, nb * 1024 + mb * 128:nb * 1024 + (mb + 1) * 128],
                                             stT[:, nb * SEQ + tc * 512:nb * SEQ + (tc + 1) * 512], start=first, stop=False)
                                    first = False
                                    e.matmul(bank[:, 0:512], cim_sb[:, nb * 1024 + mb * 128:nb * 1024 + (mb + 1) * 128],
                                             stT[:, (2 + nb) * SEQ + tc * 512:(2 + nb) * SEQ + (tc + 1) * 512], start=False, stop=False)
                                for kb in range(8):
                                    inst = e.matmul(bank[:, 0:512], dT_sb[:, kb * 1024 + mb * 128:kb * 1024 + (mb + 1) * 128],
                                                    tmslice(lnxT, kb, tc * 512, 512),
                                                    start=False, stop=(kb == 7))
                                inst.then_inc(ctx.sem("M3T"), 1)
                    ctx.bump("M3T", 32)

                    def y_vector(e):
                        for g in range(32):
                            mb, tc = g // 4, g % 4
                            e.wait_ge(ctx.sem("M3T"), bYM + g + 1)
                            e.tensor_copy(out=yT[:, mb * SEQ + tc * 512:mb * SEQ + (tc + 1) * 512],
                                          in_=pa[6 + g % 2][:, 0:512]).then_inc(ctx.sem("V3C"), 1)
                    ctx.bump("V3C", 32)

                    block.tensor(bu_tensor)
                    block.vector(bu_vector)
                    block.vector(sc_vector)
                    block.tensor(sc_tensor)
                    block.scalar(sc_scalar)
                    block.sync(y_sync)
                    block.tensor(y_tensor)
                    block.vector(y_vector)
                with ExitStack() as esz:
                    sbz = lambda n, s, d: sbuf(ctx, n + btag, s, d, esz)
                    hgl = sbz("qH", [128, 32 * 512], BF16)
                    w1b = [sbz(f"qV1{i}", [128, 8 * 128], BF16) for i in range(2)]
                    w2b = [sbz(f"qV2{i}", [128, 32 * 128], BF16) for i in range(2)]
                    zbf = sbz("qZ", [128, 8 * 512], BF16)
                    ztok = [sbz(f"qZT{i}", [128, D], BF16) for i in range(2)]
                    zlr = [sbz(f"qZL{i}", [128, D], BF16) for i in range(2)]
                    orow = [sbz(f"qO{i}", [128, D], F32) for i in range(2)]
                    bZM, bZA, bZV = ctx.val("A3P"), ctx.val("A3A"), ctx.val("A3D")
                    bZW = [ctx.val("A4T0"), ctx.val("A4T1")]
                    bZU = [ctx.val("A4L0"), ctx.val("A4L1")]
                    bZT = [ctx.val("A3X0"), ctx.val("A3X1")]
                    bZL = [ctx.val("CVL0"), ctx.val("CVL1")]
                    bZS = [ctx.val("FSO0"), ctx.val("FSO1")]
                    mmh_inc = lambda gi: 40 * (gi // 32) + (gi % 32) + 1
                    mmz_inc = lambda gj: 40 * (gj // 8) + 32 + (gj % 8) + 1
                    evac_inc = lambda tcq, mbq: 16 * tcq + mbq + 1
                    add2_inc = lambda ti: 16 * (ti // 4) + 8 + 2 * (ti % 4) + 2

                    def z_sync(e):
                        e.wait_ge(ctx.sem("M3T"), bYM + 32)
                        e.wait_ge(ctx.sem("LTX0"), xs0)
                        e.wait_ge(ctx.sem("LTX1"), xs1)
                        for tc in range(4):
                            for hb in range(32):
                                gi = tc * 32 + hb
                                if gi >= 2:
                                    e.wait_ge(ctx.sem("A3P"), bZM + mmh_inc(gi - 2))
                                e.dma_start(out=w1b[gi % 2][:, :].rearrange("p (k c) -> p k c", k=8),
                                            in_=bass.AP(w1l1.tensor, hb * 128,
                                                        [[4096, 128], [128 * 4096, 8], [1, 128]])).then_inc(ctx.sem(f"A4T{gi % 2}"), 16)
                            for mb in range(8):
                                gj = tc * 8 + mb
                                if gj >= 2:
                                    e.wait_ge(ctx.sem("A3P"), bZM + mmz_inc(gj - 2))
                                e.dma_start(out=w2b[gj % 2][:, :].rearrange("p (k c) -> p k c", k=32),
                                            in_=bass.AP(w2l1.tensor, mb * 128,
                                                        [[1024, 128], [128 * 1024, 32], [1, 128]])).then_inc(ctx.sem(f"A4L{gj % 2}"), 16)
                            for i in range(4):
                                ti = tc * 4 + i
                                e.wait_ge(ctx.sem("A3D"), bZV + 16 * tc + 8)
                                if ti >= 2:
                                    e.wait_ge(ctx.sem("A3D"), bZV + add2_inc(ti - 2))
                                e.dma_start_transpose(out=b3(ztok[ti % 2][:, :]),
                                                      in_=b3(zbf[:, i * 1024:(i + 1) * 1024])).then_inc(ctx.sem(f"A3X{ti % 2}"), 16)
                                if ti >= 2:
                                    e.wait_ge(ctx.sem(f"FSO{ti % 2}"), bZS[ti % 2] + 16 * ((ti - 2) // 2 + 1))
                                e.dma_start(out=orow[ti % 2][:, :],
                                            in_=resid[row0 + ti * 128:row0 + (ti + 1) * 128, :]).then_inc(ctx.sem(f"CVL{ti % 2}"), 16)
                                e.dma_start(out=zlr[ti % 2][:, :],
                                            in_=zln[row0 + ti * 128:row0 + (ti + 1) * 128, :]).then_inc(ctx.sem(f"CVL{ti % 2}"), 16)
                    ctx.bump("A4T0", 16 * 64)
                    ctx.bump("A4T1", 16 * 64)
                    ctx.bump("A4L0", 16 * 16)
                    ctx.bump("A4L1", 16 * 16)
                    ctx.bump("A3X0", 16 * 8)
                    ctx.bump("A3X1", 16 * 8)
                    ctx.bump("CVL0", 32 * 8)
                    ctx.bump("CVL1", 32 * 8)

                    def z_tensor(e):
                        e.wait_ge(ctx.sem("V3C"), bYV + 32)
                        for tc in range(4):
                            for hb in range(32):
                                gi = tc * 32 + hb
                                e.wait_ge(ctx.sem(f"A4T{gi % 2}"), bZW[gi % 2] + 16 * (gi // 2 + 1))
                                if gi >= 2:
                                    e.wait_ge(ctx.sem("A3A"), bZA + 32 * ((gi - 2) // 32) + ((gi - 2) % 32) + 1)
                                for kb in range(8):
                                    inst = e.matmul(pa[gi % 2][:, 0:512], w1b[gi % 2][:, kb * 128:(kb + 1) * 128],
                                                    yT[:, kb * SEQ + tc * 512:kb * SEQ + (tc + 1) * 512],
                                                    start=(kb == 0), stop=(kb == 7))
                                inst.then_inc(ctx.sem("A3P"), 1)
                            for mb in range(8):
                                gj = tc * 8 + mb
                                e.wait_ge(ctx.sem("A3A"), bZA + 32 * (tc + 1))
                                e.wait_ge(ctx.sem(f"A4L{gj % 2}"), bZU[gj % 2] + 16 * (gj // 2 + 1))
                                if mb >= 2:
                                    e.wait_ge(ctx.sem("A3D"), bZV + evac_inc(tc, mb - 2))
                                elif tc >= 1:
                                    e.wait_ge(ctx.sem("A3D"), bZV + evac_inc(tc - 1, 6 + mb))
                                for hb in range(32):
                                    inst = e.matmul(pa[2 + mb % 2][:, 0:512], w2b[gj % 2][:, hb * 128:(hb + 1) * 128],
                                                    hgl[:, hb * 512:(hb + 1) * 512],
                                                    start=(hb == 0), stop=(hb == 31))
                                inst.then_inc(ctx.sem("A3P"), 1)
                    ctx.bump("A3P", 40 * 4)

                    def z_scalar(e):
                        for tc in range(4):
                            for hb in range(32):
                                gi = tc * 32 + hb
                                e.wait_ge(ctx.sem("A3P"), bZM + mmh_inc(gi))
                                e.activation(out=hgl[:, hb * 512:(hb + 1) * 512], in_=pa[gi % 2][:, 0:512],
                                             func=AF.Gelu_apprx_tanh).then_inc(ctx.sem("A3A"), 1)
                    ctx.bump("A3A", 32 * 4)

                    def z_vector(e):
                        k = [bZV]

                        def step(inst):
                            inst.then_inc(ctx.sem("A3D"), 1)
                            k[0] += 1
                        for tc in range(4):
                            for mb in range(8):
                                gj = tc * 8 + mb
                                e.wait_ge(ctx.sem("A3P"), bZM + mmz_inc(gj))
                                if mb == 0 and tc >= 1:
                                    e.wait_ge(ctx.sem("A3X0"), bZT[0] + 16 * 2 * tc)
                                    e.wait_ge(ctx.sem("A3X1"), bZT[1] + 16 * 2 * tc)
                                step(e.tensor_scalar(
                                    out=zbf[:, :].rearrange("p (i c) -> p i c", c=1024)[:, :, mb * 128:(mb + 1) * 128],
                                    in0=pa[2 + mb % 2][:, 0:512],
                                    scalar1=sgv[:, :], scalar2=None, op0=ALU.mult))
                            for i in range(4):
                                ti = tc * 4 + i
                                e.wait_ge(ctx.sem(f"A3X{ti % 2}"), bZT[ti % 2] + 16 * (ti // 2 + 1))
                                e.wait_ge(ctx.sem(f"CVL{ti % 2}"), bZL[ti % 2] + 32 * (ti // 2 + 1))
                                step(e.tensor_tensor(out=orow[ti % 2][:, :], in0=orow[ti % 2][:, :],
                                                     in1=zlr[ti % 2][:, :], op=ALU.add))
                                e.wait_ge(ctx.sem("A3D"), k[0])
                                step(e.tensor_tensor(out=orow[ti % 2][:, :], in0=orow[ti % 2][:, :],
                                                     in1=ztok[ti % 2][:, :], op=ALU.add))
                    ctx.bump("A3D", 16 * 4)

                    def z_gpsimd(e):
                        for ti in range(16):
                            e.wait_ge(ctx.sem("A3D"), bZV + add2_inc(ti))
                            e.dma_start(out=resid[row0 + ti * 128:row0 + (ti + 1) * 128, :],
                                        in_=orow[ti % 2][:, :]).then_inc(ctx.sem(f"FSO{ti % 2}"), 16)
                        e.wait_ge(ctx.sem("FSO0"), bZS[0] + 16 * 8)
                        e.wait_ge(ctx.sem("FSO1"), bZS[1] + 16 * 8)
                    ctx.bump("FSO0", 16 * 8)
                    ctx.bump("FSO1", 16 * 8)

                    block.sync(z_sync)
                    block.tensor(z_tensor)
                    block.scalar(z_scalar)
                    block.vector(z_vector)
                    block.gpsimd(z_gpsimd)
            barrier(ctx, block)


def host_constants():
    import ml_dtypes
    bf = ml_dtypes.bfloat16
    return {
        "tri": np.triu(np.ones((128, 128), np.float32)).astype(bf),
        "tri32": np.triu(np.ones((32, 32), np.float32), 1).astype(np.float32),
        "ones32": np.ones((32, 1), np.float32),
        "onescol": np.ones((128, 1)).astype(bf),
        "ones128b": np.ones((1, 128)).astype(bf),
        "ones128f": np.ones((1, 128), np.float32),
        "ones1x32": np.ones((1, 32), np.float32),
        "ident": np.eye(128).astype(bf),
        "amask": _attn_mask(),
    }


def _attn_mask():
    m = np.zeros((128, 640), np.float32)
    i = np.arange(128)[:, None]
    j = np.arange(128)[None, :]
    m[:, 0:128] = np.where(j > i, 0.0, -1e30)
    m[:, 512:640] = np.where(j <= i, 0.0, -1e30)
    return m


CONST_SPEC = [("tri", [128, 128], BF16), ("tri32", [32, 32], F32), ("ones32", [32, 1], F32),
              ("onescol", [128, 1], BF16), ("ones128b", [1, 128], BF16),
              ("ones128f", [1, 128], F32), ("ones1x32", [1, 32], F32),
              ("ident", [128, 128], BF16), ("amask", [128, 640], F32)]


def declare_common(nc, ctx, es):
    P = {}
    consts = []
    for (nm, shp, dt) in CONST_SPEC:
        dram = nc.dram_tensor("cst_" + nm, shp, dt, kind="ExternalInput")
        t = sbuf(ctx, nm + "_c", shp, dt, es)
        P[nm + "_sb"] = t
        consts.append((dram, t))
    P["iota16_sb"] = sbuf(ctx, "iota16_c", [128, NEXP], F32, es)
    P["eps_sb"] = sbuf(ctx, "eps_c", [128, 1], F32, es)
    P_EPS[0] = P["eps_sb"]
    P["psum"] = [es.enter_context(nc.psum_tensor(f"pbank{i}", [128, 512], F32)) for i in range(8)]
    P["xln"] = nc.dram_tensor("scr_xln", [NTOK, D], BF16, kind="Internal")
    P["perm"] = nc.dram_tensor("scr_perm", [NSLOT, 1], U32, kind="Internal")
    P["wslot"] = nc.dram_tensor("scr_wslot", [NSLOT, 1], F32, kind="Internal")
    P["ttot"] = nc.dram_tensor("scr_ttot", [32, NEXP], F32, kind="Internal")
    P["ctab"] = nc.dram_tensor("scr_ctab", [32, NEXP], F32, kind="Internal")
    P["etab"] = nc.dram_tensor("scr_etab", [1, NCT], I32, kind="Internal")
    return P, consts


def emit_prelude(ctx, block, P, consts):
    total = ctx.val("CONST") + 16 * len(consts) + 2

    def f_sync(e):
        for (dram, t) in consts:
            e.dma_start(out=t[:, :], in_=dram.ap()).then_inc(ctx.sem("CONST"), 16)
        e.wait_ge(ctx.sem("CONST"), total)

    def f_gpsimd(e):
        e.iota(P["iota16_sb"][:, :], [[1, NEXP]], base=0, channel_multiplier=0,
               allow_small_or_imprecise_dtypes=True).then_inc(ctx.sem("CONST"), 1)
        e.memset(P["eps_sb"][:, :], float(LN_EPS)).then_inc(ctx.sem("CONST"), 1)
        e.wait_ge(ctx.sem("CONST"), total)
    ctx.bump("CONST", 16 * len(consts) + 2)
    block.sync(f_sync)
    block.gpsimd(f_gpsimd)


def const_inputs():
    return {"cst_" + k: v for k, v in host_constants().items()}


def emit_dbg_dump(ctx, block, P, pairs, tag):
    with ExitStack() as es:
        dts = sorted({str(p[4]) for p in pairs})
        bufs = {k: sbuf(ctx, f"db{k}{tag}", [128, 2048],
                        next(p[4] for p in pairs if str(p[4]) == k), es) for k in dts}
        ld, st = "DBL", "DBS"
        bld, bst = ctx.val(ld), ctx.val(st)
        tiles = []
        for (s, d, rows, cols, dt) in pairs:
            for r0 in range(0, rows, 128):
                for c0 in range(0, cols, 2048):
                    tiles.append((s, d, r0, min(128, rows - r0), c0, min(2048, cols - c0), dt))

        def f_sync(e):
            for i, (s, d, r0, rr, c0, cc, dt) in enumerate(tiles):
                b = bufs[str(dt)]
                if i >= 1:
                    e.wait_ge(ctx.sem(st), bst + 16 * i)
                e.dma_start(out=b[:rr, :cc], in_=s[r0:r0 + rr, c0:c0 + cc]).then_inc(ctx.sem(ld), 16)
                e.wait_ge(ctx.sem(ld), bld + 16 * (i + 1))
                e.dma_start(out=d[r0:r0 + rr, c0:c0 + cc], in_=b[:rr, :cc]).then_inc(ctx.sem(st), 16)
            e.wait_ge(ctx.sem(st), bst + 16 * len(tiles))
        ctx.bump(ld, 16 * len(tiles))
        ctx.bump(st, 16 * len(tiles))
        block.sync(f_sync)


def build_unipool_test(nlayers, dbg=False):
    nc = bass.Bass("TRN2", target_bir_lowering=False)
    es = ExitStack()
    ctx = Ctx(nc, es)
    x0 = nc.dram_tensor("x0", [NTOK, D], F32, kind="ExternalInput").ap()
    w1 = nc.dram_tensor("l0_w1", [NEXP * D, FF], F32, kind="ExternalInput")
    w2 = nc.dram_tensor("l0_w2", [NEXP * FF, D], F32, kind="ExternalInput")
    routers = nc.dram_tensor("l0_routers", [4 * D, NEXP], F32, kind="ExternalInput")
    out = nc.dram_tensor("out", [NTOK, D], F32, kind="ExternalOutput").ap()
    resid = nc.dram_tensor("scr_resid", [NTOK, D], F32, kind="Internal").ap()

    P, consts = declare_common(nc, ctx, es)
    P["w1bf"] = nc.dram_tensor("scr_w1bf", [NEXP * 4 * 128, 8 * 1024], BF16, kind="Internal")
    P["w2bf"] = nc.dram_tensor("scr_w2bf", [NEXP * 4 * 128, 8 * 1024], BF16, kind="Internal")
    P["routers_bf"] = nc.dram_tensor("scr_rbf", [4 * D, NEXP], BF16, kind="Internal")

    if dbg:
        P["dbg_ex"] = nc.dram_tensor("dbg_ex", [128, NEXP * NTILE], F32, kind="ExternalOutput")
        P["dbg_idx"] = nc.dram_tensor("dbg_idx", [128, NTILE], F32, kind="ExternalOutput")
        P["dbg_xt"] = nc.dram_tensor("dbg_xt", [128, 2 * D], BF16, kind="ExternalOutput")
    with nc.Block() as block:
        emit_prelude(ctx, block, P, consts)
        with ExitStack() as es2:
            emit_convert_pretile(ctx, block, es2, w1.ap(), w2.ap(),
                                 P["w1bf"].ap(), P["w2bf"].ap(), "wp0")
            emit_convert(ctx, block, es2, [
                (routers.ap(), P["routers_bf"].ap(), 4 * D, NEXP, None),
            ], "w0")
            barrier(ctx, block)
        for r in range(nlayers):
            if nlayers == 1:
                mode = "single"
            elif r == 0:
                mode = "first"
            elif r == nlayers - 1:
                mode = "last"
            else:
                mode = "mid"
            emit_unipool(ctx, block, P, r, f"r{r}", mode, xsrc=x0, xdst=out)
            barrier(ctx, block)
        if dbg:
            dperm = nc.dram_tensor("dbg_perm", [NSLOT, 1], U32, kind="ExternalOutput").ap()
            dwslot = nc.dram_tensor("dbg_wslot", [NSLOT, 1], F32, kind="ExternalOutput").ap()
            dttot = nc.dram_tensor("dbg_ttot", [32, NEXP], F32, kind="ExternalOutput").ap()
            dctab = nc.dram_tensor("dbg_ctab", [32, NEXP], F32, kind="ExternalOutput").ap()
            detab = nc.dram_tensor("dbg_etab", [1, NCT], I32, kind="ExternalOutput").ap()
            dzbuf = nc.dram_tensor("dbg_zbuf", [NTOK, D], BF16, kind="ExternalOutput").ap()
            dw1 = nc.dram_tensor("dbg_w1pre", [256, 8192], BF16, kind="ExternalOutput").ap()
            dw2 = nc.dram_tensor("dbg_w2pre", [256, 8192], BF16, kind="ExternalOutput").ap()
            emit_dbg_dump(ctx, block, P, [
                (P["w1bf"].ap()[0:256, :], dw1, 256, 8192, BF16),
                (P["w2bf"].ap()[640:896, :], dw2, 256, 8192, BF16),
                (P["perm"].ap().rearrange("(p c) o -> p (c o)", p=128), dperm.rearrange("(p c) o -> p (c o)", p=128), 128, 64, U32),
                (P["wslot"].ap().rearrange("(p c) o -> p (c o)", p=128), dwslot.rearrange("(p c) o -> p (c o)", p=128), 128, 64, F32),
                (P["ttot"].ap(), dttot, 32, NEXP, F32),
                (P["ctab"].ap(), dctab, 32, NEXP, F32),
                (P["etab"].ap(), detab, 1, NCT, I32),
                (P["xln"].ap(), dzbuf, NTOK, D, BF16),
            ], "dbg")
            barrier(ctx, block)
    es.close()
    return nc




def build_attn_test():
    nc = bass.Bass("TRN2", target_bir_lowering=False)
    es = ExitStack()
    ctx = Ctx(nc, es)
    x0 = nc.dram_tensor("x0", [NTOK, D], F32, kind="ExternalInput").ap()
    wq = nc.dram_tensor("wq", [D, D], F32, kind="ExternalInput")
    wk = nc.dram_tensor("wk", [D, D], F32, kind="ExternalInput")
    wv = nc.dram_tensor("wv", [D, D], F32, kind="ExternalInput")
    wo = nc.dram_tensor("wo", [D, D], F32, kind="ExternalInput")
    out = nc.dram_tensor("out", [NTOK, D], F32, kind="ExternalOutput")
    P, consts = declare_common(nc, ctx, es)
    P["resid"] = out
    for nm in ("wq", "wk", "wv", "wo"):
        P[nm + "_bf"] = nc.dram_tensor("scr_" + nm, [D, D], BF16, kind="Internal")
    with nc.Block() as block:
        emit_prelude(ctx, block, P, consts)
        with ExitStack() as es2:
            emit_convert(ctx, block, es2, [
                (wq.ap(), P["wq_bf"].ap(), D, D, None),
                (wk.ap(), P["wk_bf"].ap(), D, D, None),
                (wv.ap(), P["wv_bf"].ap(), D, D, None),
                (wo.ap(), P["wo_bf"].ap(), D, D, None),
            ], "aw")
            barrier(ctx, block)
        emit_dbg_dump(ctx, block, P, [(x0, out.ap(), NTOK, D, F32)], "ini")
        barrier(ctx, block)
        emit_attn(ctx, block, P, "at")
    es.close()
    return nc




def build_ffn_test():
    nc = bass.Bass("TRN2", target_bir_lowering=False)
    es = ExitStack()
    ctx = Ctx(nc, es)
    x0 = nc.dram_tensor("x0", [NTOK, D], F32, kind="ExternalInput").ap()
    wg = nc.dram_tensor("wg", [FFI, D], I32, kind="ExternalInput")
    wu = nc.dram_tensor("wu", [FFI, D], I32, kind="ExternalInput")
    wd = nc.dram_tensor("wd", [D, FFI], I32, kind="ExternalInput")
    out = nc.dram_tensor("out", [NTOK, D], F32, kind="ExternalOutput")
    P, consts = declare_common(nc, ctx, es)
    P["resid"] = out
    wg_bf = nc.dram_tensor("scr_wg", [FFI, D], BF16, kind="Internal")
    wu_bf = nc.dram_tensor("scr_wu", [FFI, D], BF16, kind="Internal")
    wd_bf = nc.dram_tensor("scr_wd", [D, FFI], BF16, kind="Internal")
    P["wgT"] = nc.dram_tensor("scr_wgT", [D, FFI], BF16, kind="Internal")
    P["wuT"] = nc.dram_tensor("scr_wuT", [D, FFI], BF16, kind="Internal")
    P["wdT"] = nc.dram_tensor("scr_wdT", [FFI, D], BF16, kind="Internal")
    with nc.Block() as block:
        emit_prelude(ctx, block, P, consts)
        with ExitStack() as es2:
            emit_convert(ctx, block, es2, [
                (wg.ap(), wg_bf.ap(), FFI, D, None),
                (wu.ap(), wu_bf.ap(), FFI, D, None),
                (wd.ap(), wd_bf.ap(), D, FFI, None),
            ], "fw")
            barrier(ctx, block)
        with ExitStack() as es3:
            emit_transposeT(ctx, block, es3, [
                (wg_bf.ap(), P["wgT"].ap()),
                (wu_bf.ap(), P["wuT"].ap()),
                (wd_bf.ap(), P["wdT"].ap()),
            ], "ft")
            barrier(ctx, block)
        emit_dbg_dump(ctx, block, P, [(x0, out.ap(), NTOK, D, F32)], "ini")
        barrier(ctx, block)
        emit_ffn_int8(ctx, block, P, "ff")
    es.close()
    return nc




def build_lru_test():
    nc = bass.Bass("TRN2", target_bir_lowering=False)
    es = ExitStack()
    ctx = Ctx(nc, es)
    x0 = nc.dram_tensor("x0", [NTOK, D], F32, kind="ExternalInput").ap()
    nu = nc.dram_tensor("nu", [1, DST], F32, kind="ExternalInput")
    th = nc.dram_tensor("th", [1, DST], F32, kind="ExternalInput")
    bre = nc.dram_tensor("bre", [DST, D], F32, kind="ExternalInput")
    bim = nc.dram_tensor("bim", [DST, D], F32, kind="ExternalInput")
    cre = nc.dram_tensor("cre", [D, DST], F32, kind="ExternalInput")
    cim = nc.dram_tensor("cim", [D, DST], F32, kind="ExternalInput")
    dm = nc.dram_tensor("dm", [D, D], F32, kind="ExternalInput")
    w1 = nc.dram_tensor("w1", [D, 4096], F32, kind="ExternalInput")
    w2 = nc.dram_tensor("w2", [4096, D], F32, kind="ExternalInput")
    res = nc.dram_tensor("res", [1, 1], F32, kind="ExternalInput")
    out = nc.dram_tensor("out", [NTOK, D], F32, kind="ExternalOutput")
    P, consts = declare_common(nc, ctx, es)
    P["resid"] = out
    P["l1_nu"], P["l1_th"], P["l1_res"] = nu, th, res
    bre_bf = nc.dram_tensor("scr_bre", [DST, D], BF16, kind="Internal")
    bim_bf = nc.dram_tensor("scr_bim", [DST, D], BF16, kind="Internal")
    cre_bf = nc.dram_tensor("scr_cre", [D, DST], BF16, kind="Internal")
    cimn_bf = nc.dram_tensor("scr_cim", [D, DST], BF16, kind="Internal")
    d_bf = nc.dram_tensor("scr_dm", [D, D], BF16, kind="Internal")
    P["bTre"] = nc.dram_tensor("scr_bTre", [D, DST], BF16, kind="Internal")
    P["bTim"] = nc.dram_tensor("scr_bTim", [D, DST], BF16, kind="Internal")
    P["creT"] = nc.dram_tensor("scr_creT", [DST, D], BF16, kind="Internal")
    P["cimnT"] = nc.dram_tensor("scr_cimnT", [DST, D], BF16, kind="Internal")
    P["dT"] = nc.dram_tensor("scr_dT", [D, D], BF16, kind="Internal")
    P["w1l1_bf"] = nc.dram_tensor("scr_w1l1", [D, 4096], BF16, kind="Internal")
    P["w2l1_bf"] = nc.dram_tensor("scr_w2l1", [4096, D], BF16, kind="Internal")
    P["lru_scratch"] = nc.dram_tensor("scr_lrs", [8, 512], F32, kind="Internal")
    with nc.Block() as block:
        emit_prelude(ctx, block, P, consts)
        with ExitStack() as es2:
            emit_convert(ctx, block, es2, [
                (bre.ap(), bre_bf.ap(), DST, D, None),
                (bim.ap(), bim_bf.ap(), DST, D, None),
                (cre.ap(), cre_bf.ap(), D, DST, None),
                (cim.ap(), cimn_bf.ap(), D, DST, -1.0),
                (dm.ap(), d_bf.ap(), D, D, None),
                (w1.ap(), P["w1l1_bf"].ap(), D, 4096, None),
                (w2.ap(), P["w2l1_bf"].ap(), 4096, D, None),
            ], "lw")
            barrier(ctx, block)
        with ExitStack() as es3:
            emit_transposeT(ctx, block, es3, [
                (bre_bf.ap(), P["bTre"].ap()),
                (bim_bf.ap(), P["bTim"].ap()),
                (cre_bf.ap(), P["creT"].ap()),
                (cimn_bf.ap(), P["cimnT"].ap()),
                (d_bf.ap(), P["dT"].ap()),
            ], "lt")
            barrier(ctx, block)
        emit_dbg_dump(ctx, block, P, [(x0, out.ap(), NTOK, D, F32)], "ini")
        barrier(ctx, block)
        emit_lru(ctx, block, P, "lr")
    es.close()
    return nc




def build_full():
    nc = bass.Bass("TRN2", target_bir_lowering=False)
    es = ExitStack()
    ctx = Ctx(nc, es)
    x0 = nc.dram_tensor("x0", [NTOK, D], F32, kind="ExternalInput").ap()
    l0w1 = nc.dram_tensor("l0_w1", [NEXP * D, FF], F32, kind="ExternalInput")
    l0w2 = nc.dram_tensor("l0_w2", [NEXP * FF, D], F32, kind="ExternalInput")
    routers = nc.dram_tensor("l0_routers", [4 * D, NEXP], F32, kind="ExternalInput")
    wq = nc.dram_tensor("wq", [D, D], F32, kind="ExternalInput")
    wk = nc.dram_tensor("wk", [D, D], F32, kind="ExternalInput")
    wv = nc.dram_tensor("wv", [D, D], F32, kind="ExternalInput")
    wo = nc.dram_tensor("wo", [D, D], F32, kind="ExternalInput")
    nu = nc.dram_tensor("nu", [1, DST], F32, kind="ExternalInput")
    th = nc.dram_tensor("th", [1, DST], F32, kind="ExternalInput")
    bre = nc.dram_tensor("bre", [DST, D], F32, kind="ExternalInput")
    bim = nc.dram_tensor("bim", [DST, D], F32, kind="ExternalInput")
    cre = nc.dram_tensor("cre", [D, DST], F32, kind="ExternalInput")
    cim = nc.dram_tensor("cim", [D, DST], F32, kind="ExternalInput")
    dm = nc.dram_tensor("dm", [D, D], F32, kind="ExternalInput")
    l1w1 = nc.dram_tensor("l1_w1", [D, 4096], F32, kind="ExternalInput")
    l1w2 = nc.dram_tensor("l1_w2", [4096, D], F32, kind="ExternalInput")
    res = nc.dram_tensor("res", [1, 1], F32, kind="ExternalInput")
    wg = nc.dram_tensor("wg", [FFI, D], I32, kind="ExternalInput")
    wu = nc.dram_tensor("wu", [FFI, D], I32, kind="ExternalInput")
    wd = nc.dram_tensor("wd", [D, FFI], I32, kind="ExternalInput")
    y = nc.dram_tensor("y", [NTOK, D], F32, kind="ExternalOutput")

    P, consts = declare_common(nc, ctx, es)
    P["resid"] = y
    P["l1_nu"], P["l1_th"], P["l1_res"] = nu, th, res
    P["w1bf"] = nc.dram_tensor("scr_w1bf", [NEXP * 4 * 128, 8 * 1024], BF16, kind="Internal")
    P["w2bf"] = nc.dram_tensor("scr_w2bf", [NEXP * 4 * 128, 8 * 1024], BF16, kind="Internal")
    P["routers_bf"] = nc.dram_tensor("scr_rbf", [4 * D, NEXP], BF16, kind="Internal")
    for nm in ("wq", "wk", "wv", "wo"):
        P[nm + "_bf"] = nc.dram_tensor("scr_" + nm, [D, D], BF16, kind="Internal")
    wg_bf = nc.dram_tensor("scr_wg", [FFI, D], BF16, kind="Internal")
    wu_bf = nc.dram_tensor("scr_wu", [FFI, D], BF16, kind="Internal")
    wd_bf = nc.dram_tensor("scr_wd", [D, FFI], BF16, kind="Internal")
    P["wgT"] = nc.dram_tensor("scr_wgT", [D, FFI], BF16, kind="Internal")
    P["wuT"] = nc.dram_tensor("scr_wuT", [D, FFI], BF16, kind="Internal")
    P["wdT"] = nc.dram_tensor("scr_wdT", [FFI, D], BF16, kind="Internal")
    bre_bf = nc.dram_tensor("scr_bre", [DST, D], BF16, kind="Internal")
    bim_bf = nc.dram_tensor("scr_bim", [DST, D], BF16, kind="Internal")
    cre_bf = nc.dram_tensor("scr_cre", [D, DST], BF16, kind="Internal")
    cimn_bf = nc.dram_tensor("scr_cim", [D, DST], BF16, kind="Internal")
    d_bf = nc.dram_tensor("scr_dm", [D, D], BF16, kind="Internal")
    P["bTre"] = nc.dram_tensor("scr_bTre", [D, DST], BF16, kind="Internal")
    P["bTim"] = nc.dram_tensor("scr_bTim", [D, DST], BF16, kind="Internal")
    P["creT"] = nc.dram_tensor("scr_creT", [DST, D], BF16, kind="Internal")
    P["cimnT"] = nc.dram_tensor("scr_cimnT", [DST, D], BF16, kind="Internal")
    P["dT"] = nc.dram_tensor("scr_dT", [D, D], BF16, kind="Internal")
    P["w1l1_bf"] = nc.dram_tensor("scr_w1l1", [D, 4096], BF16, kind="Internal")
    P["w2l1_bf"] = nc.dram_tensor("scr_w2l1", [4096, D], BF16, kind="Internal")
    P["lru_scratch"] = nc.dram_tensor("scr_lrs", [8, 512], F32, kind="Internal")

    with nc.Block() as block:
        emit_prelude(ctx, block, P, consts)
        with ExitStack() as es2p:
            emit_convert_pretile(ctx, block, es2p, l0w1.ap(), l0w2.ap(),
                                 P["w1bf"].ap(), P["w2bf"].ap(), "wp0")
            barrier(ctx, block)
        with ExitStack() as es2:
            emit_convert(ctx, block, es2, [
                (routers.ap(), P["routers_bf"].ap(), 4 * D, NEXP, None),
                (wq.ap(), P["wq_bf"].ap(), D, D, None),
                (wk.ap(), P["wk_bf"].ap(), D, D, None),
                (wv.ap(), P["wv_bf"].ap(), D, D, None),
                (wo.ap(), P["wo_bf"].ap(), D, D, None),
                (wg.ap(), wg_bf.ap(), FFI, D, None),
                (wu.ap(), wu_bf.ap(), FFI, D, None),
                (wd.ap(), wd_bf.ap(), D, FFI, None),
                (bre.ap(), bre_bf.ap(), DST, D, None),
                (bim.ap(), bim_bf.ap(), DST, D, None),
                (cre.ap(), cre_bf.ap(), D, DST, None),
                (cim.ap(), cimn_bf.ap(), D, DST, -1.0),
                (dm.ap(), d_bf.ap(), D, D, None),
                (l1w1.ap(), P["w1l1_bf"].ap(), D, 4096, None),
                (l1w2.ap(), P["w2l1_bf"].ap(), 4096, D, None),
            ], "cv")
            barrier(ctx, block)
        with ExitStack() as es3:
            emit_transposeT(ctx, block, es3, [
                (wg_bf.ap(), P["wgT"].ap()),
                (wu_bf.ap(), P["wuT"].ap()),
                (wd_bf.ap(), P["wdT"].ap()),
                (bre_bf.ap(), P["bTre"].ap()),
                (bim_bf.ap(), P["bTim"].ap()),
                (cre_bf.ap(), P["creT"].ap()),
                (cimn_bf.ap(), P["cimnT"].ap()),
                (d_bf.ap(), P["dT"].ap()),
            ], "tt")
            barrier(ctx, block)
        resid = y.ap()
        for r in range(4):
            mode = "first" if r == 0 else ("last" if r == 3 else "mid")
            emit_unipool(ctx, block, P, r, f"ua{r}", mode,
                         xsrc=(x0 if True else None), xdst=resid)
            barrier(ctx, block)
        emit_attn(ctx, block, P, "at")
        for r in range(4):
            mode = "first" if r == 0 else ("last" if r == 3 else "mid")
            emit_unipool(ctx, block, P, r, f"ub{r}", mode, xsrc=resid, xdst=resid)
            barrier(ctx, block)
        emit_ffn_int8(ctx, block, P, "f1")
        emit_lru(ctx, block, P, "lr")
        emit_ffn_int8(ctx, block, P, "f2")
    es.close()
    return nc


def shard_inputs(inputs, core_id):
    x = np.ascontiguousarray(inputs["L0_attn_x"]).reshape(16, SEQ, D)
    m = {
        "x0": np.ascontiguousarray(x[2 * core_id:2 * core_id + 2].reshape(NTOK, D)),
        "l0_w1": inputs["L0_ffn_W1"].reshape(NEXP * D, FF),
        "l0_w2": inputs["L0_ffn_W2"].reshape(NEXP * FF, D),
        "l0_routers": inputs["L0_ffn_routers"].reshape(4 * D, NEXP),
        "wq": inputs["L0_attn_Wq"], "wk": inputs["L0_attn_Wk"],
        "wv": inputs["L0_attn_Wv"], "wo": inputs["L0_attn_Wo"],
        "nu": inputs["L1_attn_nu_log"].reshape(1, DST),
        "th": inputs["L1_attn_theta_log"].reshape(1, DST),
        "bre": inputs["L1_attn_B_re"], "bim": inputs["L1_attn_B_im"],
        "cre": inputs["L1_attn_C_re"], "cim": inputs["L1_attn_C_im"],
        "dm": inputs["L1_attn_D_mat"],
        "l1_w1": inputs["L1_attn_W1"], "l1_w2": inputs["L1_attn_W2"],
        "res": np.asarray(inputs["L1_attn_res_logit"], dtype=np.float32).reshape(1, 1),
        "wg": inputs["L1_ffn_w_gate"], "wu": inputs["L1_ffn_w_up"],
        "wd": inputs["L1_ffn_w_down"],
    }
    m = {k: np.ascontiguousarray(v) for k, v in m.items()}
    m.update(const_inputs())
    return m


def kernel(**inputs):
    nc = build_full()
    in_maps = [shard_inputs(inputs, c) for c in range(8)]
    res = run_bass_kernel_spmd(nc, in_maps, core_ids=list(range(8)))
    out = np.concatenate([res.results[c]["y"].reshape(2, SEQ, D) for c in range(8)], axis=0)
    return out.astype(np.float32)


